# version 09
# baseline (speedup 1.0000x reference)
import numpy as np
import ml_dtypes
import concourse.bass as bass
import concourse.mybir as mybir
from concourse.bass_utils import run_bass_kernel_spmd
from contextlib import ExitStack

F32 = mybir.dt.float32
BF16 = mybir.dt.bfloat16
AF = mybir.ActivationFunctionType
ALU = mybir.AluOpType
AX = mybir.AxisListType

D = 2048
NT = 2048
TT = 128
NTT = 16
BLK = 1024
NBLK = 2
NH = 32
HD = 64
QKV = 3 * D
IA = 8192
IAH = 4096
IF = 8192
NCORE = 8
NEG = -1.0e30

DEBUG_TAPS = False


def ts(i, n):
    return slice(i * n, (i + 1) * n)


def DMA(out, in_):
    return lambda e: e.dma_start(out=out, in_=in_)


def MM(out, lhsT, rhs, start, stop):
    return lambda e: e.matmul(out, lhsT, rhs, start=start, stop=stop)


def TR(out, in_, idn):
    return lambda e: e.transpose(out, in_, idn)


def CP(out, in_):
    return lambda e: e.tensor_copy(out, in_)


def SCP(out, in_):
    return lambda e: e.copy(out, in_)


def TT_(out, a, b, op):
    return lambda e: e.tensor_tensor(out, a, b, op)


def TSC(out, a, s1, s2, op0, op1=None, acc=None):
    if op1 is None:
        return lambda e: e.tensor_scalar(out, a, s1, None, op0, accum_out=acc)
    return lambda e: e.tensor_scalar(out, a, s1, s2, op0, op1, accum_out=acc)


def STT(out, in0, s, in1, op0, op1):
    return lambda e: e.scalar_tensor_tensor(out, in0, s, in1, op0, op1)


def ACTV(out, in_, f, bias=0.0, scale=1.0, acc=None):
    return lambda e: e.activation(out, in_, f, bias=bias, scale=scale, accum_out=acc)


def RSUM(out, in_):
    return lambda e: e.reduce_sum(out, in_, axis=AX.X)


def RECIP(out, in_):
    return lambda e: e.reciprocal(out, in_)


class Sched:
    ENGS = ("sync", "scalar", "vector", "tensor", "gpsimd")

    def __init__(self, nc, stack):
        self.nc = nc
        self.ops = {e: [] for e in self.ENGS}
        self.sems = {}
        self.counts = {}
        self.guaranteed = {e: {} for e in self.ENGS}
        self.pending = None
        self.pending_applied = set()
        self.marks = []
        self.names = {e: [] for e in self.ENGS}

    def mark(self, label):
        self.marks.append((label, {e: len(self.ops[e]) for e in self.ENGS}))

    def sem(self, name, stack):
        self.sems[name] = stack.enter_context(self.nc.semaphore(name))
        self.counts[name] = 0
        return name

    def barrier(self):
        self.pending = {s: c for s, c in self.counts.items() if c > 0}
        self.pending_applied = set()

    def op(self, eng, emit, waits=(), sig=None, inc=1, self_edge=False):
        tok = None
        if self_edge and sig is not None and self.counts[sig] > 0:
            waits = list(waits) + [(sig, self.counts[sig])]
        if sig is not None:
            self.counts[sig] += inc
            tok = self.counts[sig]
        ws = []
        if self.pending is not None and eng not in self.pending_applied:
            self.pending_applied.add(eng)
            for s, v in self.pending.items():
                ws.append((s, v))
        for s, v in waits:
            if v is None:
                continue
            ws.append((s, v))
        g = self.guaranteed[eng]
        ws2 = []
        for s, v in ws:
            if g.get(s, -1) >= v:
                continue
            g[s] = v
            ws2.append((s, v))
        self.ops[eng].append((emit, ws2, sig, inc))
        return tok

    def emit(self):
        nc = self.nc
        with nc.Block(no_gpsimd_drain=True) as block:
            for eng in self.ENGS:
                deco = getattr(block, eng)
                lst = self.ops[eng]
                sems = self.sems

                def make(lst, eng):
                    def body(handle):
                        nm = self.names[eng]
                        for emit, ws, sig, inc in lst:
                            for s, v in ws:
                                handle.wait_ge(sems[s], v)
                            inst = emit(handle)
                            if sig is not None:
                                inst.then_inc(sems[sig], inc)
                            try:
                                nm.append(inst.ins.name)
                            except Exception:
                                nm.append(None)
                    return body

                deco(make(lst, eng))


def build():
    nc = bass.Bass("TRN2", target_bir_lowering=False)
    dbg = "ExternalOutput" if DEBUG_TAPS else "Internal"

    x = nc.dram_tensor("x", [NT, D], F32, kind="ExternalInput")
    wqkvT = nc.dram_tensor("wqkvT", [D, QKV], F32, kind="ExternalInput")
    woT = nc.dram_tensor("woT", [D, D], F32, kind="ExternalInput")
    wupT = nc.dram_tensor("wupT", [D, IA], F32, kind="ExternalInput")
    wdnT = nc.dram_tensor("wdnT", [IAH, D], F32, kind="ExternalInput")
    wg = nc.dram_tensor("wg", [D, IF], F32, kind="ExternalInput")
    wu = nc.dram_tensor("wu", [D, IF], F32, kind="ExternalInput")
    wd = nc.dram_tensor("wd", [IF, D], F32, kind="ExternalInput")
    gam_a = nc.dram_tensor("gam_a", [128, D], F32, kind="ExternalInput")
    gam_f = nc.dram_tensor("gam_f", [128, D], F32, kind="ExternalInput")
    cosTd = nc.dram_tensor("cosT", [128, NT], F32, kind="ExternalInput")
    sinmTd = nc.dram_tensor("sinmT", [128, NT], F32, kind="ExternalInput")
    masktri = nc.dram_tensor("masktri", [128, 128], F32, kind="ExternalInput")
    ident32d = nc.dram_tensor("ident32", [128, 128], F32, kind="ExternalInput")
    ident16d = nc.dram_tensor("ident16", [128, 128], BF16, kind="ExternalInput")

    out = nc.dram_tensor("out", [NT, D], F32, kind="ExternalOutput")

    xn32 = nc.dram_tensor("xn32", [NT, D], F32, kind=dbg)
    y32 = nc.dram_tensor("y32", [NT, D], F32, kind=dbg)
    att32 = nc.dram_tensor("att32", [NT, D], F32, kind=dbg)
    xnT16 = nc.dram_tensor("xnT16", [16, 128, NT], BF16, kind=dbg)
    qk16 = nc.dram_tensor("qk16", [32, 128, NT], BF16, kind=dbg)
    v16 = nc.dram_tensor("v16", [16, 128, D], BF16, kind=dbg)
    ctx16 = nc.dram_tensor("ctx16", [16, 128, NT], BF16, kind=dbg)
    h2T16 = nc.dram_tensor("h2T16", [16, 128, NT], BF16, kind=dbg)
    hsT16 = nc.dram_tensor("hsT16", [64, 128, NT], BF16, kind=dbg)

    xnT16v = xnT16.ap().rearrange("c p n -> p c n")
    qk16v = qk16.ap().rearrange("c p n -> p c n")
    v16v = v16.ap().rearrange("c p n -> p c n")
    ctx16v = ctx16.ap().rearrange("c p n -> p c n")
    h2T16v = h2T16.ap().rearrange("c p n -> p c n")
    hsT16v = hsT16.ap().rearrange("c p n -> p c n")
    wqkvTv = wqkvT.ap().rearrange("(c p) n -> p c n", p=128)
    woTv = woT.ap().rearrange("(c p) n -> p c n", p=128)
    wupTv = wupT.ap().rearrange("(c p) n -> p c n", p=128)
    wdnTv = wdnT.ap().rearrange("(c p) n -> p c n", p=128)
    wgv = wg.ap().rearrange("(c p) n -> p c n", p=128)
    wuv = wu.ap().rearrange("(c p) n -> p c n", p=128)
    wdv = wd.ap().rearrange("(c p) n -> p c n", p=128)

    stack = ExitStack()
    sc = Sched(nc, stack)
    for s in ("sWD", "sXD", "sSD", "sT", "sV", "sS"):
        sc.sem(s, stack)

    def sb(name, shape, dtype):
        return stack.enter_context(nc.sbuf_tensor(name, shape, dtype))

    id32 = sb("id32", [128, 128], F32)
    id16 = sb("id16", [128, 128], BF16)
    msk = sb("msk", [128, 128], F32)
    sc.op("sync", DMA(id32[:, :], ident32d[:, :]), sig="sXD", inc=16)
    sc.op("sync", DMA(id16[:, :], ident16d[:, :]), sig="sXD", inc=16)
    mask_tok = sc.op("sync", DMA(msk[:, :], masktri[:, :]), sig="sXD", inc=16)

    h1T_cm = nc.sbuf_tensor("h1T", [128, 16, NT], BF16)
    h1T = h1T_cm.__enter__()

    sc.mark("ph0")
    with ExitStack() as ph:
        _sb = lambda n, s, d: ph.enter_context(nc.sbuf_tensor(n, s, d))
        _ps = lambda n, s, d=F32: ph.enter_context(nc.psum_tensor(n, s, d))
        xt = [_sb(f"xt{i}", [128, D], F32) for i in range(2)]
        xnsb = [_sb(f"xnsb{i}", [128, D], F32) for i in range(2)]
        h1sb = [_sb(f"h1sb{i}", [128, D], F32) for i in range(2)]
        sqscr = [_sb(f"sqscr{i}", [128, D], F32) for i in range(2)]
        stgx = [_sb(f"stgx{i}", [128, 16, 128], BF16) for i in range(2)]
        st0 = _sb("st0", [128, 32], F32)
        gama = _sb("gama", [128, D], F32)
        psTr = _ps("psTr", [128, D], F32)

        ga_tok = sc.op("sync", DMA(gama[:, :], gam_a[:, :]), sig="sXD", inc=16)
        ld = [None, None]
        xn_done = [None, None]
        h1_done = [None, None]
        dx = [None, None]
        dxT = [None, None]
        h1ev = [None, None]
        trev_last = None
        for t_i in range(NTT):
            p = t_i % 2
            b0 = 16 * p
            (S1, M, MSQ, VAR, VA, SQT, RSTD, S2C, BB, S2S, S2R) = (
                b0, b0 + 1, b0 + 2, b0 + 3, b0 + 4, b0 + 5, b0 + 6, b0 + 7,
                b0 + 8, b0 + 9, b0 + 10)
            col = lambda c: st0[:, c:c + 1]
            w = [("sV", xn_done[p])] if xn_done[p] else []
            ld[p] = sc.op("sync", DMA(xt[p][:, :], x[ts(t_i, TT), :]), waits=w,
                          sig="sXD", inc=16)
            w = [("sXD", ld[p])]
            if h1_done[p]:
                w.append(("sV", h1_done[p]))
            s2tok = sc.op("scalar", ACTV(sqscr[p][:, :], xt[p][:, :], AF.Square,
                                         acc=col(S2C)), waits=w, sig="sS")
            sc.op("vector", RSUM(col(S1), xt[p][:, :]), waits=[("sXD", ld[p])], sig="sV")
            sc.op("vector", TSC(col(M), col(S1), 1.0 / D, None, ALU.mult), sig="sV",
                  self_edge=True)
            sc.op("vector", TT_(col(MSQ), col(M), col(M), ALU.mult), sig="sV",
                  self_edge=True)
            sc.op("vector", TSC(col(VAR), col(S2C), 1.0 / D, col(MSQ), ALU.mult,
                                ALU.subtract), waits=[("sS", s2tok)], sig="sV",
                  self_edge=True)
            va_tok = sc.op("vector", TSC(col(VA), col(VAR), 1e-5, None, ALU.add),
                           sig="sV", self_edge=True)
            sqtok = sc.op("scalar", ACTV(col(SQT), col(VA), AF.Sqrt),
                          waits=[("sV", va_tok)], sig="sS")
            sc.op("vector", RECIP(col(RSTD), col(SQT)), waits=[("sS", sqtok)], sig="sV")
            w = []
            if dxT[p]:
                w.append(("sXD", dxT[p]))
            xn_done[p] = sc.op("vector", TSC(xnsb[p][:, :], xt[p][:, :], col(M),
                                             col(RSTD), ALU.subtract, ALU.mult),
                               waits=w, sig="sV", self_edge=True)
            dx[p] = sc.op("sync", DMA(xn32[ts(t_i, TT), :], xnsb[p][:, :]),
                          waits=[("sV", xn_done[p])], sig="sXD", inc=16)
            sc.op("vector", RECIP(col(BB), col(VA)), sig="sV")
            sc.op("vector", TT_(col(BB), col(VAR), col(BB), ALU.mult), sig="sV",
                  self_edge=True)
            bb_tok = sc.op("vector", TSC(col(BB), col(BB), 1e-5, None, ALU.add),
                           sig="sV", self_edge=True)
            sq2 = sc.op("scalar", ACTV(col(S2S), col(BB), AF.Sqrt),
                        waits=[("sV", bb_tok)], sig="sS")
            sc.op("vector", RECIP(col(S2R), col(S2S)), waits=[("sS", sq2)], sig="sV")
            w = [("sXD", ga_tok)]
            if h1ev[p]:
                w.append(("sV", h1ev[p]))
            h1_done[p] = sc.op("vector", STT(h1sb[p][:, :], xnsb[p][:, :], col(S2R),
                                             gama[:, :], ALU.mult, ALU.mult),
                               waits=w, sig="sV", self_edge=True)
            w = [("sV", xn_done[p])]
            if trev_last:
                w.append(("sV", trev_last))
            for ci in range(16):
                tk = sc.op("tensor", TR(psTr[:, ts(ci, 128)], xnsb[p][:, ts(ci, 128)],
                                        id32[:, :]),
                           waits=(w if ci == 0 else []), sig=("sT" if ci == 15 else None))
            wv = [("sT", tk)]
            if dxT[p]:
                wv.append(("sXD", dxT[p]))
            trev = sc.op("vector", CP(stgx[p][:, :, :],
                                      psTr.ap().rearrange("p (c n) -> p c n", n=128)),
                         waits=wv, sig="sV")
            dxT[p] = sc.op("sync", DMA(xnT16v[:, :, ts(t_i, TT)], stgx[p][:, :, :]),
                           waits=[("sV", trev)], sig="sXD", inc=16)
            w = [("sV", h1_done[p]), ("sV", trev)]
            for ci in range(16):
                tk = sc.op("tensor", TR(psTr[:, ts(ci, 128)], h1sb[p][:, ts(ci, 128)],
                                        id32[:, :]),
                           waits=(w if ci == 0 else []), sig=("sT" if ci == 15 else None))
            trev_last = sc.op("vector",
                              CP(h1T[:, :, ts(t_i, TT)],
                                 psTr.ap().rearrange("p (c n) -> p c n", n=128)),
                              waits=[("sT", tk)], sig="sV")
            h1ev[p] = trev_last
    sc.barrier()

    sc.mark("ph1")
    with ExitStack() as ph:
        _sb = lambda n, s, d: ph.enter_context(nc.sbuf_tensor(n, s, d))
        _ps = lambda n, s, d=F32: ph.enter_context(nc.psum_tensor(n, s, d))
        cos_sb = _sb("cos_sb", [128, NT], BF16)
        sinm_sb = _sb("sinm_sb", [128, NT], BF16)
        wbuf = [_sb(f"w1_{i}", [128, 16, 128], BF16) for i in range(2)]
        qraw = [_sb(f"qraw{i}", [128, NT], BF16) for i in range(2)]
        qshf = [_sb(f"qshf{i}", [128, NT], BF16) for i in range(2)]
        qtmp = [_sb(f"qtmp{i}", [128, NT], BF16) for i in range(2)]
        qout = [_sb(f"qout{i}", [128, NT], BF16) for i in range(2)]
        vwbuf = [_sb(f"vw{i}", [128, 16, 512], BF16) for i in range(2)]
        vstg = [_sb(f"vstg{i}", [128, D], BF16) for i in range(2)]
        ps1 = [_ps(f"ps1_{i}", [128, NT], F32) for i in range(2)]

        sc.op("gpsimd", DMA(cos_sb[:, :], cosTd[:, :]), sig="sWD", inc=16)
        rope_tok = sc.op("gpsimd", DMA(sinm_sb[:, :], sinmTd[:, :]), sig="sWD", inc=16)

        wtok = [None, None]
        mm_done = [None, None]
        ev_first = [None, None]
        ev_done = [None, None]
        st_done = [None, None]
        for c in range(32):
            p = c % 2
            w = [("sT", mm_done[p])] if mm_done[p] else []
            wtok[p] = sc.op("gpsimd", DMA(wbuf[p][:, :, :], wqkvTv[:, :, ts(c, 128)]),
                            waits=w, sig="sWD", inc=16)
            w = [("sWD", wtok[p])]
            if ev_first[p]:
                w.append(("sV", ev_first[p]))
            first = True
            for sp in range(4):
                for ci in range(16):
                    tk = sc.op("tensor", MM(ps1[p][:, ts(sp, 512)], wbuf[p][:, ci, :],
                                            h1T[:, ci, ts(sp, 512)], ci == 0, ci == 15),
                               waits=(w if first else []),
                               sig=("sT" if (sp == 3 and ci == 15) else None))
                    first = False
            mm_done[p] = tk
            w = [("sT", mm_done[p])]
            if st_done[p]:
                w.append(("sXD", st_done[p]))
            ev_first[p] = sc.op("vector", CP(qraw[p][:, :], ps1[p][:, :]), waits=w,
                                sig="sV")
            sc.op("vector", CP(qshf[p][0:32, :], qraw[p][32:64, :]), sig="sV")
            sc.op("vector", CP(qshf[p][32:64, :], qraw[p][0:32, :]), sig="sV")
            sc.op("vector", CP(qshf[p][64:96, :], qraw[p][96:128, :]), sig="sV")
            sc.op("vector", CP(qshf[p][96:128, :], qraw[p][64:96, :]), sig="sV")
            sc.op("vector", TT_(qtmp[p][:, :], qraw[p][:, :], cos_sb[:, :], ALU.mult),
                  waits=[("sWD", rope_tok)], sig="sV")
            sc.op("vector", TT_(qshf[p][:, :], qshf[p][:, :], sinm_sb[:, :], ALU.mult),
                  sig="sV")
            ev_done[p] = sc.op("vector", TT_(qout[p][:, :], qtmp[p][:, :],
                                             qshf[p][:, :], ALU.add), sig="sV")
            st_done[p] = sc.op("sync", DMA(qk16v[:, c, :], qout[p][:, :]),
                               waits=[("sV", ev_done[p])], sig="sXD", inc=16)
        qk_v_fence = sc.counts["sV"]
        vw_tok = [None, None]
        vmm = [None, None]
        vev = [None, None]
        vst = [None, None]
        for vs in range(4):
            pw = vs % 2
            w = [("sT", vmm[0]), ("sT", vmm[1])] if vs >= 2 else []
            vw_tok[pw] = sc.op("gpsimd", DMA(vwbuf[pw][:, :, :],
                                             wqkvTv[:, :, 4096 + vs * 512: 4096 + (vs + 1) * 512]),
                               waits=w, sig="sWD", inc=16)
            for ti in range(NTT):
                pp = ti % 2
                w = [("sWD", vw_tok[pw]), ("sV", qk_v_fence)]
                if vev[pp]:
                    w.append(("sV", vev[pp]))
                first = True
                for ci in range(16):
                    tk = sc.op("tensor", MM(ps1[pp][:, 0:512], h1T[:, ci, ts(ti, 128)],
                                            vwbuf[pw][:, ci, :], ci == 0, ci == 15),
                               waits=(w if first else []),
                               sig=("sT" if ci == 15 else None))
                    first = False
                vmm[pp] = tk
                w = [("sT", vmm[pp])]
                if vst[pp]:
                    w.append(("sXD", vst[pp]))
                vev[pp] = sc.op("vector", CP(vstg[pp][:, ts(vs, 512)], ps1[pp][:, 0:512]),
                                waits=w, sig="sV")
                vst[pp] = sc.op("sync", DMA(v16v[:, ti, ts(vs, 512)],
                                            vstg[pp][:, ts(vs, 512)]),
                                waits=[("sV", vev[pp])], sig="sXD", inc=16)
    sc.barrier()

    h1T_cm.__exit__(None, None, None)

    sc.mark("ph2")
    with ExitStack() as ph:
        _sb = lambda n, s, d: ph.enter_context(nc.sbuf_tensor(n, s, d))
        _ps = lambda n, s, d=F32: ph.enter_context(nc.psum_tensor(n, s, d))
        Kb = _sb("Kb", [128, 16, BLK], BF16)
        Vb = _sb("Vb", [128, 8, D], BF16)
        ctxF = _sb("ctxF", [128, 16, BLK], BF16)
        Qc = [_sb(f"Qc{i}", [128, BLK], BF16) for i in range(2)]
        Ebuf = [_sb(f"Ebuf{i}", [128, BLK], BF16) for i in range(2)]
        Pbuf = [_sb(f"Pbuf{i}", [128, BLK], BF16) for i in range(2)]
        PTs = [_sb(f"PTs{i}", [128, BLK], BF16) for i in range(2)]
        sumc = _sb("sumc", [128, 4], F32)
        psS = [_ps(f"psS{i}", [128, BLK], F32) for i in range(2)]
        psPT = _ps("psPT", [128, 2 * BLK], BF16)
        psCtx = _ps("psCtx", [128, BLK], F32)
        SKEW = 2
        NU = 256

        ctx_st = None
        for b in range(NBLK):
            w = []
            if b > 0:
                w = [("sT", sc.counts["sT"]), ("sXD", ctx_st)]
            kld = sc.op("sync", DMA(Kb[:, :, :], qk16v[:, 16:32, ts(b, BLK)]),
                        waits=w, sig="sXD", inc=16)
            vld = sc.op("sync", DMA(Vb[:, :, :], v16v[:, ts(b, 8), :]),
                        sig="sXD", inc=16)
            q_ld = [None, None]
            q_last_mm = [None, None]
            sc_tok = [None] * NU
            exp_tok = [None] * NU
            av_tok = [None] * NU
            ctx_ev_tok = [None] * NU
            pscale_by_par = [None, None]
            pt_ev_last = None
            first_ev_of_block = True

            def unit(u):
                ci, rem = divmod(u, 16)
                hh, j = divmod(rem, 8)
                return ci, hh, j

            def emit_ctx_ev(v):
                nonlocal first_ev_of_block
                ci, hh, j = unit(v)
                r = hh * 64
                w = [("sT", av_tok[v])]
                if first_ev_of_block and ctx_st:
                    w.append(("sXD", ctx_st))
                first_ev_of_block = False
                ctx_ev_tok[v] = sc.op("scalar",
                                      SCP(ctxF[r:r + 64, ci, ts(j, 128)],
                                          psCtx[0:64, (v % 2) * 512:(v % 2) * 512 + 128]),
                                      waits=w, sig="sS")

            mk_tok = [None] * NU
            scale_tok = [None] * NU
            pt_tok = [None] * NU
            ptev_tok = [None] * NU
            for u in range(NU + 7):
                v = u - 2
                if 0 <= v < NU:
                    _, _, j = unit(v)
                    span = (j + 1) * 128
                    pv = v % 2
                    we = [("sV", mk_tok[v])]
                    if v >= 2 and scale_tok[v - 2]:
                        we.append(("sV", scale_tok[v - 2]))
                    exp_tok[v] = sc.op("scalar",
                                       ACTV(Ebuf[pv][:, 0:span], psS[pv][:, 0:span],
                                            AF.Exp, scale=0.125,
                                            acc=sumc[:, pv:pv + 1]),
                                       waits=we, sig="sS")
                v = u - 3
                if 0 <= v < NU:
                    _, _, j = unit(v)
                    span = (j + 1) * 128
                    pv = v % 2
                    sc.op("vector", RECIP(sumc[:, 2 + pv:3 + pv], sumc[:, pv:pv + 1]),
                          waits=[("sS", exp_tok[v])], sig="sV")
                    w2 = []
                    if v >= 2 and pt_tok[v - 2]:
                        w2.append(("sT", pt_tok[v - 2]))
                    scale_tok[v] = sc.op("vector", TSC(Pbuf[pv][:, 0:span],
                                                       Ebuf[pv][:, 0:span],
                                                       sumc[:, 2 + pv:3 + pv], None,
                                                       ALU.mult), waits=w2,
                                         sig="sV", self_edge=True)
                v = u - 6
                if 0 <= v < NU:
                    ci, hh, j = unit(v)
                    h = ci * 2 + hh
                    pv = v % 2
                    w = [("sV", ptev_tok[v]), ("sXD", vld)]
                    if v >= 2 and ctx_ev_tok[v - 2]:
                        w.append(("sS", ctx_ev_tok[v - 2]))
                    cbase = pv * 512
                    for i in range(j + 1):
                        tk = sc.op("tensor", MM(psCtx[0:64, cbase:cbase + 128],
                                                Vb[:, i, ts(h, 64)],
                                                PTs[pv][:, ts(i, 128)], i == 0, i == j),
                                   waits=(w if i == 0 else []),
                                   sig=("sT" if i == j else None))
                    av_tok[v] = tk
                    emit_ctx_ev(v)
                v = u - 5
                if 0 <= v < NU:
                    _, _, j = unit(v)
                    span = (j + 1) * 128
                    pv = v % 2
                    w = [("sT", pt_tok[v])]
                    if v >= 2 and av_tok[v - 2]:
                        w.append(("sT", av_tok[v - 2]))
                    ptev_tok[v] = sc.op("vector",
                                        CP(PTs[pv][:, 0:span],
                                           psPT[:, pv * 1024: pv * 1024 + span]),
                                        waits=w, sig="sV")
                v = u - 4
                if 0 <= v < NU:
                    _, _, j = unit(v)
                    pv = v % 2
                    w = [("sV", scale_tok[v])]
                    if v >= 2 and ptev_tok[v - 2]:
                        w.append(("sV", ptev_tok[v - 2]))
                    for i in range(j + 1):
                        tk = sc.op("tensor",
                                   TR(psPT[:, pv * 1024 + i * 128: pv * 1024 + i * 128 + 128],
                                      Pbuf[pv][:, ts(i, 128)], id16[:, :]),
                                   waits=(w if i == 0 else []),
                                   sig=("sT" if i == j else None))
                    pt_tok[v] = tk
                v = u - 1
                if 0 <= v < NU:
                    _, _, j = unit(v)
                    pv = v % 2
                    mk_tok[v] = sc.op("vector", TT_(psS[pv][:, ts(j, 128)],
                                                    psS[pv][:, ts(j, 128)],
                                                    msk[:, :], ALU.add),
                                      waits=[("sT", sc_tok[v]), ("sXD", mask_tok)],
                                      sig="sV")
                if u < NU:
                    ci, hh, j = unit(u)
                    pq = ci % 2
                    if hh == 0 and j == 0:
                        w = [("sXD", kld)]
                        if q_last_mm[pq]:
                            w.append(("sT", q_last_mm[pq]))
                        q_ld[pq] = sc.op("sync",
                                         DMA(Qc[pq][:, :], qk16v[:, ci, ts(b, BLK)]),
                                         waits=w, sig="sXD", inc=16)
                    r = hh * 64
                    span = (j + 1) * 128
                    slot = u % 2
                    w = [("sXD", q_ld[pq])]
                    if u >= 2 and exp_tok[u - 2]:
                        w.append(("sS", exp_tok[u - 2]))
                    first = True
                    nsp = (span + 511) // 512
                    for s_i in range(nsp):
                        lo, hi = s_i * 512, min(span, (s_i + 1) * 512)
                        tk = sc.op("tensor",
                                   MM(psS[slot][:, lo:hi], Qc[pq][r:r + 64, ts(j, 128)],
                                      Kb[r:r + 64, ci, lo:hi], True, True),
                                   waits=(w if first else []),
                                   sig=("sT" if s_i == nsp - 1 else None))
                        first = False
                    sc_tok[u] = tk
                    q_last_mm[pq] = tk
            ctx_st = sc.op("sync", DMA(ctx16v[:, :, ts(b, BLK)], ctxF[:, :, :]),
                           waits=[("sS", ctx_ev_tok[NU - 1])], sig="sXD", inc=16)
    sc.barrier()

    sc.mark("ph3")
    with ExitStack() as ph:
        _sb = lambda n, s, d: ph.enter_context(nc.sbuf_tensor(n, s, d))
        _ps = lambda n, s, d=F32: ph.enter_context(nc.psum_tensor(n, s, d))
        woTb = _sb("woTb", [128, 16, D], BF16)
        gamf = _sb("gamf", [128, D], F32)
        ctxh = _sb("ctxh", [128, 16, BLK], BF16)
        xnld = [_sb(f"xnld{i}", [128, D], F32) for i in range(2)]
        ytile = [_sb(f"ytile{i}", [128, D], F32) for i in range(2)]
        h2t = [_sb(f"h2t{i}", [128, D], BF16) for i in range(2)]
        h2stg = [_sb(f"h2stg{i}", [128, 16, 128], BF16) for i in range(2)]
        ysq = [_sb(f"ysq{i}", [128, D], F32) for i in range(2)]
        st3 = _sb("st3", [128, 8], F32)
        ps3 = [_ps(f"ps3_{i}", [128, 512], F32) for i in range(6)]
        psT3 = _ps("psT3", [128, NT], BF16)

        wo_ld = sc.op("gpsimd", DMA(woTb[:, :, :], woTv[:, :, :]), sig="sWD", inc=16)
        gf_ld = sc.op("sync", DMA(gamf[:, :], gam_f[:, :]), sig="sXD", inc=16)

        ev3 = [None] * 6
        ylast = [None, None]
        ydma = [None, None]
        h2dma = [None, None]
        h2ev = [None, None]
        sqts = [None, None]
        h2toks = [None] * NTT
        ctx_ld = None

        def emit_h2tr(t_j):
            pp = t_j % 2
            base = pp * 1024
            w = [("sV", h2toks[t_j])]
            if h2ev[pp]:
                w.append(("sV", h2ev[pp]))
            for ci in range(8):
                tk = sc.op("tensor", TR(psT3[:, base + ci * 128: base + ci * 128 + 128],
                                        h2t[pp][:, ts(ci, 128)], id16[:, :]),
                           waits=(w if ci == 0 else []),
                           sig=("sT" if ci == 7 else None))
            ev_a = sc.op("vector",
                         CP(h2stg[pp][:, 0:8, :],
                            psT3.ap().rearrange("p (c n) -> p c n", n=128)[:, 8 * pp: 8 * pp + 8, :]),
                         waits=[("sT", tk)], sig="sV")
            for ci in range(8):
                tk = sc.op("tensor", TR(psT3[:, base + ci * 128: base + ci * 128 + 128],
                                        h2t[pp][:, ts(8 + ci, 128)], id16[:, :]),
                           waits=([("sV", ev_a)] if ci == 0 else []),
                           sig=("sT" if ci == 7 else None))
            h2ev[pp] = sc.op("vector",
                             CP(h2stg[pp][:, 8:16, :],
                                psT3.ap().rearrange("p (c n) -> p c n", n=128)[:, 8 * pp: 8 * pp + 8, :]),
                             waits=[("sT", tk)], sig="sV")
            h2dma[pp] = sc.op("sync", DMA(h2T16v[:, :, ts(t_j, TT)],
                                          h2stg[pp][:, :, :]),
                              waits=[("sV", h2ev[pp])], sig="sXD", inc=16)

        def emit_stats(t_j):
            pp = t_j % 2
            b0 = 4 * pp
            SQC, VV, SQR, RSD = b0, b0 + 1, b0 + 2, b0 + 3
            col = lambda c: st3[:, c:c + 1]
            sqt = sc.op("scalar", ACTV(ysq[pp][:, :], ytile[pp][:, :], AF.Square,
                                       acc=col(SQC)), waits=[("sV", ytok3[t_j])],
                        sig="sS")
            sqts[pp] = sqt
            vv_tok = sc.op("vector", TSC(col(VV), col(SQC), 1.0 / D, 1e-5, ALU.mult,
                                         ALU.add), waits=[("sS", sqt)], sig="sV")
            sq2 = sc.op("scalar", ACTV(col(SQR), col(VV), AF.Sqrt),
                        waits=[("sV", vv_tok)], sig="sS")
            rst = sc.op("vector", RECIP(col(RSD), col(SQR)), waits=[("sS", sq2)],
                        sig="sV")
            w = [("sV", rst), ("sXD", gf_ld)]
            if h2dma[pp]:
                w.append(("sXD", h2dma[pp]))
            h2tok = sc.op("vector", STT(h2t[pp][:, :], ytile[pp][:, :], col(RSD),
                                        gamf[:, :], ALU.mult, ALU.mult),
                          waits=w, sig="sV", self_edge=True)
            ylast[pp] = h2tok
            h2toks[t_j] = h2tok

        ytok3 = [None] * NTT
        for half in range(2):
            w = [("sT", sc.counts["sT"])] if half > 0 else []
            ctx_ld = sc.op("sync", DMA(ctxh[:, :, :], ctx16v[:, :, ts(half, BLK)]),
                           waits=w, sig="sXD", inc=16)
            for tt in range(8):
                t_i = half * 8 + tt
                pp = t_i % 2
                w0 = [("sXD", ctx_ld), ("sWD", wo_ld)]
                slots = [(4 * t_i + k) % 6 for k in range(4)]
                mmtok = [None] * 4
                for os_ in range(4):
                    slot = slots[os_]
                    for ci in range(16):
                        w = []
                        if ci == 0:
                            if os_ == 0:
                                w += w0
                            if ev3[slot]:
                                w.append(("sV", ev3[slot]))
                        tk = sc.op("tensor", MM(ps3[slot][:, :],
                                                ctxh[:, ci, ts(tt, 128)],
                                                woTb[:, ci, ts(os_, 512)],
                                                ci == 0, ci == 15),
                                   waits=w,
                                   sig=("sT" if ci == 15 else None))
                    mmtok[os_] = tk
                if t_i >= 2:
                    emit_h2tr(t_i - 2)
                w = [("sV", ylast[pp])] if ylast[pp] else []
                xn_ld = sc.op("sync", DMA(xnld[pp][:, :], xn32[ts(t_i, TT), :]),
                              waits=w, sig="sXD", inc=16)
                for os_ in range(4):
                    off = os_ * 512
                    w = [("sT", mmtok[os_]), ("sXD", xn_ld)]
                    if os_ == 0:
                        if ydma[pp]:
                            w.append(("sXD", ydma[pp]))
                        if sqts[pp]:
                            w.append(("sS", sqts[pp]))
                    etok = sc.op("vector", TT_(ytile[pp][:, off:off + 512],
                                               ps3[slots[os_]][:, :],
                                               xnld[pp][:, off:off + 512], ALU.add),
                                 waits=w, sig="sV")
                    ev3[slots[os_]] = etok
                ytok3[t_i] = etok
                ydma[pp] = sc.op("sync", DMA(y32[ts(t_i, TT), :], ytile[pp][:, :]),
                                 waits=[("sV", etok)], sig="sXD", inc=16)
                if t_i >= 1:
                    emit_stats(t_i - 1)
        emit_stats(NTT - 1)
        emit_h2tr(NTT - 2)
        emit_h2tr(NTT - 1)
    sc.barrier()

    sc.mark("ph4")
    with ExitStack() as ph:
        _sb = lambda n, s, d: ph.enter_context(nc.sbuf_tensor(n, s, d))
        h2Tg = _sb("h2Tg", [128, 16, BLK], BF16)
        actF = _sb("actF", [128, 32, BLK], BF16)
        wup_b = [_sb(f"wup_{i}", [128, 16, 128], BF16) for i in range(4)]
        sgb = [_sb(f"sgb_{i}", [128, BLK], F32) for i in range(2)]
        wdn_b = [_sb(f"wdn_{i}", [128, 32, 128], BF16) for i in range(2)]
        dF = [_sb(f"dF_{i}", [128, BLK], BF16) for i in range(2)]
        down2T = _sb("down2T", [128, 8, D], BF16)
        yb = [_sb(f"yb_{i}", [128, D], F32) for i in range(2)]
        d2f = _sb("d2f", [128, D], F32)
        ab = _sb("ab", [128, D], F32)

        wtu = [None, None]
        mmup = [None, None]
        evup = [None, None]
        wtd = [None, None]
        mmd = [None, None]
        evd = [None, None]
        trd_tok = [None, None]
        d2ev_last = None
        adma = None
        atok = [None, None]
        up_mm_last = None
        down_mm_last = None
        down_psum_free = []
        h2_ld_next = sc.op("sync", DMA(h2Tg[:, :, :], h2T16v[:, :, ts(0, BLK)]),
                           sig="sXD", inc=16)
        for g in range(2):
            h2_ld = h2_ld_next
            with ExitStack() as php:
                _ps2 = lambda n, s, d=F32: php.enter_context(nc.psum_tensor(n, s, d))
                ps4 = [_ps2(f"ps4_{g}_{i}", [128, BLK], F32) for i in range(4)]
                for c in range(32):
                    p = c % 2
                    pg_, pv_ = 2 * p, 2 * p + 1
                    w = [("sT", mmup[p])] if mmup[p] else []
                    sc.op("gpsimd", DMA(wup_b[pg_][:, :, :], wupTv[:, :, ts(c, 128)]),
                          waits=w, sig="sWD", inc=16)
                    wtu[p] = sc.op("gpsimd", DMA(wup_b[pv_][:, :, :],
                                                 wupTv[:, :, ts(c + 32, 128)]),
                                   sig="sWD", inc=16)
                    w = [("sWD", wtu[p]), ("sXD", h2_ld)]
                    if evup[p]:
                        w.append(("sV", evup[p]))
                    if c < 2:
                        for t_ in down_psum_free:
                            w.append(t_)
                    first = True
                    for which in range(2):
                        slot = 2 * p + which
                        for sp in range(2):
                            for ci in range(16):
                                tk = sc.op("tensor",
                                           MM(ps4[slot][:, ts(sp, 512)],
                                              wup_b[slot][:, ci, :],
                                              h2Tg[:, ci, ts(sp, 512)], ci == 0, ci == 15),
                                           waits=(w if first else []),
                                           sig=("sT" if (which == 1 and sp == 1 and ci == 15) else None))
                                first = False
                    mmup[p] = tk
                    w = [("sT", tk)]
                    if evup[p]:
                        w.append(("sV", evup[p]))
                    wact = list(w)
                    if c < 2 and down_mm_last:
                        wact.append(("sT", down_mm_last))
                    sg_tok = sc.op("scalar", ACTV(sgb[p][:, :], ps4[pg_][:, :], AF.Silu),
                                   waits=w, sig="sS")
                    wact.append(("sS", sg_tok))
                    evup[p] = sc.op("vector", TT_(actF[:, c, :], sgb[p][:, :],
                                                  ps4[pv_][:, :], ALU.mult),
                                    waits=wact, sig="sV")
                up_done = sc.counts["sV"]
                up_mm_last = mmup[0] if (mmup[0] or 0) > (mmup[1] or 0) else mmup[1]
            with ExitStack() as php:
                _ps2 = lambda n, s, d=F32: php.enter_context(nc.psum_tensor(n, s, d))
                ps4d = [_ps2(f"ps4d_{g}_{i}", [128, BLK], F32) for i in range(2)]
                psT4 = _ps2(f"psT4_{g}", [128, 2 * BLK], BF16)

                def emit_tr4(oc):
                    nonlocal d2ev_last
                    p = oc % 2
                    base = p * 1024
                    w = [("sV", evd[p])]
                    if d2ev_last:
                        w.append(("sV", d2ev_last))
                    for ti in range(8):
                        tk = sc.op("tensor",
                                   TR(psT4[:, base + ti * 128: base + ti * 128 + 128],
                                      dF[p][:, ts(ti, 128)], id16[:, :]),
                                   waits=(w if ti == 0 else []),
                                   sig=("sT" if ti == 7 else None))
                    trd_tok[p] = tk
                    d2ev_last = sc.op("vector",
                                      CP(down2T[:, :, ts(oc, 128)],
                                         psT4.ap().rearrange("p (c n) -> p c n", n=128)[:, 8 * p: 8 * p + 8, :]),
                                      waits=[("sT", tk)], sig="sV")

                for oc in range(16):
                    p = oc % 2
                    w = [("sV", up_done)]
                    if mmd[p]:
                        w.append(("sT", mmd[p]))
                    wtd[p] = sc.op("gpsimd", DMA(wdn_b[p][:, :, :],
                                                 wdnTv[:, :, ts(oc, 128)]),
                                   waits=w, sig="sWD", inc=16)
                    w = [("sWD", wtd[p])]
                    if evd[p]:
                        w.append(("sV", evd[p]))
                    if oc < 2 and g == 0:
                        pass
                    first = True
                    for sp in range(2):
                        for ic in range(32):
                            tk = sc.op("tensor", MM(ps4d[p][:, ts(sp, 512)],
                                                    wdn_b[p][:, ic, :],
                                                    actF[:, ic, ts(sp, 512)],
                                                    ic == 0, ic == 31),
                                       waits=(w if first else []),
                                       sig=("sT" if (sp == 1 and ic == 31) else None))
                            first = False
                    mmd[p] = tk
                    if oc >= 1:
                        emit_tr4(oc - 1)
                    w = [("sT", tk)]
                    if trd_tok[p]:
                        w.append(("sT", trd_tok[p]))
                    evd[p] = sc.op("vector", CP(dF[p][:, :], ps4d[p][:, :]),
                                   waits=w, sig="sV")
                emit_tr4(15)
                down_mm_last = mmd[0] if (mmd[0] or 0) > (mmd[1] or 0) else mmd[1]
                down_psum_free = [("sV", d2ev_last), ("sV", evd[0]), ("sV", evd[1])]
                if g == 0:
                    h2_ld_next = sc.op("sync",
                                       DMA(h2Tg[:, :, :], h2T16v[:, :, ts(1, BLK)]),
                                       waits=[("sT", up_mm_last)], sig="sXD", inc=16)
                for ti in range(8):
                    t_i = g * 8 + ti
                    pp = ti % 2
                    w = [("sV", atok[pp])] if atok[pp] else []
                    y_ld = sc.op("scalar", DMA(yb[pp][:, :], y32[ts(t_i, TT), :]),
                                 waits=w, sig="sSD", inc=16)
                    w = [("sV", d2ev_last)]
                    if atok[1 - pp]:
                        w.append(("sV", atok[1 - pp]))
                    cvt = sc.op("vector", CP(d2f[:, :], down2T[:, ti, :]),
                                waits=w, sig="sV")
                    w = [("sSD", y_ld), ("sV", cvt)]
                    if adma:
                        w.append(("sXD", adma))
                    atok[pp] = sc.op("vector", TT_(ab[:, :], yb[pp][:, :],
                                                   d2f[:, :], ALU.add),
                                     waits=w, sig="sV")
                    adma = sc.op("sync", DMA(att32[ts(t_i, TT), :], ab[:, :]),
                                 waits=[("sV", atok[pp])], sig="sXD", inc=16)
    sc.barrier()

    sc.mark("ph5a")
    with ExitStack() as ph:
        _sb = lambda n, s, d: ph.enter_context(nc.sbuf_tensor(n, s, d))
        _ps = lambda n, s, d=F32: ph.enter_context(nc.psum_tensor(n, s, d))
        xnT2 = _sb("xnT2", [128, 16, NT], BF16)
        wgb = [_sb(f"wgb{i}", [128, 16, 512], BF16) for i in range(2)]
        wub = [_sb(f"wub{i}", [128, 16, 512], BF16) for i in range(2)]
        ps5 = [_ps(f"ps5_{i}", [128, 512], F32) for i in range(4)]
        psT5 = _ps("psT5", [128, 2 * BLK], BF16)
        sg5 = [_sb(f"sg5{i}", [128, 512], F32) for i in range(2)]
        hb = [_sb(f"hb{i}", [128, 512], BF16) for i in range(2)]
        qb = [_sb(f"qb{i}", [128, 512], BF16) for i in range(2)]
        cb = [_sb(f"cb{i}", [128, 6, 128], BF16) for i in range(2)]
        tb = [_sb(f"tb{i}", [128, 4, 128], BF16) for i in range(2)]
        kb = [_sb(f"kb{i}", [128, 512], BF16) for i in range(2)]
        hsb = [_sb(f"hsb{i}", [128, 512], BF16) for i in range(2)]
        stg5 = [_sb(f"stg5_{i}", [128, 4, 512], BF16) for i in range(2)]

        xnT2_ld = sc.op("sync", DMA(xnT2[:, :, :], xnT16v[:, :, :]), sig="sXD", inc=16)
        wt = [None, None]
        mm5 = [None, None]
        ev5 = [None] * (16 * NTT)
        sg_prev = [None, None]
        ht_prev = [None, None]
        tr_tok = [None, None]
        tr5ev = [None, None]
        stgdma = [None, None]
        TSKEW = 2

        def emit_tr5(v):
            s_v, t_v = divmod(v, NTT)
            pt = v % 2
            base = pt * 1024
            w = [("sV", ev5[v])]
            if tr5ev[pt]:
                w.append(("sV", tr5ev[pt]))
            for ii in range(4):
                tk = sc.op("tensor", TR(psT5[:, base + ii * 128: base + ii * 128 + 128],
                                        hsb[pt][:, ts(ii, 128)], id16[:, :]),
                           waits=(w if ii == 0 else []),
                           sig=("sT" if ii == 3 else None))
            tr_tok[pt] = tk
            tsub = t_v % 4
            pb = (t_v // 4) % 2
            w = [("sT", tk)]
            if tsub == 0 and stgdma[pb]:
                w.append(("sXD", stgdma[pb]))
            tr5ev[pt] = sc.op("vector",
                              CP(stg5[pb].ap().rearrange("p c (u n) -> p c u n", n=128)[:, :, tsub, :],
                                 psT5.ap().rearrange("p (c n) -> p c n", n=128)[:, 8 * pt: 8 * pt + 4, :]),
                              waits=w, sig="sV")
            if tsub == 3:
                stgdma[pb] = sc.op("sync",
                                   DMA(hsT16v[:, ts(s_v, 4), t_v // 4 * 512: t_v // 4 * 512 + 512],
                                       stg5[pb][:, :, :]),
                                   waits=[("sV", tr5ev[0]), ("sV", tr5ev[1])],
                                   sig="sXD", inc=16)

        for s in range(16):
            pw = s % 2
            w = [("sT", mm5[pw])] if wt[pw] else []
            sc.op("gpsimd", DMA(wgb[pw][:, :, :], wgv[:, :, ts(s, 512)]), waits=w,
                  sig="sWD", inc=16)
            wt[pw] = sc.op("gpsimd", DMA(wub[pw][:, :, :], wuv[:, :, ts(s, 512)]),
                           sig="sWD", inc=16)
            for t_i in range(NTT):
                u = s * NTT + t_i
                pt = u % 2
                gsl, usl = 2 * pt, 2 * pt + 1
                w = [("sWD", wt[pw]), ("sXD", xnT2_ld)]
                if sg_prev[pt]:
                    w.append(("sS", sg_prev[pt]))
                if ht_prev[pt]:
                    w.append(("sV", ht_prev[pt]))
                first = True
                for which in range(2):
                    slot = 2 * pt + which
                    wsrc = wgb[pw] if which == 0 else wub[pw]
                    for ci in range(16):
                        tk = sc.op("tensor", MM(ps5[slot][:, :],
                                                xnT2[:, ci, ts(t_i, 128)],
                                                wsrc[:, ci, :], ci == 0, ci == 15),
                                   waits=(w if first else []),
                                   sig=("sT" if (which == 1 and ci == 15) else None))
                        first = False
                mm5[pw] = tk
                if u >= TSKEW:
                    emit_tr5(u - TSKEW)
                w = [("sT", tk)]
                if ht_prev[pt]:
                    w.append(("sV", ht_prev[pt]))
                sg_prev[pt] = sc.op("scalar", ACTV(sg5[pt][:, :], ps5[gsl][:, :],
                                                   AF.Silu), waits=w, sig="sS")
                w = [("sS", sg_prev[pt])]
                if tr_tok[pt]:
                    w.append(("sT", tr_tok[pt]))
                htok = sc.op("vector", TT_(hb[pt][:, :], sg5[pt][:, :], ps5[usl][:, :],
                                           ALU.mult), waits=w, sig="sV")
                ht_prev[pt] = htok
                A = lambda buf: buf[:, 0:128]
                B = lambda buf: buf[:, 128:256]
                C = lambda buf: buf[:, 256:384]
                Dp = lambda buf: buf[:, 384:512]
                sc.op("vector", TT_(qb[pt][:, :], hb[pt][:, :], hb[pt][:, :], ALU.mult),
                      sig="sV")
                q_ = qb[pt]
                c_ = cb[pt]
                sc.op("vector", TT_(c_[:, 0, :], A(q_), B(q_), ALU.is_ge), sig="sV")
                sc.op("vector", TT_(c_[:, 1, :], A(q_), C(q_), ALU.is_ge), sig="sV")
                sc.op("vector", TT_(c_[:, 2, :], A(q_), Dp(q_), ALU.is_ge), sig="sV")
                sc.op("vector", TT_(c_[:, 3, :], B(q_), C(q_), ALU.is_ge), sig="sV")
                sc.op("vector", TT_(c_[:, 4, :], B(q_), Dp(q_), ALU.is_ge), sig="sV")
                sc.op("vector", TT_(c_[:, 5, :], C(q_), Dp(q_), ALU.is_ge), sig="sV")
                t_ = tb[pt]
                k_ = kb[pt]
                tA, tB, tC, tD = t_[:, 0, :], t_[:, 1, :], t_[:, 2, :], t_[:, 3, :]
                sc.op("vector", TT_(tA, c_[:, 0, :], c_[:, 1, :], ALU.add), sig="sV")
                sc.op("vector", TT_(tB, c_[:, 3, :], c_[:, 4, :], ALU.add), sig="sV")
                sc.op("vector", TT_(tC, c_[:, 1, :], c_[:, 3, :], ALU.add), sig="sV")
                sc.op("vector", TT_(tD, c_[:, 2, :], c_[:, 4, :], ALU.add), sig="sV")
                sc.op("vector", TT_(tA, tA, c_[:, 2, :], ALU.add), sig="sV")
                sc.op("vector", TT_(tB, tB, c_[:, 0, :], ALU.subtract), sig="sV")
                sc.op("vector", TT_(tC, tC, c_[:, 5, :], ALU.subtract), sig="sV")
                sc.op("vector", TT_(tD, tD, c_[:, 5, :], ALU.add), sig="sV")
                sc.op("vector", TSC(A(k_), tA, 1.5, None, ALU.is_ge), sig="sV")
                sc.op("vector", TSC(B(k_), tB, 0.5, None, ALU.is_ge), sig="sV")
                sc.op("vector", TSC(C(k_), tC, 0.5, None, ALU.is_le), sig="sV")
                sc.op("vector", TSC(Dp(k_), tD, 1.5, None, ALU.is_le), sig="sV")
                ev5[u] = sc.op("vector", TT_(hsb[pt][:, :], hb[pt][:, :], kb[pt][:, :],
                                             ALU.mult), sig="sV", self_edge=True)
        for v in range(16 * NTT - TSKEW, 16 * NTT):
            emit_tr5(v)
    sc.barrier()

    sc.mark("ph5b")
    with ExitStack() as ph:
        _sb = lambda n, s, d: ph.enter_context(nc.sbuf_tensor(n, s, d))
        _ps = lambda n, s, d=F32: ph.enter_context(nc.psum_tensor(n, s, d))
        hsTq = _sb("hsTq", [128, 64, 512], BF16)
        wdb = [_sb(f"wdb{i}", [128, 64, 128], BF16) for i in range(4)]
        dFb = [_sb(f"dFb{i}", [128, 512], BF16) for i in range(2)]
        ffnT = _sb("ffnT", [128, 4, D], BF16)
        xb = [_sb(f"xb5{i}", [128, D], F32) for i in range(2)]
        atb = [_sb(f"atb5{i}", [128, D], F32) for i in range(2)]
        ff32 = _sb("ff32", [128, D], F32)
        ob = _sb("ob5", [128, D], F32)
        ps5d = [_ps(f"ps5d_{i}", [128, 512], F32) for i in range(2)]
        psT5b = _ps("psT5b", [128, 2 * BLK], BF16)

        hs_ld = None
        odma = [None, None]
        wt = [None, None, None, None]
        mmd = [None, None, None, None]
        slot_oc = [None, None, None, None]
        evd = [None, None]
        tr_tok = [None, None]
        f_ev_last = None
        def preload_hs(qt):
            w = [("sT", t) for t in mmd if t]
            a = sc.op("sync", DMA(hsTq[:, 0:32, :], hsT16v[:, 0:32, ts(qt, 512)]),
                      waits=w, sig="sXD", inc=16)
            b = sc.op("sync", DMA(hsTq[:, 32:64, :], hsT16v[:, 32:64, ts(qt, 512)]),
                      sig="sXD", inc=16)
            return a, b

        hs_next = preload_hs(0)
        for qt in range(4):
            hs_ld1, hs_ld = hs_next

            def emit_tr5b(oc):
                nonlocal f_ev_last
                p = oc % 2
                base = p * 1024
                w = [("sV", evd[p])]
                if f_ev_last:
                    w.append(("sV", f_ev_last))
                for ii in range(4):
                    tk = sc.op("tensor",
                               TR(psT5b[:, base + ii * 128: base + ii * 128 + 128],
                                  dFb[p][:, ts(ii, 128)], id16[:, :]),
                               waits=(w if ii == 0 else []),
                               sig=("sT" if ii == 3 else None))
                tr_tok[p] = tk
                f_ev_last = sc.op("vector",
                                  CP(ffnT[:, :, ts(oc, 128)],
                                     psT5b.ap().rearrange("p (c n) -> p c n", n=128)[:, 8 * p: 8 * p + 4, :]),
                                  waits=[("sT", tk)], sig="sV")

            oc_order = [(12 * qt + k) % 16 for k in range(16)]
            for oc in oc_order:
                p = oc % 2
                p4 = oc % 4
                if slot_oc[p4] != oc:
                    w = []
                    if wt[p4]:
                        w.append(("sT", mmd[p4]))
                    wt[p4] = sc.op("gpsimd", DMA(wdb[p4][:, :, :],
                                                 wdv[:, :, ts(oc, 128)]),
                                   waits=w, sig="sWD", inc=16)
                    slot_oc[p4] = oc
                w = [("sWD", wt[p4]), ("sXD", hs_ld1)]
                if evd[p]:
                    w.append(("sV", evd[p]))
                first = True
                for ic in range(64):
                    if ic == 32:
                        w2 = [("sXD", hs_ld)]
                    else:
                        w2 = []
                    tk = sc.op("tensor", MM(ps5d[p][:, :], wdb[p4][:, ic, :],
                                            hsTq[:, ic, :], ic == 0, ic == 63),
                               waits=(w if first else w2),
                               sig=("sT" if ic == 63 else None))
                    first = False
                mmd[p4] = tk
                ko = oc_order.index(oc)
                if ko >= 1:
                    emit_tr5b(oc_order[ko - 1])
                w = [("sT", tk)]
                if tr_tok[p]:
                    w.append(("sT", tr_tok[p]))
                evd[p] = sc.op("vector", CP(dFb[p][:, :], ps5d[p][:, :]),
                               waits=w, sig="sV")
            emit_tr5b(oc_order[15])
            if qt < 3:
                hs_next = preload_hs(qt + 1)
            for ti in range(4):
                t_i = qt * 4 + ti
                pp = ti % 2
                w = [("sXD", odma[pp])] if odma[pp] else []
                x_ld = sc.op("scalar", DMA(xb[pp][:, :], x[ts(t_i, TT), :]), waits=w,
                             sig="sSD", inc=16)
                at_ld = sc.op("sync", DMA(atb[pp][:, :], att32[ts(t_i, TT), :]),
                              waits=w, sig="sXD", inc=16)
                cvt = sc.op("vector", CP(ff32[:, :], ffnT[:, ti, :]),
                            waits=[("sV", f_ev_last)], sig="sV")
                wo_ = [("sSD", x_ld), ("sXD", at_ld), ("sV", cvt)]
                for t_ in odma:
                    if t_:
                        wo_.append(("sXD", t_))
                o1 = sc.op("vector", TT_(ob[:, :], xb[pp][:, :], atb[pp][:, :],
                                         ALU.add), waits=wo_, sig="sV")
                o2 = sc.op("vector", TT_(ob[:, :], ob[:, :], ff32[:, :],
                                         ALU.add), sig="sV")
                odma[pp] = sc.op("sync", DMA(out[ts(t_i, TT), :], ob[:, :]),
                                 waits=[("sV", o2)], sig="sXD", inc=16)

    sc.op("sync", (lambda e: e.nop()),
          waits=[("sXD", sc.counts["sXD"]), ("sSD", sc.counts["sSD"]),
                 ("sWD", sc.counts["sWD"])])

    sc.mark("tail")
    sc.emit()
    nc.finalize()
    nc._sched = sc
    nc._keepalive = stack
    return nc


def _host_inputs(inputs):
    f32 = np.float32
    xall = np.ascontiguousarray(np.asarray(inputs["L0_attn_x"], f32).reshape(NCORE * NT, D))
    wqkvT = np.ascontiguousarray(np.asarray(inputs["L0_attn_w_qkv"], f32).T)
    woT = np.ascontiguousarray(np.asarray(inputs["L0_attn_w_o"], f32).T)
    wupT = np.ascontiguousarray(np.asarray(inputs["L0_attn_w_up"], f32).T)
    wdnT = np.ascontiguousarray(np.asarray(inputs["L0_attn_w_down"], f32).T)
    wg0 = np.asarray(inputs["L0_ffn_w_gate"], f32)
    wu0 = np.asarray(inputs["L0_ffn_w_up"], f32)
    wd0 = np.asarray(inputs["L0_ffn_w_down"], f32)
    gidx = np.arange(IF) // 4
    m = np.arange(IF) % 4
    new = (gidx // 128) * 512 + m * 128 + (gidx % 128)
    inv = np.empty(IF, np.int64)
    inv[new] = np.arange(IF)
    wg_ = np.ascontiguousarray(wg0[:, inv])
    wu_ = np.ascontiguousarray(wu0[:, inv])
    wd_ = np.ascontiguousarray(wd0[inv, :])
    gam_a = np.ascontiguousarray(np.tile(np.asarray(inputs["L0_attn_attn_norm_w"], f32)[None, :], (128, 1)))
    gam_f = np.ascontiguousarray(np.tile(np.asarray(inputs["L0_attn_ffn_norm_w"], f32)[None, :], (128, 1)))
    ident32 = np.eye(128, dtype=f32)
    ident16 = np.eye(128, dtype=ml_dtypes.bfloat16)
    qq, kk = np.meshgrid(np.arange(128), np.arange(128), indexing="ij")
    masktri = np.where(kk > qq, NEG, 0.0).astype(f32)
    invf = (1.0 / 10000.0 ** (np.arange(0, HD, 2, dtype=np.float64) / HD)).astype(f32)
    p = np.arange(128)
    sgn = np.where((p % 64) < 32, -1.0, 1.0).astype(f32)
    maps = []
    for c in range(NCORE):
        pos = ((c % 2) * NT + np.arange(NT)).astype(f32)
        ang = pos[None, :] * invf[p % 32][:, None]
        cosT = np.cos(ang).astype(f32)
        sinmT = (sgn[:, None] * np.sin(ang)).astype(f32)
        maps.append({
            "x": np.ascontiguousarray(xall[c * NT:(c + 1) * NT]),
            "wqkvT": wqkvT, "woT": woT, "wupT": wupT, "wdnT": wdnT,
            "wg": wg_, "wu": wu_, "wd": wd_,
            "gam_a": gam_a, "gam_f": gam_f,
            "cosT": np.ascontiguousarray(cosT),
            "sinmT": np.ascontiguousarray(sinmT),
            "masktri": masktri, "ident32": ident32, "ident16": ident16,
        })
    return maps


def kernel(**inputs):
    maps = _host_inputs(inputs)
    nc = build()
    res = run_bass_kernel_spmd(nc, maps, core_ids=list(range(NCORE)))
    outs = [res.results[c]["out"] for c in range(NCORE)]
    return np.concatenate(outs, axis=0).reshape(4, 4096, D).astype(np.float32)


# version 10
# speedup vs baseline: 1.0168x; 1.0168x over previous
import numpy as np
import ml_dtypes
import concourse.bass as bass
import concourse.mybir as mybir
from concourse.bass_utils import run_bass_kernel_spmd
from contextlib import ExitStack

F32 = mybir.dt.float32
BF16 = mybir.dt.bfloat16
AF = mybir.ActivationFunctionType
ALU = mybir.AluOpType
AX = mybir.AxisListType

D = 2048
NT = 2048
TT = 128
NTT = 16
BLK = 1024
NBLK = 2
NH = 32
HD = 64
QKV = 3 * D
IA = 8192
IAH = 4096
IF = 8192
NCORE = 8
NEG = -1.0e30

DEBUG_TAPS = False


def ts(i, n):
    return slice(i * n, (i + 1) * n)


def DMA(out, in_):
    return lambda e: e.dma_start(out=out, in_=in_)


def MM(out, lhsT, rhs, start, stop):
    return lambda e: e.matmul(out, lhsT, rhs, start=start, stop=stop)


def TR(out, in_, idn):
    return lambda e: e.transpose(out, in_, idn)


def CP(out, in_):
    return lambda e: e.tensor_copy(out, in_)


def SCP(out, in_):
    return lambda e: e.copy(out, in_)


def TT_(out, a, b, op):
    return lambda e: e.tensor_tensor(out, a, b, op)


def TSC(out, a, s1, s2, op0, op1=None, acc=None):
    if op1 is None:
        return lambda e: e.tensor_scalar(out, a, s1, None, op0, accum_out=acc)
    return lambda e: e.tensor_scalar(out, a, s1, s2, op0, op1, accum_out=acc)


def STT(out, in0, s, in1, op0, op1):
    return lambda e: e.scalar_tensor_tensor(out, in0, s, in1, op0, op1)


def ACTV(out, in_, f, bias=0.0, scale=1.0, acc=None):
    return lambda e: e.activation(out, in_, f, bias=bias, scale=scale, accum_out=acc)


def RSUM(out, in_):
    return lambda e: e.reduce_sum(out, in_, axis=AX.X)


def RECIP(out, in_):
    return lambda e: e.reciprocal(out, in_)


class Sched:
    ENGS = ("sync", "scalar", "vector", "tensor", "gpsimd")

    def __init__(self, nc, stack):
        self.nc = nc
        self.ops = {e: [] for e in self.ENGS}
        self.sems = {}
        self.counts = {}
        self.guaranteed = {e: {} for e in self.ENGS}
        self.pending = None
        self.pending_applied = set()
        self.marks = []
        self.names = {e: [] for e in self.ENGS}

    def mark(self, label):
        self.marks.append((label, {e: len(self.ops[e]) for e in self.ENGS}))

    def sem(self, name, stack):
        self.sems[name] = stack.enter_context(self.nc.semaphore(name))
        self.counts[name] = 0
        return name

    def barrier(self):
        self.pending = {s: c for s, c in self.counts.items() if c > 0}
        self.pending_applied = set()

    def op(self, eng, emit, waits=(), sig=None, inc=1, self_edge=False):
        tok = None
        if self_edge and sig is not None and self.counts[sig] > 0:
            waits = list(waits) + [(sig, self.counts[sig])]
        if sig is not None:
            self.counts[sig] += inc
            tok = self.counts[sig]
        ws = []
        if self.pending is not None and eng not in self.pending_applied:
            self.pending_applied.add(eng)
            for s, v in self.pending.items():
                ws.append((s, v))
        for s, v in waits:
            if v is None:
                continue
            ws.append((s, v))
        g = self.guaranteed[eng]
        ws2 = []
        for s, v in ws:
            if g.get(s, -1) >= v:
                continue
            g[s] = v
            ws2.append((s, v))
        self.ops[eng].append((emit, ws2, sig, inc))
        return tok

    def emit(self):
        nc = self.nc
        with nc.Block(no_gpsimd_drain=True) as block:
            for eng in self.ENGS:
                deco = getattr(block, eng)
                lst = self.ops[eng]
                sems = self.sems

                def make(lst, eng):
                    def body(handle):
                        nm = self.names[eng]
                        for emit, ws, sig, inc in lst:
                            for s, v in ws:
                                handle.wait_ge(sems[s], v)
                            inst = emit(handle)
                            if sig is not None:
                                inst.then_inc(sems[sig], inc)
                            try:
                                nm.append(inst.ins.name)
                            except Exception:
                                nm.append(None)
                    return body

                deco(make(lst, eng))


def build():
    nc = bass.Bass("TRN2", target_bir_lowering=False)
    dbg = "ExternalOutput" if DEBUG_TAPS else "Internal"

    x = nc.dram_tensor("x", [NT, D], F32, kind="ExternalInput")
    wqkvT = nc.dram_tensor("wqkvT", [48, 128, 16, 128], F32, kind="ExternalInput")
    wvT = nc.dram_tensor("wvT", [4, 128, 16, 512], F32, kind="ExternalInput")
    woT = nc.dram_tensor("woT", [D, D], F32, kind="ExternalInput")
    wupT = nc.dram_tensor("wupT", [64, 128, 16, 128], F32, kind="ExternalInput")
    wdnT = nc.dram_tensor("wdnT", [16, 128, 32, 128], F32, kind="ExternalInput")
    wg = nc.dram_tensor("wg", [16, 128, 16, 512], F32, kind="ExternalInput")
    wu = nc.dram_tensor("wu", [16, 128, 16, 512], F32, kind="ExternalInput")
    wd = nc.dram_tensor("wd", [16, 128, 64, 128], F32, kind="ExternalInput")
    gam_a = nc.dram_tensor("gam_a", [128, D], F32, kind="ExternalInput")
    gam_f = nc.dram_tensor("gam_f", [128, D], F32, kind="ExternalInput")
    cosTd = nc.dram_tensor("cosT", [128, NT], F32, kind="ExternalInput")
    sinmTd = nc.dram_tensor("sinmT", [128, NT], F32, kind="ExternalInput")
    masktri = nc.dram_tensor("masktri", [128, 128], F32, kind="ExternalInput")
    ident32d = nc.dram_tensor("ident32", [128, 128], F32, kind="ExternalInput")
    ident16d = nc.dram_tensor("ident16", [128, 128], BF16, kind="ExternalInput")

    out = nc.dram_tensor("out", [NT, D], F32, kind="ExternalOutput")

    xn32 = nc.dram_tensor("xn32", [NT, D], F32, kind=dbg)
    y32 = nc.dram_tensor("y32", [NT, D], F32, kind=dbg)
    att32 = nc.dram_tensor("att32", [NT, D], F32, kind=dbg)
    xnT16 = nc.dram_tensor("xnT16", [16, 128, NT], BF16, kind=dbg)
    qk16 = nc.dram_tensor("qk16", [32, 128, NT], BF16, kind=dbg)
    v16 = nc.dram_tensor("v16", [16, 128, D], BF16, kind=dbg)
    ctx16 = nc.dram_tensor("ctx16", [16, 128, NT], BF16, kind=dbg)
    h2T16 = nc.dram_tensor("h2T16", [16, 128, NT], BF16, kind=dbg)
    hsT16 = nc.dram_tensor("hsT16", [64, 128, NT], BF16, kind=dbg)

    xnT16v = xnT16.ap().rearrange("c p n -> p c n")
    qk16v = qk16.ap().rearrange("c p n -> p c n")
    v16v = v16.ap().rearrange("c p n -> p c n")
    ctx16v = ctx16.ap().rearrange("c p n -> p c n")
    h2T16v = h2T16.ap().rearrange("c p n -> p c n")
    hsT16v = hsT16.ap().rearrange("c p n -> p c n")
    woTv = woT.ap().rearrange("(c p) n -> p c n", p=128)

    stack = ExitStack()
    sc = Sched(nc, stack)
    for s in ("sWD", "sXD", "sSD", "sT", "sV", "sS"):
        sc.sem(s, stack)

    def sb(name, shape, dtype):
        return stack.enter_context(nc.sbuf_tensor(name, shape, dtype))

    id32 = sb("id32", [128, 128], F32)
    id16 = sb("id16", [128, 128], BF16)
    msk = sb("msk", [128, 128], F32)
    sc.op("sync", DMA(id32[:, :], ident32d[:, :]), sig="sXD", inc=16)
    sc.op("sync", DMA(id16[:, :], ident16d[:, :]), sig="sXD", inc=16)
    mask_tok = sc.op("sync", DMA(msk[:, :], masktri[:, :]), sig="sXD", inc=16)

    h1T_cm = nc.sbuf_tensor("h1T", [128, 16, NT], BF16)
    h1T = h1T_cm.__enter__()

    sc.mark("ph0")
    with ExitStack() as ph:
        _sb = lambda n, s, d: ph.enter_context(nc.sbuf_tensor(n, s, d))
        _ps = lambda n, s, d=F32: ph.enter_context(nc.psum_tensor(n, s, d))
        xt = [_sb(f"xt{i}", [128, D], F32) for i in range(2)]
        xnsb = [_sb(f"xnsb{i}", [128, D], F32) for i in range(2)]
        h1sb = [_sb(f"h1sb{i}", [128, D], F32) for i in range(2)]
        sqscr = [_sb(f"sqscr{i}", [128, D], F32) for i in range(2)]
        stgx = [_sb(f"stgx{i}", [128, 16, 128], BF16) for i in range(2)]
        st0 = _sb("st0", [128, 32], F32)
        gama = _sb("gama", [128, D], F32)
        psTr = _ps("psTr", [128, D], F32)

        ga_tok = sc.op("sync", DMA(gama[:, :], gam_a[:, :]), sig="sXD", inc=16)
        ld = [None, None]
        xn_done = [None, None]
        h1_done = [None, None]
        dx = [None, None]
        dxT = [None, None]
        h1ev = [None, None]
        trev_last = None
        for t_i in range(NTT):
            p = t_i % 2
            b0 = 16 * p
            (S1, M, MSQ, VAR, VA, SQT, RSTD, S2C, BB, S2S, S2R) = (
                b0, b0 + 1, b0 + 2, b0 + 3, b0 + 4, b0 + 5, b0 + 6, b0 + 7,
                b0 + 8, b0 + 9, b0 + 10)
            col = lambda c: st0[:, c:c + 1]
            w = [("sV", xn_done[p])] if xn_done[p] else []
            ld[p] = sc.op("sync", DMA(xt[p][:, :], x[ts(t_i, TT), :]), waits=w,
                          sig="sXD", inc=16)
            w = [("sXD", ld[p])]
            if h1_done[p]:
                w.append(("sV", h1_done[p]))
            s2tok = sc.op("scalar", ACTV(sqscr[p][:, :], xt[p][:, :], AF.Square,
                                         acc=col(S2C)), waits=w, sig="sS")
            sc.op("vector", RSUM(col(S1), xt[p][:, :]), waits=[("sXD", ld[p])], sig="sV")
            sc.op("vector", TSC(col(M), col(S1), 1.0 / D, None, ALU.mult), sig="sV",
                  self_edge=True)
            sc.op("vector", TT_(col(MSQ), col(M), col(M), ALU.mult), sig="sV",
                  self_edge=True)
            sc.op("vector", TSC(col(VAR), col(S2C), 1.0 / D, col(MSQ), ALU.mult,
                                ALU.subtract), waits=[("sS", s2tok)], sig="sV",
                  self_edge=True)
            va_tok = sc.op("vector", TSC(col(VA), col(VAR), 1e-5, None, ALU.add),
                           sig="sV", self_edge=True)
            sqtok = sc.op("scalar", ACTV(col(SQT), col(VA), AF.Sqrt),
                          waits=[("sV", va_tok)], sig="sS")
            sc.op("vector", RECIP(col(RSTD), col(SQT)), waits=[("sS", sqtok)], sig="sV")
            w = []
            if dxT[p]:
                w.append(("sXD", dxT[p]))
            xn_done[p] = sc.op("vector", TSC(xnsb[p][:, :], xt[p][:, :], col(M),
                                             col(RSTD), ALU.subtract, ALU.mult),
                               waits=w, sig="sV", self_edge=True)
            dx[p] = sc.op("sync", DMA(xn32[ts(t_i, TT), :], xnsb[p][:, :]),
                          waits=[("sV", xn_done[p])], sig="sXD", inc=16)
            sc.op("vector", RECIP(col(BB), col(VA)), sig="sV")
            sc.op("vector", TT_(col(BB), col(VAR), col(BB), ALU.mult), sig="sV",
                  self_edge=True)
            bb_tok = sc.op("vector", TSC(col(BB), col(BB), 1e-5, None, ALU.add),
                           sig="sV", self_edge=True)
            sq2 = sc.op("scalar", ACTV(col(S2S), col(BB), AF.Sqrt),
                        waits=[("sV", bb_tok)], sig="sS")
            sc.op("vector", RECIP(col(S2R), col(S2S)), waits=[("sS", sq2)], sig="sV")
            w = [("sXD", ga_tok)]
            if h1ev[p]:
                w.append(("sV", h1ev[p]))
            h1_done[p] = sc.op("vector", STT(h1sb[p][:, :], xnsb[p][:, :], col(S2R),
                                             gama[:, :], ALU.mult, ALU.mult),
                               waits=w, sig="sV", self_edge=True)
            w = [("sV", xn_done[p])]
            if trev_last:
                w.append(("sV", trev_last))
            for ci in range(16):
                tk = sc.op("tensor", TR(psTr[:, ts(ci, 128)], xnsb[p][:, ts(ci, 128)],
                                        id32[:, :]),
                           waits=(w if ci == 0 else []), sig=("sT" if ci == 15 else None))
            wv = [("sT", tk)]
            if dxT[p]:
                wv.append(("sXD", dxT[p]))
            trev = sc.op("vector", CP(stgx[p][:, :, :],
                                      psTr.ap().rearrange("p (c n) -> p c n", n=128)),
                         waits=wv, sig="sV")
            dxT[p] = sc.op("sync", DMA(xnT16v[:, :, ts(t_i, TT)], stgx[p][:, :, :]),
                           waits=[("sV", trev)], sig="sXD", inc=16)
            w = [("sV", h1_done[p]), ("sV", trev)]
            for ci in range(16):
                tk = sc.op("tensor", TR(psTr[:, ts(ci, 128)], h1sb[p][:, ts(ci, 128)],
                                        id32[:, :]),
                           waits=(w if ci == 0 else []), sig=("sT" if ci == 15 else None))
            trev_last = sc.op("vector",
                              CP(h1T[:, :, ts(t_i, TT)],
                                 psTr.ap().rearrange("p (c n) -> p c n", n=128)),
                              waits=[("sT", tk)], sig="sV")
            h1ev[p] = trev_last
    sc.barrier()

    sc.mark("ph1")
    with ExitStack() as ph:
        _sb = lambda n, s, d: ph.enter_context(nc.sbuf_tensor(n, s, d))
        _ps = lambda n, s, d=F32: ph.enter_context(nc.psum_tensor(n, s, d))
        cos_sb = _sb("cos_sb", [128, NT], BF16)
        sinm_sb = _sb("sinm_sb", [128, NT], BF16)
        wbuf = [_sb(f"w1_{i}", [128, 16, 128], BF16) for i in range(2)]
        qraw = [_sb(f"qraw{i}", [128, NT], BF16) for i in range(2)]
        qshf = [_sb(f"qshf{i}", [128, NT], BF16) for i in range(2)]
        qtmp = [_sb(f"qtmp{i}", [128, NT], BF16) for i in range(2)]
        qout = [_sb(f"qout{i}", [128, NT], BF16) for i in range(2)]
        vwbuf = [_sb(f"vw{i}", [128, 16, 512], BF16) for i in range(2)]
        vstg = [_sb(f"vstg{i}", [128, D], BF16) for i in range(2)]
        ps1 = [_ps(f"ps1_{i}", [128, NT], F32) for i in range(2)]

        sc.op("gpsimd", DMA(cos_sb[:, :], cosTd[:, :]), sig="sWD", inc=16)
        rope_tok = sc.op("gpsimd", DMA(sinm_sb[:, :], sinmTd[:, :]), sig="sWD", inc=16)

        wtok = [None, None]
        mm_done = [None, None]
        ev_first = [None, None]
        ev_done = [None, None]
        st_done = [None, None]
        for c in range(32):
            p = c % 2
            w = [("sT", mm_done[p])] if mm_done[p] else []
            wtok[p] = sc.op("gpsimd", DMA(wbuf[p][:, :, :], wqkvT.ap()[c]),
                            waits=w, sig="sWD", inc=16)
            w = [("sWD", wtok[p])]
            if ev_first[p]:
                w.append(("sV", ev_first[p]))
            first = True
            for sp in range(4):
                for ci in range(16):
                    tk = sc.op("tensor", MM(ps1[p][:, ts(sp, 512)], wbuf[p][:, ci, :],
                                            h1T[:, ci, ts(sp, 512)], ci == 0, ci == 15),
                               waits=(w if first else []),
                               sig=("sT" if (sp == 3 and ci == 15) else None))
                    first = False
            mm_done[p] = tk
            w = [("sT", mm_done[p])]
            if st_done[p]:
                w.append(("sXD", st_done[p]))
            ev_first[p] = sc.op("vector", CP(qraw[p][:, :], ps1[p][:, :]), waits=w,
                                sig="sV")
            sc.op("vector", CP(qshf[p][0:32, :], qraw[p][32:64, :]), sig="sV")
            sc.op("vector", CP(qshf[p][32:64, :], qraw[p][0:32, :]), sig="sV")
            sc.op("vector", CP(qshf[p][64:96, :], qraw[p][96:128, :]), sig="sV")
            sc.op("vector", CP(qshf[p][96:128, :], qraw[p][64:96, :]), sig="sV")
            sc.op("vector", TT_(qtmp[p][:, :], qraw[p][:, :], cos_sb[:, :], ALU.mult),
                  waits=[("sWD", rope_tok)], sig="sV")
            sc.op("vector", TT_(qshf[p][:, :], qshf[p][:, :], sinm_sb[:, :], ALU.mult),
                  sig="sV")
            ev_done[p] = sc.op("vector", TT_(qout[p][:, :], qtmp[p][:, :],
                                             qshf[p][:, :], ALU.add), sig="sV")
            st_done[p] = sc.op("sync", DMA(qk16v[:, c, :], qout[p][:, :]),
                               waits=[("sV", ev_done[p])], sig="sXD", inc=16)
        qk_v_fence = sc.counts["sV"]
        vw_tok = [None, None]
        vmm = [None, None]
        vev = [None, None]
        vst = [None, None]
        for vs in range(4):
            pw = vs % 2
            w = [("sT", vmm[0]), ("sT", vmm[1])] if vs >= 2 else []
            vw_tok[pw] = sc.op("gpsimd", DMA(vwbuf[pw][:, :, :], wvT.ap()[vs]),
                               waits=w, sig="sWD", inc=16)
            for ti in range(NTT):
                pp = ti % 2
                w = [("sWD", vw_tok[pw]), ("sV", qk_v_fence)]
                if vev[pp]:
                    w.append(("sV", vev[pp]))
                first = True
                for ci in range(16):
                    tk = sc.op("tensor", MM(ps1[pp][:, 0:512], h1T[:, ci, ts(ti, 128)],
                                            vwbuf[pw][:, ci, :], ci == 0, ci == 15),
                               waits=(w if first else []),
                               sig=("sT" if ci == 15 else None))
                    first = False
                vmm[pp] = tk
                w = [("sT", vmm[pp])]
                if vst[pp]:
                    w.append(("sXD", vst[pp]))
                vev[pp] = sc.op("vector", CP(vstg[pp][:, ts(vs, 512)], ps1[pp][:, 0:512]),
                                waits=w, sig="sV")
                vst[pp] = sc.op("sync", DMA(v16v[:, ti, ts(vs, 512)],
                                            vstg[pp][:, ts(vs, 512)]),
                                waits=[("sV", vev[pp])], sig="sXD", inc=16)
    sc.barrier()

    h1T_cm.__exit__(None, None, None)

    sc.mark("ph2")
    with ExitStack() as ph:
        _sb = lambda n, s, d: ph.enter_context(nc.sbuf_tensor(n, s, d))
        _ps = lambda n, s, d=F32: ph.enter_context(nc.psum_tensor(n, s, d))
        Kb = _sb("Kb", [128, 16, BLK], BF16)
        Vb = _sb("Vb", [128, 8, D], BF16)
        ctxF = _sb("ctxF", [128, 16, BLK], BF16)
        Qc = [_sb(f"Qc{i}", [128, BLK], BF16) for i in range(2)]
        Ebuf = [_sb(f"Ebuf{i}", [128, BLK], BF16) for i in range(2)]
        Pbuf = [_sb(f"Pbuf{i}", [128, BLK], BF16) for i in range(2)]
        PTs = [_sb(f"PTs{i}", [128, BLK], BF16) for i in range(2)]
        sumc = _sb("sumc", [128, 4], F32)
        psS = [_ps(f"psS{i}", [128, BLK], F32) for i in range(2)]
        psPT = _ps("psPT", [128, 2 * BLK], BF16)
        psCtx = _ps("psCtx", [128, BLK], F32)
        SKEW = 2
        NU = 256

        ctx_st = None
        for b in range(NBLK):
            w = []
            if b > 0:
                w = [("sT", sc.counts["sT"]), ("sXD", ctx_st)]
            kld = sc.op("sync", DMA(Kb[:, :, :], qk16v[:, 16:32, ts(b, BLK)]),
                        waits=w, sig="sXD", inc=16)
            vld = sc.op("sync", DMA(Vb[:, :, :], v16v[:, ts(b, 8), :]),
                        sig="sXD", inc=16)
            q_ld = [None, None]
            q_last_mm = [None, None]
            sc_tok = [None] * NU
            exp_tok = [None] * NU
            av_tok = [None] * NU
            ctx_ev_tok = [None] * NU
            pscale_by_par = [None, None]
            pt_ev_last = None
            first_ev_of_block = True

            def unit(u):
                ci, rem = divmod(u, 16)
                hh, j = divmod(rem, 8)
                return ci, hh, j

            def emit_ctx_ev(v):
                nonlocal first_ev_of_block
                ci, hh, j = unit(v)
                r = hh * 64
                w = [("sT", av_tok[v])]
                if first_ev_of_block and ctx_st:
                    w.append(("sXD", ctx_st))
                first_ev_of_block = False
                ctx_ev_tok[v] = sc.op("scalar",
                                      SCP(ctxF[r:r + 64, ci, ts(j, 128)],
                                          psCtx[0:64, (v % 2) * 512:(v % 2) * 512 + 128]),
                                      waits=w, sig="sS")

            mk_tok = [None] * NU
            scale_tok = [None] * NU
            pt_tok = [None] * NU
            ptev_tok = [None] * NU
            for u in range(NU + 7):
                v = u - 2
                if 0 <= v < NU:
                    _, _, j = unit(v)
                    span = (j + 1) * 128
                    pv = v % 2
                    we = [("sV", mk_tok[v])]
                    if v >= 2 and scale_tok[v - 2]:
                        we.append(("sV", scale_tok[v - 2]))
                    exp_tok[v] = sc.op("scalar",
                                       ACTV(Ebuf[pv][:, 0:span], psS[pv][:, 0:span],
                                            AF.Exp, scale=0.125,
                                            acc=sumc[:, pv:pv + 1]),
                                       waits=we, sig="sS")
                v = u - 3
                if 0 <= v < NU:
                    _, _, j = unit(v)
                    span = (j + 1) * 128
                    pv = v % 2
                    sc.op("vector", RECIP(sumc[:, 2 + pv:3 + pv], sumc[:, pv:pv + 1]),
                          waits=[("sS", exp_tok[v])], sig="sV")
                    w2 = []
                    if v >= 2 and pt_tok[v - 2]:
                        w2.append(("sT", pt_tok[v - 2]))
                    scale_tok[v] = sc.op("vector", TSC(Pbuf[pv][:, 0:span],
                                                       Ebuf[pv][:, 0:span],
                                                       sumc[:, 2 + pv:3 + pv], None,
                                                       ALU.mult), waits=w2,
                                         sig="sV", self_edge=True)
                v = u - 6
                if 0 <= v < NU:
                    ci, hh, j = unit(v)
                    h = ci * 2 + hh
                    pv = v % 2
                    w = [("sV", ptev_tok[v]), ("sXD", vld)]
                    if v >= 2 and ctx_ev_tok[v - 2]:
                        w.append(("sS", ctx_ev_tok[v - 2]))
                    cbase = pv * 512
                    for i in range(j + 1):
                        tk = sc.op("tensor", MM(psCtx[0:64, cbase:cbase + 128],
                                                Vb[:, i, ts(h, 64)],
                                                PTs[pv][:, ts(i, 128)], i == 0, i == j),
                                   waits=(w if i == 0 else []),
                                   sig=("sT" if i == j else None))
                    av_tok[v] = tk
                    emit_ctx_ev(v)
                v = u - 5
                if 0 <= v < NU:
                    _, _, j = unit(v)
                    span = (j + 1) * 128
                    pv = v % 2
                    w = [("sT", pt_tok[v])]
                    if v >= 2 and av_tok[v - 2]:
                        w.append(("sT", av_tok[v - 2]))
                    ptev_tok[v] = sc.op("vector",
                                        CP(PTs[pv][:, 0:span],
                                           psPT[:, pv * 1024: pv * 1024 + span]),
                                        waits=w, sig="sV")
                v = u - 4
                if 0 <= v < NU:
                    _, _, j = unit(v)
                    pv = v % 2
                    w = [("sV", scale_tok[v])]
                    if v >= 2 and ptev_tok[v - 2]:
                        w.append(("sV", ptev_tok[v - 2]))
                    for i in range(j + 1):
                        tk = sc.op("tensor",
                                   TR(psPT[:, pv * 1024 + i * 128: pv * 1024 + i * 128 + 128],
                                      Pbuf[pv][:, ts(i, 128)], id16[:, :]),
                                   waits=(w if i == 0 else []),
                                   sig=("sT" if i == j else None))
                    pt_tok[v] = tk
                v = u - 1
                if 0 <= v < NU:
                    _, _, j = unit(v)
                    pv = v % 2
                    mk_tok[v] = sc.op("vector", TT_(psS[pv][:, ts(j, 128)],
                                                    psS[pv][:, ts(j, 128)],
                                                    msk[:, :], ALU.add),
                                      waits=[("sT", sc_tok[v]), ("sXD", mask_tok)],
                                      sig="sV")
                if u < NU:
                    ci, hh, j = unit(u)
                    pq = ci % 2
                    if hh == 0 and j == 0:
                        w = [("sXD", kld)]
                        if q_last_mm[pq]:
                            w.append(("sT", q_last_mm[pq]))
                        q_ld[pq] = sc.op("sync",
                                         DMA(Qc[pq][:, :], qk16v[:, ci, ts(b, BLK)]),
                                         waits=w, sig="sXD", inc=16)
                    r = hh * 64
                    span = (j + 1) * 128
                    slot = u % 2
                    w = [("sXD", q_ld[pq])]
                    if u >= 2 and exp_tok[u - 2]:
                        w.append(("sS", exp_tok[u - 2]))
                    first = True
                    nsp = (span + 511) // 512
                    for s_i in range(nsp):
                        lo, hi = s_i * 512, min(span, (s_i + 1) * 512)
                        tk = sc.op("tensor",
                                   MM(psS[slot][:, lo:hi], Qc[pq][r:r + 64, ts(j, 128)],
                                      Kb[r:r + 64, ci, lo:hi], True, True),
                                   waits=(w if first else []),
                                   sig=("sT" if s_i == nsp - 1 else None))
                        first = False
                    sc_tok[u] = tk
                    q_last_mm[pq] = tk
            ctx_st = sc.op("sync", DMA(ctx16v[:, :, ts(b, BLK)], ctxF[:, :, :]),
                           waits=[("sS", ctx_ev_tok[NU - 1])], sig="sXD", inc=16)
    sc.barrier()

    sc.mark("ph3")
    with ExitStack() as ph:
        _sb = lambda n, s, d: ph.enter_context(nc.sbuf_tensor(n, s, d))
        _ps = lambda n, s, d=F32: ph.enter_context(nc.psum_tensor(n, s, d))
        woTb = _sb("woTb", [128, 16, D], BF16)
        gamf = _sb("gamf", [128, D], F32)
        ctxh = _sb("ctxh", [128, 16, BLK], BF16)
        xnld = [_sb(f"xnld{i}", [128, D], F32) for i in range(2)]
        ytile = [_sb(f"ytile{i}", [128, D], F32) for i in range(2)]
        h2t = [_sb(f"h2t{i}", [128, D], BF16) for i in range(2)]
        h2stg = [_sb(f"h2stg{i}", [128, 16, 128], BF16) for i in range(2)]
        ysq = [_sb(f"ysq{i}", [128, D], F32) for i in range(2)]
        st3 = _sb("st3", [128, 8], F32)
        ps3 = [_ps(f"ps3_{i}", [128, 512], F32) for i in range(6)]
        psT3 = _ps("psT3", [128, NT], BF16)

        wo_ld = sc.op("gpsimd", DMA(woTb[:, :, :], woTv[:, :, :]), sig="sWD", inc=16)
        gf_ld = sc.op("sync", DMA(gamf[:, :], gam_f[:, :]), sig="sXD", inc=16)

        ev3 = [None] * 6
        ylast = [None, None]
        ydma = [None, None]
        h2dma = [None, None]
        h2ev = [None, None]
        sqts = [None, None]
        h2toks = [None] * NTT
        ctx_ld = None

        def emit_h2tr(t_j):
            pp = t_j % 2
            base = pp * 1024
            w = [("sV", h2toks[t_j])]
            if h2ev[pp]:
                w.append(("sV", h2ev[pp]))
            for ci in range(8):
                tk = sc.op("tensor", TR(psT3[:, base + ci * 128: base + ci * 128 + 128],
                                        h2t[pp][:, ts(ci, 128)], id16[:, :]),
                           waits=(w if ci == 0 else []),
                           sig=("sT" if ci == 7 else None))
            ev_a = sc.op("vector",
                         CP(h2stg[pp][:, 0:8, :],
                            psT3.ap().rearrange("p (c n) -> p c n", n=128)[:, 8 * pp: 8 * pp + 8, :]),
                         waits=[("sT", tk)], sig="sV")
            for ci in range(8):
                tk = sc.op("tensor", TR(psT3[:, base + ci * 128: base + ci * 128 + 128],
                                        h2t[pp][:, ts(8 + ci, 128)], id16[:, :]),
                           waits=([("sV", ev_a)] if ci == 0 else []),
                           sig=("sT" if ci == 7 else None))
            h2ev[pp] = sc.op("vector",
                             CP(h2stg[pp][:, 8:16, :],
                                psT3.ap().rearrange("p (c n) -> p c n", n=128)[:, 8 * pp: 8 * pp + 8, :]),
                             waits=[("sT", tk)], sig="sV")
            h2dma[pp] = sc.op("sync", DMA(h2T16v[:, :, ts(t_j, TT)],
                                          h2stg[pp][:, :, :]),
                              waits=[("sV", h2ev[pp])], sig="sXD", inc=16)

        def emit_stats(t_j):
            pp = t_j % 2
            b0 = 4 * pp
            SQC, VV, SQR, RSD = b0, b0 + 1, b0 + 2, b0 + 3
            col = lambda c: st3[:, c:c + 1]
            sqt = sc.op("scalar", ACTV(ysq[pp][:, :], ytile[pp][:, :], AF.Square,
                                       acc=col(SQC)), waits=[("sV", ytok3[t_j])],
                        sig="sS")
            sqts[pp] = sqt
            vv_tok = sc.op("vector", TSC(col(VV), col(SQC), 1.0 / D, 1e-5, ALU.mult,
                                         ALU.add), waits=[("sS", sqt)], sig="sV")
            sq2 = sc.op("scalar", ACTV(col(SQR), col(VV), AF.Sqrt),
                        waits=[("sV", vv_tok)], sig="sS")
            rst = sc.op("vector", RECIP(col(RSD), col(SQR)), waits=[("sS", sq2)],
                        sig="sV")
            w = [("sV", rst), ("sXD", gf_ld)]
            if h2dma[pp]:
                w.append(("sXD", h2dma[pp]))
            h2tok = sc.op("vector", STT(h2t[pp][:, :], ytile[pp][:, :], col(RSD),
                                        gamf[:, :], ALU.mult, ALU.mult),
                          waits=w, sig="sV", self_edge=True)
            ylast[pp] = h2tok
            h2toks[t_j] = h2tok

        ytok3 = [None] * NTT
        for half in range(2):
            w = [("sT", sc.counts["sT"])] if half > 0 else []
            ctx_ld = sc.op("sync", DMA(ctxh[:, :, :], ctx16v[:, :, ts(half, BLK)]),
                           waits=w, sig="sXD", inc=16)
            for tt in range(8):
                t_i = half * 8 + tt
                pp = t_i % 2
                w0 = [("sXD", ctx_ld), ("sWD", wo_ld)]
                slots = [(4 * t_i + k) % 6 for k in range(4)]
                mmtok = [None] * 4
                for os_ in range(4):
                    slot = slots[os_]
                    for ci in range(16):
                        w = []
                        if ci == 0:
                            if os_ == 0:
                                w += w0
                            if ev3[slot]:
                                w.append(("sV", ev3[slot]))
                        tk = sc.op("tensor", MM(ps3[slot][:, :],
                                                ctxh[:, ci, ts(tt, 128)],
                                                woTb[:, ci, ts(os_, 512)],
                                                ci == 0, ci == 15),
                                   waits=w,
                                   sig=("sT" if ci == 15 else None))
                    mmtok[os_] = tk
                if t_i >= 2:
                    emit_h2tr(t_i - 2)
                w = [("sV", ylast[pp])] if ylast[pp] else []
                xn_ld = sc.op("sync", DMA(xnld[pp][:, :], xn32[ts(t_i, TT), :]),
                              waits=w, sig="sXD", inc=16)
                for os_ in range(4):
                    off = os_ * 512
                    w = [("sT", mmtok[os_]), ("sXD", xn_ld)]
                    if os_ == 0:
                        if ydma[pp]:
                            w.append(("sXD", ydma[pp]))
                        if sqts[pp]:
                            w.append(("sS", sqts[pp]))
                    etok = sc.op("vector", TT_(ytile[pp][:, off:off + 512],
                                               ps3[slots[os_]][:, :],
                                               xnld[pp][:, off:off + 512], ALU.add),
                                 waits=w, sig="sV")
                    ev3[slots[os_]] = etok
                ytok3[t_i] = etok
                ydma[pp] = sc.op("sync", DMA(y32[ts(t_i, TT), :], ytile[pp][:, :]),
                                 waits=[("sV", etok)], sig="sXD", inc=16)
                if t_i >= 1:
                    emit_stats(t_i - 1)
        emit_stats(NTT - 1)
        emit_h2tr(NTT - 2)
        emit_h2tr(NTT - 1)
    sc.barrier()

    sc.mark("ph4")
    with ExitStack() as ph:
        _sb = lambda n, s, d: ph.enter_context(nc.sbuf_tensor(n, s, d))
        h2Tg = _sb("h2Tg", [128, 16, BLK], BF16)
        actF = _sb("actF", [128, 32, BLK], BF16)
        wup_b = [_sb(f"wup_{i}", [128, 16, 128], BF16) for i in range(4)]
        sgb = [_sb(f"sgb_{i}", [128, BLK], F32) for i in range(2)]
        wdn_b = [_sb(f"wdn_{i}", [128, 32, 128], BF16) for i in range(2)]
        dF = [_sb(f"dF_{i}", [128, BLK], BF16) for i in range(2)]
        down2T = _sb("down2T", [128, 8, D], BF16)
        yb = [_sb(f"yb_{i}", [128, D], F32) for i in range(2)]
        d2f = _sb("d2f", [128, D], F32)
        ab = _sb("ab", [128, D], F32)

        wtu = [None, None]
        mmup = [None, None]
        evup = [None, None]
        wtd = [None, None]
        mmd = [None, None]
        evd = [None, None]
        trd_tok = [None, None]
        d2ev_last = None
        adma = None
        atok = [None, None]
        up_mm_last = None
        down_mm_last = None
        down_psum_free = []
        h2_ld_next = sc.op("sync", DMA(h2Tg[:, :, :], h2T16v[:, :, ts(0, BLK)]),
                           sig="sXD", inc=16)
        for g in range(2):
            h2_ld = h2_ld_next
            with ExitStack() as php:
                _ps2 = lambda n, s, d=F32: php.enter_context(nc.psum_tensor(n, s, d))
                ps4 = [_ps2(f"ps4_{g}_{i}", [128, BLK], F32) for i in range(4)]
                for c in range(32):
                    p = c % 2
                    pg_, pv_ = 2 * p, 2 * p + 1
                    w = [("sT", mmup[p])] if mmup[p] else []
                    sc.op("gpsimd", DMA(wup_b[pg_][:, :, :], wupT.ap()[c]),
                          waits=w, sig="sWD", inc=16)
                    wtu[p] = sc.op("gpsimd", DMA(wup_b[pv_][:, :, :],
                                                 wupT.ap()[c + 32]),
                                   sig="sWD", inc=16)
                    w = [("sWD", wtu[p]), ("sXD", h2_ld)]
                    if evup[p]:
                        w.append(("sV", evup[p]))
                    if c < 2:
                        for t_ in down_psum_free:
                            w.append(t_)
                    first = True
                    for which in range(2):
                        slot = 2 * p + which
                        for sp in range(2):
                            for ci in range(16):
                                tk = sc.op("tensor",
                                           MM(ps4[slot][:, ts(sp, 512)],
                                              wup_b[slot][:, ci, :],
                                              h2Tg[:, ci, ts(sp, 512)], ci == 0, ci == 15),
                                           waits=(w if first else []),
                                           sig=("sT" if (which == 1 and sp == 1 and ci == 15) else None))
                                first = False
                    mmup[p] = tk
                    w = [("sT", tk)]
                    if evup[p]:
                        w.append(("sV", evup[p]))
                    wact = list(w)
                    if c < 2 and down_mm_last:
                        wact.append(("sT", down_mm_last))
                    sg_tok = sc.op("scalar", ACTV(sgb[p][:, :], ps4[pg_][:, :], AF.Silu),
                                   waits=w, sig="sS")
                    wact.append(("sS", sg_tok))
                    evup[p] = sc.op("vector", TT_(actF[:, c, :], sgb[p][:, :],
                                                  ps4[pv_][:, :], ALU.mult),
                                    waits=wact, sig="sV")
                up_done = sc.counts["sV"]
                up_mm_last = mmup[0] if (mmup[0] or 0) > (mmup[1] or 0) else mmup[1]
            with ExitStack() as php:
                _ps2 = lambda n, s, d=F32: php.enter_context(nc.psum_tensor(n, s, d))
                ps4d = [_ps2(f"ps4d_{g}_{i}", [128, BLK], F32) for i in range(2)]
                psT4 = _ps2(f"psT4_{g}", [128, 2 * BLK], BF16)

                def emit_tr4(oc):
                    nonlocal d2ev_last
                    p = oc % 2
                    base = p * 1024
                    w = [("sV", evd[p])]
                    if d2ev_last:
                        w.append(("sV", d2ev_last))
                    for ti in range(8):
                        tk = sc.op("tensor",
                                   TR(psT4[:, base + ti * 128: base + ti * 128 + 128],
                                      dF[p][:, ts(ti, 128)], id16[:, :]),
                                   waits=(w if ti == 0 else []),
                                   sig=("sT" if ti == 7 else None))
                    trd_tok[p] = tk
                    d2ev_last = sc.op("vector",
                                      CP(down2T[:, :, ts(oc, 128)],
                                         psT4.ap().rearrange("p (c n) -> p c n", n=128)[:, 8 * p: 8 * p + 8, :]),
                                      waits=[("sT", tk)], sig="sV")

                for oc in range(16):
                    p = oc % 2
                    w = [("sV", up_done)]
                    if mmd[p]:
                        w.append(("sT", mmd[p]))
                    wtd[p] = sc.op("gpsimd", DMA(wdn_b[p][:, :, :], wdnT.ap()[oc]),
                                   waits=w, sig="sWD", inc=16)
                    w = [("sWD", wtd[p])]
                    if evd[p]:
                        w.append(("sV", evd[p]))
                    if oc < 2 and g == 0:
                        pass
                    first = True
                    for sp in range(2):
                        for ic in range(32):
                            tk = sc.op("tensor", MM(ps4d[p][:, ts(sp, 512)],
                                                    wdn_b[p][:, ic, :],
                                                    actF[:, ic, ts(sp, 512)],
                                                    ic == 0, ic == 31),
                                       waits=(w if first else []),
                                       sig=("sT" if (sp == 1 and ic == 31) else None))
                            first = False
                    mmd[p] = tk
                    if oc >= 1:
                        emit_tr4(oc - 1)
                    w = [("sT", tk)]
                    if trd_tok[p]:
                        w.append(("sT", trd_tok[p]))
                    evd[p] = sc.op("vector", CP(dF[p][:, :], ps4d[p][:, :]),
                                   waits=w, sig="sV")
                emit_tr4(15)
                down_mm_last = mmd[0] if (mmd[0] or 0) > (mmd[1] or 0) else mmd[1]
                down_psum_free = [("sV", d2ev_last), ("sV", evd[0]), ("sV", evd[1])]
                if g == 0:
                    h2_ld_next = sc.op("sync",
                                       DMA(h2Tg[:, :, :], h2T16v[:, :, ts(1, BLK)]),
                                       waits=[("sT", up_mm_last)], sig="sXD", inc=16)
                for ti in range(8):
                    t_i = g * 8 + ti
                    pp = ti % 2
                    w = [("sV", atok[pp])] if atok[pp] else []
                    y_ld = sc.op("scalar", DMA(yb[pp][:, :], y32[ts(t_i, TT), :]),
                                 waits=w, sig="sSD", inc=16)
                    w = [("sV", d2ev_last)]
                    if atok[1 - pp]:
                        w.append(("sV", atok[1 - pp]))
                    cvt = sc.op("vector", CP(d2f[:, :], down2T[:, ti, :]),
                                waits=w, sig="sV")
                    w = [("sSD", y_ld), ("sV", cvt)]
                    if adma:
                        w.append(("sXD", adma))
                    atok[pp] = sc.op("vector", TT_(ab[:, :], yb[pp][:, :],
                                                   d2f[:, :], ALU.add),
                                     waits=w, sig="sV")
                    adma = sc.op("sync", DMA(att32[ts(t_i, TT), :], ab[:, :]),
                                 waits=[("sV", atok[pp])], sig="sXD", inc=16)
    sc.barrier()

    sc.mark("ph5a")
    with ExitStack() as ph:
        _sb = lambda n, s, d: ph.enter_context(nc.sbuf_tensor(n, s, d))
        _ps = lambda n, s, d=F32: ph.enter_context(nc.psum_tensor(n, s, d))
        xnT2 = _sb("xnT2", [128, 16, NT], BF16)
        wgb = [_sb(f"wgb{i}", [128, 16, 512], BF16) for i in range(2)]
        wub = [_sb(f"wub{i}", [128, 16, 512], BF16) for i in range(2)]
        ps5 = [_ps(f"ps5_{i}", [128, 512], F32) for i in range(4)]
        psT5 = _ps("psT5", [128, 2 * BLK], BF16)
        sg5 = [_sb(f"sg5{i}", [128, 512], F32) for i in range(2)]
        hb = [_sb(f"hb{i}", [128, 512], BF16) for i in range(2)]
        qb = [_sb(f"qb{i}", [128, 512], BF16) for i in range(2)]
        cb = [_sb(f"cb{i}", [128, 6, 128], BF16) for i in range(2)]
        tb = [_sb(f"tb{i}", [128, 4, 128], BF16) for i in range(2)]
        kb = [_sb(f"kb{i}", [128, 512], BF16) for i in range(2)]
        hsb = [_sb(f"hsb{i}", [128, 512], BF16) for i in range(2)]
        stg5 = [_sb(f"stg5_{i}", [128, 4, 512], BF16) for i in range(2)]

        xnT2_ld = sc.op("sync", DMA(xnT2[:, :, :], xnT16v[:, :, :]), sig="sXD", inc=16)
        wt = [None, None]
        mm5 = [None, None]
        ev5 = [None] * (16 * NTT)
        sg_prev = [None, None]
        ht_prev = [None, None]
        tr_tok = [None, None]
        tr5ev = [None, None]
        stgdma = [None, None]
        TSKEW = 2

        def emit_tr5(v):
            s_v, t_v = divmod(v, NTT)
            pt = v % 2
            base = pt * 1024
            w = [("sV", ev5[v])]
            if tr5ev[pt]:
                w.append(("sV", tr5ev[pt]))
            for ii in range(4):
                tk = sc.op("tensor", TR(psT5[:, base + ii * 128: base + ii * 128 + 128],
                                        hsb[pt][:, ts(ii, 128)], id16[:, :]),
                           waits=(w if ii == 0 else []),
                           sig=("sT" if ii == 3 else None))
            tr_tok[pt] = tk
            tsub = t_v % 4
            pb = (t_v // 4) % 2
            w = [("sT", tk)]
            if tsub == 0 and stgdma[pb]:
                w.append(("sXD", stgdma[pb]))
            tr5ev[pt] = sc.op("vector",
                              CP(stg5[pb].ap().rearrange("p c (u n) -> p c u n", n=128)[:, :, tsub, :],
                                 psT5.ap().rearrange("p (c n) -> p c n", n=128)[:, 8 * pt: 8 * pt + 4, :]),
                              waits=w, sig="sV")
            if tsub == 3:
                stgdma[pb] = sc.op("sync",
                                   DMA(hsT16v[:, ts(s_v, 4), t_v // 4 * 512: t_v // 4 * 512 + 512],
                                       stg5[pb][:, :, :]),
                                   waits=[("sV", tr5ev[0]), ("sV", tr5ev[1])],
                                   sig="sXD", inc=16)

        for s in range(16):
            pw = s % 2
            w = [("sT", mm5[pw])] if wt[pw] else []
            sc.op("gpsimd", DMA(wgb[pw][:, :, :], wg.ap()[s]), waits=w,
                  sig="sWD", inc=16)
            wt[pw] = sc.op("gpsimd", DMA(wub[pw][:, :, :], wu.ap()[s]),
                           sig="sWD", inc=16)
            for t_i in range(NTT):
                u = s * NTT + t_i
                pt = u % 2
                gsl, usl = 2 * pt, 2 * pt + 1
                w = [("sWD", wt[pw]), ("sXD", xnT2_ld)]
                if sg_prev[pt]:
                    w.append(("sS", sg_prev[pt]))
                if ht_prev[pt]:
                    w.append(("sV", ht_prev[pt]))
                first = True
                for which in range(2):
                    slot = 2 * pt + which
                    wsrc = wgb[pw] if which == 0 else wub[pw]
                    for ci in range(16):
                        tk = sc.op("tensor", MM(ps5[slot][:, :],
                                                xnT2[:, ci, ts(t_i, 128)],
                                                wsrc[:, ci, :], ci == 0, ci == 15),
                                   waits=(w if first else []),
                                   sig=("sT" if (which == 1 and ci == 15) else None))
                        first = False
                mm5[pw] = tk
                if u >= TSKEW:
                    emit_tr5(u - TSKEW)
                w = [("sT", tk)]
                if ht_prev[pt]:
                    w.append(("sV", ht_prev[pt]))
                sg_prev[pt] = sc.op("scalar", ACTV(sg5[pt][:, :], ps5[gsl][:, :],
                                                   AF.Silu), waits=w, sig="sS")
                w = [("sS", sg_prev[pt])]
                if tr_tok[pt]:
                    w.append(("sT", tr_tok[pt]))
                htok = sc.op("vector", TT_(hb[pt][:, :], sg5[pt][:, :], ps5[usl][:, :],
                                           ALU.mult), waits=w, sig="sV")
                ht_prev[pt] = htok
                A = lambda buf: buf[:, 0:128]
                B = lambda buf: buf[:, 128:256]
                C = lambda buf: buf[:, 256:384]
                Dp = lambda buf: buf[:, 384:512]
                sc.op("vector", TT_(qb[pt][:, :], hb[pt][:, :], hb[pt][:, :], ALU.mult),
                      sig="sV")
                q_ = qb[pt]
                c_ = cb[pt]
                sc.op("vector", TT_(c_[:, 0, :], A(q_), B(q_), ALU.is_ge), sig="sV")
                sc.op("vector", TT_(c_[:, 1, :], A(q_), C(q_), ALU.is_ge), sig="sV")
                sc.op("vector", TT_(c_[:, 2, :], A(q_), Dp(q_), ALU.is_ge), sig="sV")
                sc.op("vector", TT_(c_[:, 3, :], B(q_), C(q_), ALU.is_ge), sig="sV")
                sc.op("vector", TT_(c_[:, 4, :], B(q_), Dp(q_), ALU.is_ge), sig="sV")
                sc.op("vector", TT_(c_[:, 5, :], C(q_), Dp(q_), ALU.is_ge), sig="sV")
                t_ = tb[pt]
                k_ = kb[pt]
                tA, tB, tC, tD = t_[:, 0, :], t_[:, 1, :], t_[:, 2, :], t_[:, 3, :]
                sc.op("vector", TT_(tA, c_[:, 0, :], c_[:, 1, :], ALU.add), sig="sV")
                sc.op("vector", TT_(tB, c_[:, 3, :], c_[:, 4, :], ALU.add), sig="sV")
                sc.op("vector", TT_(tC, c_[:, 1, :], c_[:, 3, :], ALU.add), sig="sV")
                sc.op("vector", TT_(tD, c_[:, 2, :], c_[:, 4, :], ALU.add), sig="sV")
                sc.op("vector", TT_(tA, tA, c_[:, 2, :], ALU.add), sig="sV")
                sc.op("vector", TT_(tB, tB, c_[:, 0, :], ALU.subtract), sig="sV")
                sc.op("vector", TT_(tC, tC, c_[:, 5, :], ALU.subtract), sig="sV")
                sc.op("vector", TT_(tD, tD, c_[:, 5, :], ALU.add), sig="sV")
                sc.op("vector", TSC(A(k_), tA, 1.5, None, ALU.is_ge), sig="sV")
                sc.op("vector", TSC(B(k_), tB, 0.5, None, ALU.is_ge), sig="sV")
                sc.op("vector", TSC(C(k_), tC, 0.5, None, ALU.is_le), sig="sV")
                sc.op("vector", TSC(Dp(k_), tD, 1.5, None, ALU.is_le), sig="sV")
                ev5[u] = sc.op("vector", TT_(hsb[pt][:, :], hb[pt][:, :], kb[pt][:, :],
                                             ALU.mult), sig="sV", self_edge=True)
        for v in range(16 * NTT - TSKEW, 16 * NTT):
            emit_tr5(v)
    sc.barrier()

    sc.mark("ph5b")
    with ExitStack() as ph:
        _sb = lambda n, s, d: ph.enter_context(nc.sbuf_tensor(n, s, d))
        _ps = lambda n, s, d=F32: ph.enter_context(nc.psum_tensor(n, s, d))
        hsTq = _sb("hsTq", [128, 64, 512], BF16)
        wdb = [_sb(f"wdb{i}", [128, 64, 128], BF16) for i in range(4)]
        dFb = [_sb(f"dFb{i}", [128, 512], BF16) for i in range(2)]
        ffnT = _sb("ffnT", [128, 4, D], BF16)
        xb = [_sb(f"xb5{i}", [128, D], F32) for i in range(2)]
        atb = [_sb(f"atb5{i}", [128, D], F32) for i in range(2)]
        ff32 = _sb("ff32", [128, D], F32)
        ob = _sb("ob5", [128, D], F32)
        ps5d = [_ps(f"ps5d_{i}", [128, 512], F32) for i in range(2)]
        psT5b = _ps("psT5b", [128, 2 * BLK], BF16)

        hs_ld = None
        odma = [None, None]
        wt = [None, None, None, None]
        mmd = [None, None, None, None]
        slot_oc = [None, None, None, None]
        evd = [None, None]
        tr_tok = [None, None]
        f_ev_last = None
        def preload_hs(qt):
            w = [("sT", t) for t in mmd if t]
            a = sc.op("sync", DMA(hsTq[:, 0:32, :], hsT16v[:, 0:32, ts(qt, 512)]),
                      waits=w, sig="sXD", inc=16)
            b = sc.op("sync", DMA(hsTq[:, 32:64, :], hsT16v[:, 32:64, ts(qt, 512)]),
                      sig="sXD", inc=16)
            return a, b

        hs_next = preload_hs(0)
        for qt in range(4):
            hs_ld1, hs_ld = hs_next

            def emit_tr5b(oc):
                nonlocal f_ev_last
                p = oc % 2
                base = p * 1024
                w = [("sV", evd[p])]
                if f_ev_last:
                    w.append(("sV", f_ev_last))
                for ii in range(4):
                    tk = sc.op("tensor",
                               TR(psT5b[:, base + ii * 128: base + ii * 128 + 128],
                                  dFb[p][:, ts(ii, 128)], id16[:, :]),
                               waits=(w if ii == 0 else []),
                               sig=("sT" if ii == 3 else None))
                tr_tok[p] = tk
                f_ev_last = sc.op("vector",
                                  CP(ffnT[:, :, ts(oc, 128)],
                                     psT5b.ap().rearrange("p (c n) -> p c n", n=128)[:, 8 * p: 8 * p + 4, :]),
                                  waits=[("sT", tk)], sig="sV")

            oc_order = [(12 * qt + k) % 16 for k in range(16)]
            for oc in oc_order:
                p = oc % 2
                p4 = oc % 4
                if slot_oc[p4] != oc:
                    w = []
                    if wt[p4]:
                        w.append(("sT", mmd[p4]))
                    wt[p4] = sc.op("gpsimd", DMA(wdb[p4][:, :, :], wd.ap()[oc]),
                                   waits=w, sig="sWD", inc=16)
                    slot_oc[p4] = oc
                w = [("sWD", wt[p4]), ("sXD", hs_ld1)]
                if evd[p]:
                    w.append(("sV", evd[p]))
                first = True
                for ic in range(64):
                    if ic == 32:
                        w2 = [("sXD", hs_ld)]
                    else:
                        w2 = []
                    tk = sc.op("tensor", MM(ps5d[p][:, :], wdb[p4][:, ic, :],
                                            hsTq[:, ic, :], ic == 0, ic == 63),
                               waits=(w if first else w2),
                               sig=("sT" if ic == 63 else None))
                    first = False
                mmd[p4] = tk
                ko = oc_order.index(oc)
                if ko >= 1:
                    emit_tr5b(oc_order[ko - 1])
                w = [("sT", tk)]
                if tr_tok[p]:
                    w.append(("sT", tr_tok[p]))
                evd[p] = sc.op("vector", CP(dFb[p][:, :], ps5d[p][:, :]),
                               waits=w, sig="sV")
            emit_tr5b(oc_order[15])
            if qt < 3:
                hs_next = preload_hs(qt + 1)
            for ti in range(4):
                t_i = qt * 4 + ti
                pp = ti % 2
                w = [("sXD", odma[pp])] if odma[pp] else []
                x_ld = sc.op("scalar", DMA(xb[pp][:, :], x[ts(t_i, TT), :]), waits=w,
                             sig="sSD", inc=16)
                at_ld = sc.op("sync", DMA(atb[pp][:, :], att32[ts(t_i, TT), :]),
                              waits=w, sig="sXD", inc=16)
                cvt = sc.op("vector", CP(ff32[:, :], ffnT[:, ti, :]),
                            waits=[("sV", f_ev_last)], sig="sV")
                wo_ = [("sSD", x_ld), ("sXD", at_ld), ("sV", cvt)]
                for t_ in odma:
                    if t_:
                        wo_.append(("sXD", t_))
                o1 = sc.op("vector", TT_(ob[:, :], xb[pp][:, :], atb[pp][:, :],
                                         ALU.add), waits=wo_, sig="sV")
                o2 = sc.op("vector", TT_(ob[:, :], ob[:, :], ff32[:, :],
                                         ALU.add), sig="sV")
                odma[pp] = sc.op("sync", DMA(out[ts(t_i, TT), :], ob[:, :]),
                                 waits=[("sV", o2)], sig="sXD", inc=16)

    sc.op("sync", (lambda e: e.nop()),
          waits=[("sXD", sc.counts["sXD"]), ("sSD", sc.counts["sSD"]),
                 ("sWD", sc.counts["sWD"])])

    sc.mark("tail")
    sc.emit()
    nc.finalize()
    nc._sched = sc
    nc._keepalive = stack
    return nc


def _host_inputs(inputs):
    f32 = np.float32
    xall = np.ascontiguousarray(np.asarray(inputs["L0_attn_x"], f32).reshape(NCORE * NT, D))
    wqkvT0 = np.asarray(inputs["L0_attn_w_qkv"], f32).T
    wqkvT = np.ascontiguousarray(
        wqkvT0.reshape(16, 128, 48, 128).transpose(2, 1, 0, 3))
    wvT = np.ascontiguousarray(
        wqkvT0[:, 4096:].reshape(16, 128, 4, 512).transpose(2, 1, 0, 3))
    woT = np.ascontiguousarray(np.asarray(inputs["L0_attn_w_o"], f32).T)
    wupT = np.ascontiguousarray(
        np.asarray(inputs["L0_attn_w_up"], f32).T
        .reshape(16, 128, 64, 128).transpose(2, 1, 0, 3))
    wdnT = np.ascontiguousarray(
        np.asarray(inputs["L0_attn_w_down"], f32).T
        .reshape(32, 128, 16, 128).transpose(2, 1, 0, 3))
    wg0 = np.asarray(inputs["L0_ffn_w_gate"], f32)
    wu0 = np.asarray(inputs["L0_ffn_w_up"], f32)
    wd0 = np.asarray(inputs["L0_ffn_w_down"], f32)
    gidx = np.arange(IF) // 4
    m = np.arange(IF) % 4
    new = (gidx // 128) * 512 + m * 128 + (gidx % 128)
    inv = np.empty(IF, np.int64)
    inv[new] = np.arange(IF)
    wg_ = np.ascontiguousarray(
        wg0[:, inv].reshape(16, 128, 16, 512).transpose(2, 1, 0, 3))
    wu_ = np.ascontiguousarray(
        wu0[:, inv].reshape(16, 128, 16, 512).transpose(2, 1, 0, 3))
    wd_ = np.ascontiguousarray(
        wd0[inv, :].reshape(64, 128, 16, 128).transpose(2, 1, 0, 3))
    gam_a = np.ascontiguousarray(np.tile(np.asarray(inputs["L0_attn_attn_norm_w"], f32)[None, :], (128, 1)))
    gam_f = np.ascontiguousarray(np.tile(np.asarray(inputs["L0_attn_ffn_norm_w"], f32)[None, :], (128, 1)))
    ident32 = np.eye(128, dtype=f32)
    ident16 = np.eye(128, dtype=ml_dtypes.bfloat16)
    qq, kk = np.meshgrid(np.arange(128), np.arange(128), indexing="ij")
    masktri = np.where(kk > qq, NEG, 0.0).astype(f32)
    invf = (1.0 / 10000.0 ** (np.arange(0, HD, 2, dtype=np.float64) / HD)).astype(f32)
    p = np.arange(128)
    sgn = np.where((p % 64) < 32, -1.0, 1.0).astype(f32)
    maps = []
    for c in range(NCORE):
        pos = ((c % 2) * NT + np.arange(NT)).astype(f32)
        ang = pos[None, :] * invf[p % 32][:, None]
        cosT = np.cos(ang).astype(f32)
        sinmT = (sgn[:, None] * np.sin(ang)).astype(f32)
        maps.append({
            "x": np.ascontiguousarray(xall[c * NT:(c + 1) * NT]),
            "wqkvT": wqkvT, "wvT": wvT, "woT": woT, "wupT": wupT, "wdnT": wdnT,
            "wg": wg_, "wu": wu_, "wd": wd_,
            "gam_a": gam_a, "gam_f": gam_f,
            "cosT": np.ascontiguousarray(cosT),
            "sinmT": np.ascontiguousarray(sinmT),
            "masktri": masktri, "ident32": ident32, "ident16": ident16,
        })
    return maps


def kernel(**inputs):
    maps = _host_inputs(inputs)
    nc = build()
    res = run_bass_kernel_spmd(nc, maps, core_ids=list(range(NCORE)))
    outs = [res.results[c]["out"] for c in range(NCORE)]
    return np.concatenate(outs, axis=0).reshape(4, 4096, D).astype(np.float32)


# version 11
# speedup vs baseline: 1.0440x; 1.0268x over previous
import numpy as np
import ml_dtypes
import concourse.bass as bass
import concourse.mybir as mybir
from concourse.bass_utils import run_bass_kernel_spmd
from contextlib import ExitStack

F32 = mybir.dt.float32
BF16 = mybir.dt.bfloat16
AF = mybir.ActivationFunctionType
ALU = mybir.AluOpType
AX = mybir.AxisListType

D = 2048
NT = 2048
TT = 128
NTT = 16
BLK = 1024
NBLK = 2
NH = 32
HD = 64
QKV = 3 * D
IA = 8192
IAH = 4096
IF = 8192
NCORE = 8
NEG = -1.0e30

DEBUG_TAPS = False


def ts(i, n):
    return slice(i * n, (i + 1) * n)


def DMA(out, in_):
    return lambda e: e.dma_start(out=out, in_=in_)


def MM(out, lhsT, rhs, start, stop):
    return lambda e: e.matmul(out, lhsT, rhs, start=start, stop=stop)


def TR(out, in_, idn):
    return lambda e: e.transpose(out, in_, idn)


def CP(out, in_):
    return lambda e: e.tensor_copy(out, in_)


def SCP(out, in_):
    return lambda e: e.copy(out, in_)


def TT_(out, a, b, op):
    return lambda e: e.tensor_tensor(out, a, b, op)


def TSC(out, a, s1, s2, op0, op1=None, acc=None):
    if op1 is None:
        return lambda e: e.tensor_scalar(out, a, s1, None, op0, accum_out=acc)
    return lambda e: e.tensor_scalar(out, a, s1, s2, op0, op1, accum_out=acc)


def STT(out, in0, s, in1, op0, op1):
    return lambda e: e.scalar_tensor_tensor(out, in0, s, in1, op0, op1)


def ACTV(out, in_, f, bias=0.0, scale=1.0, acc=None):
    return lambda e: e.activation(out, in_, f, bias=bias, scale=scale, accum_out=acc)


def RSUM(out, in_):
    return lambda e: e.reduce_sum(out, in_, axis=AX.X)


def RECIP(out, in_):
    return lambda e: e.reciprocal(out, in_)


class Sched:
    ENGS = ("sync", "scalar", "vector", "tensor", "gpsimd")

    def __init__(self, nc, stack):
        self.nc = nc
        self.ops = {e: [] for e in self.ENGS}
        self.sems = {}
        self.counts = {}
        self.guaranteed = {e: {} for e in self.ENGS}
        self.pending = None
        self.pending_applied = set()
        self.marks = []
        self.names = {e: [] for e in self.ENGS}

    def mark(self, label):
        self.marks.append((label, {e: len(self.ops[e]) for e in self.ENGS}))

    def sem(self, name, stack):
        self.sems[name] = stack.enter_context(self.nc.semaphore(name))
        self.counts[name] = 0
        return name

    def barrier(self):
        self.pending = {s: c for s, c in self.counts.items() if c > 0}
        self.pending_applied = set()

    def op(self, eng, emit, waits=(), sig=None, inc=1, self_edge=False):
        tok = None
        if self_edge and sig is not None and self.counts[sig] > 0:
            waits = list(waits) + [(sig, self.counts[sig])]
        if sig is not None:
            self.counts[sig] += inc
            tok = self.counts[sig]
        ws = []
        if self.pending is not None and eng not in self.pending_applied:
            self.pending_applied.add(eng)
            for s, v in self.pending.items():
                ws.append((s, v))
        for s, v in waits:
            if v is None:
                continue
            ws.append((s, v))
        g = self.guaranteed[eng]
        ws2 = []
        for s, v in ws:
            if g.get(s, -1) >= v:
                continue
            g[s] = v
            ws2.append((s, v))
        self.ops[eng].append((emit, ws2, sig, inc))
        return tok

    def emit(self):
        nc = self.nc
        with nc.Block(no_gpsimd_drain=True) as block:
            for eng in self.ENGS:
                deco = getattr(block, eng)
                lst = self.ops[eng]
                sems = self.sems

                def make(lst, eng):
                    def body(handle):
                        nm = self.names[eng]
                        for emit, ws, sig, inc in lst:
                            for s, v in ws:
                                handle.wait_ge(sems[s], v)
                            inst = emit(handle)
                            if sig is not None:
                                inst.then_inc(sems[sig], inc)
                            try:
                                nm.append(inst.ins.name)
                            except Exception:
                                nm.append(None)
                    return body

                deco(make(lst, eng))


def build():
    nc = bass.Bass("TRN2", target_bir_lowering=False)
    dbg = "ExternalOutput" if DEBUG_TAPS else "Internal"

    x = nc.dram_tensor("x", [NT, D], F32, kind="ExternalInput")
    wqkvT = nc.dram_tensor("wqkvT", [48, 128, 16, 128], F32, kind="ExternalInput")
    wvT = nc.dram_tensor("wvT", [4, 128, 16, 512], F32, kind="ExternalInput")
    woT = nc.dram_tensor("woT", [D, D], F32, kind="ExternalInput")
    wupT = nc.dram_tensor("wupT", [64, 128, 16, 128], F32, kind="ExternalInput")
    wdnT = nc.dram_tensor("wdnT", [16, 128, 32, 128], F32, kind="ExternalInput")
    wg = nc.dram_tensor("wg", [16, 128, 16, 512], F32, kind="ExternalInput")
    wu = nc.dram_tensor("wu", [16, 128, 16, 512], F32, kind="ExternalInput")
    wd = nc.dram_tensor("wd", [16, 128, 64, 128], F32, kind="ExternalInput")
    gam_a = nc.dram_tensor("gam_a", [128, D], F32, kind="ExternalInput")
    gam_f = nc.dram_tensor("gam_f", [128, D], F32, kind="ExternalInput")
    cosTd = nc.dram_tensor("cosT", [128, NT], F32, kind="ExternalInput")
    sinmTd = nc.dram_tensor("sinmT", [128, NT], F32, kind="ExternalInput")
    masktri = nc.dram_tensor("masktri", [128, 128], F32, kind="ExternalInput")
    ident32d = nc.dram_tensor("ident32", [128, 128], F32, kind="ExternalInput")
    ident16d = nc.dram_tensor("ident16", [128, 128], BF16, kind="ExternalInput")

    out = nc.dram_tensor("out", [NT, D], F32, kind="ExternalOutput")

    xn32 = nc.dram_tensor("xn32", [NT, D], F32, kind=dbg)
    y32 = nc.dram_tensor("y32", [NT, D], F32, kind=dbg)
    att32 = nc.dram_tensor("att32", [NT, D], F32, kind=dbg)
    xnT16 = nc.dram_tensor("xnT16", [16, 128, NT], BF16, kind=dbg)
    qk16 = nc.dram_tensor("qk16", [32, 128, NT], BF16, kind=dbg)
    v16 = nc.dram_tensor("v16", [16, 128, D], BF16, kind=dbg)
    ctx16 = nc.dram_tensor("ctx16", [16, 128, NT], BF16, kind=dbg)
    h2T16 = nc.dram_tensor("h2T16", [16, 128, NT], BF16, kind=dbg)
    hsT16 = nc.dram_tensor("hsT16", [64, 128, NT], BF16, kind=dbg)

    xnT16v = xnT16.ap().rearrange("c p n -> p c n")
    qk16v = qk16.ap().rearrange("c p n -> p c n")
    v16v = v16.ap().rearrange("c p n -> p c n")
    ctx16v = ctx16.ap().rearrange("c p n -> p c n")
    h2T16v = h2T16.ap().rearrange("c p n -> p c n")
    hsT16v = hsT16.ap().rearrange("c p n -> p c n")
    woTv = woT.ap().rearrange("(c p) n -> p c n", p=128)

    stack = ExitStack()
    sc = Sched(nc, stack)
    for s in ("sWD", "sXD", "sSD", "sT", "sV", "sS"):
        sc.sem(s, stack)

    def sb(name, shape, dtype):
        return stack.enter_context(nc.sbuf_tensor(name, shape, dtype))

    id32 = sb("id32", [128, 128], F32)
    id16 = sb("id16", [128, 128], BF16)
    msk = sb("msk", [128, 128], F32)
    sc.op("sync", DMA(id32[:, :], ident32d[:, :]), sig="sXD", inc=16)
    sc.op("sync", DMA(id16[:, :], ident16d[:, :]), sig="sXD", inc=16)
    mask_tok = sc.op("sync", DMA(msk[:, :], masktri[:, :]), sig="sXD", inc=16)

    ph01 = ExitStack()
    h1T = ph01.enter_context(nc.sbuf_tensor("h1T", [128, 16, NT], BF16))
    cos_sb = ph01.enter_context(nc.sbuf_tensor("cos_sb", [128, NT], BF16))
    sinm_sb = ph01.enter_context(nc.sbuf_tensor("sinm_sb", [128, NT], BF16))
    wbuf = [ph01.enter_context(nc.sbuf_tensor(f"w1_{i}", [128, 16, 128], BF16))
            for i in range(2)]
    vwbuf = [ph01.enter_context(nc.sbuf_tensor(f"vw{i}", [128, 16, 512], BF16))
             for i in range(2)]
    sc.op("gpsimd", DMA(cos_sb[:, :], cosTd[:, :]), sig="sWD", inc=16)
    rope_tok = sc.op("gpsimd", DMA(sinm_sb[:, :], sinmTd[:, :]), sig="sWD", inc=16)
    wtok = [None, None]
    wtok[0] = sc.op("gpsimd", DMA(wbuf[0][:, :, :], wqkvT.ap()[0]), sig="sWD", inc=16)
    wtok[1] = sc.op("gpsimd", DMA(wbuf[1][:, :, :], wqkvT.ap()[1]), sig="sWD", inc=16)

    sc.mark("ph0")
    with ExitStack() as ph:
        _sb = lambda n, s, d: ph.enter_context(nc.sbuf_tensor(n, s, d))
        _ps = lambda n, s, d=F32: ph.enter_context(nc.psum_tensor(n, s, d))
        xt = [_sb(f"xt{i}", [128, D], F32) for i in range(2)]
        xnsb = [_sb(f"xnsb{i}", [128, D], F32) for i in range(2)]
        h1sb = [_sb(f"h1sb{i}", [128, D], F32) for i in range(2)]
        sqscr = [_sb(f"sqscr{i}", [128, D], F32) for i in range(2)]
        stgx = [_sb(f"stgx{i}", [128, 16, 128], BF16) for i in range(2)]
        st0 = _sb("st0", [128, 32], F32)
        gama = _sb("gama", [128, D], F32)
        psTr = _ps("psTr", [128, D], F32)

        ga_tok = sc.op("sync", DMA(gama[:, :], gam_a[:, :]), sig="sXD", inc=16)
        ld = [None, None]
        xn_done = [None, None]
        h1_done = [None, None]
        dx = [None, None]
        dxT = [None, None]
        h1ev = [None, None]
        trev_last = None
        for t_i in range(NTT):
            p = t_i % 2
            b0 = 16 * p
            (S1, M, MSQ, VAR, VA, SQT, RSTD, S2C, BB, S2S, S2R) = (
                b0, b0 + 1, b0 + 2, b0 + 3, b0 + 4, b0 + 5, b0 + 6, b0 + 7,
                b0 + 8, b0 + 9, b0 + 10)
            col = lambda c: st0[:, c:c + 1]
            w = [("sV", xn_done[p])] if xn_done[p] else []
            ld[p] = sc.op("sync", DMA(xt[p][:, :], x[ts(t_i, TT), :]), waits=w,
                          sig="sXD", inc=16)
            w = [("sXD", ld[p])]
            if h1_done[p]:
                w.append(("sV", h1_done[p]))
            s2tok = sc.op("scalar", ACTV(sqscr[p][:, :], xt[p][:, :], AF.Square,
                                         acc=col(S2C)), waits=w, sig="sS")
            sc.op("vector", RSUM(col(S1), xt[p][:, :]), waits=[("sXD", ld[p])], sig="sV")
            sc.op("vector", TSC(col(M), col(S1), 1.0 / D, None, ALU.mult), sig="sV",
                  self_edge=True)
            sc.op("vector", TT_(col(MSQ), col(M), col(M), ALU.mult), sig="sV",
                  self_edge=True)
            sc.op("vector", TSC(col(VAR), col(S2C), 1.0 / D, col(MSQ), ALU.mult,
                                ALU.subtract), waits=[("sS", s2tok)], sig="sV",
                  self_edge=True)
            va_tok = sc.op("vector", TSC(col(VA), col(VAR), 1e-5, None, ALU.add),
                           sig="sV", self_edge=True)
            sqtok = sc.op("scalar", ACTV(col(SQT), col(VA), AF.Sqrt),
                          waits=[("sV", va_tok)], sig="sS")
            sc.op("vector", RECIP(col(RSTD), col(SQT)), waits=[("sS", sqtok)], sig="sV")
            w = []
            if dxT[p]:
                w.append(("sXD", dxT[p]))
            xn_done[p] = sc.op("vector", TSC(xnsb[p][:, :], xt[p][:, :], col(M),
                                             col(RSTD), ALU.subtract, ALU.mult),
                               waits=w, sig="sV", self_edge=True)
            dx[p] = sc.op("sync", DMA(xn32[ts(t_i, TT), :], xnsb[p][:, :]),
                          waits=[("sV", xn_done[p])], sig="sXD", inc=16)
            sc.op("vector", RECIP(col(BB), col(VA)), sig="sV")
            sc.op("vector", TT_(col(BB), col(VAR), col(BB), ALU.mult), sig="sV",
                  self_edge=True)
            bb_tok = sc.op("vector", TSC(col(BB), col(BB), 1e-5, None, ALU.add),
                           sig="sV", self_edge=True)
            sq2 = sc.op("scalar", ACTV(col(S2S), col(BB), AF.Sqrt),
                        waits=[("sV", bb_tok)], sig="sS")
            sc.op("vector", RECIP(col(S2R), col(S2S)), waits=[("sS", sq2)], sig="sV")
            w = [("sXD", ga_tok)]
            if h1ev[p]:
                w.append(("sV", h1ev[p]))
            h1_done[p] = sc.op("vector", STT(h1sb[p][:, :], xnsb[p][:, :], col(S2R),
                                             gama[:, :], ALU.mult, ALU.mult),
                               waits=w, sig="sV", self_edge=True)
            w = [("sV", xn_done[p])]
            if trev_last:
                w.append(("sV", trev_last))
            for ci in range(16):
                tk = sc.op("tensor", TR(psTr[:, ts(ci, 128)], xnsb[p][:, ts(ci, 128)],
                                        id32[:, :]),
                           waits=(w if ci == 0 else []), sig=("sT" if ci == 15 else None))
            wv = [("sT", tk)]
            if dxT[p]:
                wv.append(("sXD", dxT[p]))
            trev = sc.op("vector", CP(stgx[p][:, :, :],
                                      psTr.ap().rearrange("p (c n) -> p c n", n=128)),
                         waits=wv, sig="sV")
            dxT[p] = sc.op("sync", DMA(xnT16v[:, :, ts(t_i, TT)], stgx[p][:, :, :]),
                           waits=[("sV", trev)], sig="sXD", inc=16)
            w = [("sV", h1_done[p]), ("sV", trev)]
            for ci in range(16):
                tk = sc.op("tensor", TR(psTr[:, ts(ci, 128)], h1sb[p][:, ts(ci, 128)],
                                        id32[:, :]),
                           waits=(w if ci == 0 else []), sig=("sT" if ci == 15 else None))
            trev_last = sc.op("vector",
                              CP(h1T[:, :, ts(t_i, TT)],
                                 psTr.ap().rearrange("p (c n) -> p c n", n=128)),
                              waits=[("sT", tk)], sig="sV")
            h1ev[p] = trev_last
        ph0_h1_done = trev_last

    sc.mark("ph1")
    with ExitStack() as ph:
        _sb = lambda n, s, d: ph.enter_context(nc.sbuf_tensor(n, s, d))
        _ps = lambda n, s, d=F32: ph.enter_context(nc.psum_tensor(n, s, d))
        qraw = [_sb(f"qraw{i}", [128, NT], BF16) for i in range(2)]
        qshf = [_sb(f"qshf{i}", [128, NT], BF16) for i in range(2)]
        qtmp = [_sb(f"qtmp{i}", [128, NT], BF16) for i in range(2)]
        qout = [_sb(f"qout{i}", [128, NT], BF16) for i in range(2)]
        vstg = [_sb(f"vstg{i}", [128, D], BF16) for i in range(2)]
        ps1 = [_ps(f"ps1_{i}", [128, NT], F32) for i in range(2)]

        mm_done = [None, None]
        ev_first = [None, None]
        ev_done = [None, None]
        st_done = [None, None]
        for c in range(32):
            p = c % 2
            if c >= 2:
                w = [("sT", mm_done[p])] if mm_done[p] else []
                wtok[p] = sc.op("gpsimd", DMA(wbuf[p][:, :, :], wqkvT.ap()[c]),
                                waits=w, sig="sWD", inc=16)
            w = [("sWD", wtok[p]), ("sV", ph0_h1_done)]
            if ev_first[p]:
                w.append(("sV", ev_first[p]))
            first = True
            for sp in range(4):
                for ci in range(16):
                    tk = sc.op("tensor", MM(ps1[p][:, ts(sp, 512)], wbuf[p][:, ci, :],
                                            h1T[:, ci, ts(sp, 512)], ci == 0, ci == 15),
                               waits=(w if first else []),
                               sig=("sT" if (sp == 3 and ci == 15) else None))
                    first = False
            mm_done[p] = tk
            w = [("sT", mm_done[p])]
            if st_done[p]:
                w.append(("sXD", st_done[p]))
            ev_first[p] = sc.op("vector", CP(qraw[p][:, :], ps1[p][:, :]), waits=w,
                                sig="sV")
            sc.op("vector", CP(qshf[p][0:32, :], qraw[p][32:64, :]), sig="sV")
            sc.op("vector", CP(qshf[p][32:64, :], qraw[p][0:32, :]), sig="sV")
            sc.op("vector", CP(qshf[p][64:96, :], qraw[p][96:128, :]), sig="sV")
            sc.op("vector", CP(qshf[p][96:128, :], qraw[p][64:96, :]), sig="sV")
            sc.op("vector", TT_(qtmp[p][:, :], qraw[p][:, :], cos_sb[:, :], ALU.mult),
                  waits=[("sWD", rope_tok)], sig="sV")
            sc.op("vector", TT_(qshf[p][:, :], qshf[p][:, :], sinm_sb[:, :], ALU.mult),
                  sig="sV")
            ev_done[p] = sc.op("vector", TT_(qout[p][:, :], qtmp[p][:, :],
                                             qshf[p][:, :], ALU.add), sig="sV")
            st_done[p] = sc.op("sync", DMA(qk16v[:, c, :], qout[p][:, :]),
                               waits=[("sV", ev_done[p])], sig="sXD", inc=16)
        qk_v_fence = sc.counts["sV"]
        vw_tok = [None, None]
        vmm = [None, None]
        vev = [None, None]
        vst = [None, None]
        for vs in range(4):
            pw = vs % 2
            w = [("sT", vmm[0]), ("sT", vmm[1])] if vs >= 2 else []
            vw_tok[pw] = sc.op("gpsimd", DMA(vwbuf[pw][:, :, :], wvT.ap()[vs]),
                               waits=w, sig="sWD", inc=16)
            for ti in range(NTT):
                pp = ti % 2
                w = [("sWD", vw_tok[pw]), ("sV", qk_v_fence)]
                if vev[pp]:
                    w.append(("sV", vev[pp]))
                first = True
                for ci in range(16):
                    tk = sc.op("tensor", MM(ps1[pp][:, 0:512], h1T[:, ci, ts(ti, 128)],
                                            vwbuf[pw][:, ci, :], ci == 0, ci == 15),
                               waits=(w if first else []),
                               sig=("sT" if ci == 15 else None))
                    first = False
                vmm[pp] = tk
                w = [("sT", vmm[pp])]
                if vst[pp]:
                    w.append(("sXD", vst[pp]))
                vev[pp] = sc.op("vector", CP(vstg[pp][:, ts(vs, 512)], ps1[pp][:, 0:512]),
                                waits=w, sig="sV")
                vst[pp] = sc.op("sync", DMA(v16v[:, ti, ts(vs, 512)],
                                            vstg[pp][:, ts(vs, 512)]),
                                waits=[("sV", vev[pp])], sig="sXD", inc=16)
    sc.barrier()

    ph01.close()

    sc.mark("ph2")
    with ExitStack() as ph:
        _sb = lambda n, s, d: ph.enter_context(nc.sbuf_tensor(n, s, d))
        _ps = lambda n, s, d=F32: ph.enter_context(nc.psum_tensor(n, s, d))
        Kb = _sb("Kb", [128, 16, BLK], BF16)
        Vb = _sb("Vb", [128, 8, D], BF16)
        ctxF = _sb("ctxF", [128, 16, BLK], BF16)
        Qc = [_sb(f"Qc{i}", [128, BLK], BF16) for i in range(2)]
        Ebuf = [_sb(f"Ebuf{i}", [128, BLK], BF16) for i in range(2)]
        Pbuf = [_sb(f"Pbuf{i}", [128, BLK], BF16) for i in range(2)]
        PTs = [_sb(f"PTs{i}", [128, BLK], BF16) for i in range(2)]
        sumc = _sb("sumc", [128, 4], F32)
        psS = [_ps(f"psS{i}", [128, BLK], F32) for i in range(2)]
        psPT = _ps("psPT", [128, 2 * BLK], BF16)
        psCtx = _ps("psCtx", [128, BLK], F32)
        SKEW = 2
        NU = 256

        ctx_st = None
        for b in range(NBLK):
            w = []
            if b > 0:
                w = [("sT", sc.counts["sT"]), ("sXD", ctx_st)]
            kld = sc.op("sync", DMA(Kb[:, :, :], qk16v[:, 16:32, ts(b, BLK)]),
                        waits=w, sig="sXD", inc=16)
            vld = sc.op("sync", DMA(Vb[:, :, :], v16v[:, ts(b, 8), :]),
                        sig="sXD", inc=16)
            q_ld = [None, None]
            q_last_mm = [None, None]
            sc_tok = [None] * NU
            exp_tok = [None] * NU
            av_tok = [None] * NU
            ctx_ev_tok = [None] * NU
            pscale_by_par = [None, None]
            pt_ev_last = None
            first_ev_of_block = True

            def unit(u):
                ci, rem = divmod(u, 16)
                hh, j = divmod(rem, 8)
                return ci, hh, j

            def emit_ctx_ev(v):
                nonlocal first_ev_of_block
                ci, hh, j = unit(v)
                r = hh * 64
                w = [("sT", av_tok[v])]
                if first_ev_of_block and ctx_st:
                    w.append(("sXD", ctx_st))
                first_ev_of_block = False
                ctx_ev_tok[v] = sc.op("scalar",
                                      SCP(ctxF[r:r + 64, ci, ts(j, 128)],
                                          psCtx[0:64, (v % 2) * 512:(v % 2) * 512 + 128]),
                                      waits=w, sig="sS")

            mk_tok = [None] * NU
            scale_tok = [None] * NU
            pt_tok = [None] * NU
            ptev_tok = [None] * NU
            for u in range(NU + 7):
                v = u - 2
                if 0 <= v < NU:
                    _, _, j = unit(v)
                    span = (j + 1) * 128
                    pv = v % 2
                    we = [("sV", mk_tok[v])]
                    if v >= 2 and scale_tok[v - 2]:
                        we.append(("sV", scale_tok[v - 2]))
                    exp_tok[v] = sc.op("scalar",
                                       ACTV(Ebuf[pv][:, 0:span], psS[pv][:, 0:span],
                                            AF.Exp, scale=0.125,
                                            acc=sumc[:, pv:pv + 1]),
                                       waits=we, sig="sS")
                v = u - 3
                if 0 <= v < NU:
                    _, _, j = unit(v)
                    span = (j + 1) * 128
                    pv = v % 2
                    sc.op("vector", RECIP(sumc[:, 2 + pv:3 + pv], sumc[:, pv:pv + 1]),
                          waits=[("sS", exp_tok[v])], sig="sV")
                    w2 = []
                    if v >= 2 and pt_tok[v - 2]:
                        w2.append(("sT", pt_tok[v - 2]))
                    scale_tok[v] = sc.op("vector", TSC(Pbuf[pv][:, 0:span],
                                                       Ebuf[pv][:, 0:span],
                                                       sumc[:, 2 + pv:3 + pv], None,
                                                       ALU.mult), waits=w2,
                                         sig="sV", self_edge=True)
                v = u - 6
                if 0 <= v < NU:
                    ci, hh, j = unit(v)
                    h = ci * 2 + hh
                    pv = v % 2
                    w = [("sV", ptev_tok[v]), ("sXD", vld)]
                    if v >= 2 and ctx_ev_tok[v - 2]:
                        w.append(("sS", ctx_ev_tok[v - 2]))
                    cbase = pv * 512
                    for i in range(j + 1):
                        tk = sc.op("tensor", MM(psCtx[0:64, cbase:cbase + 128],
                                                Vb[:, i, ts(h, 64)],
                                                PTs[pv][:, ts(i, 128)], i == 0, i == j),
                                   waits=(w if i == 0 else []),
                                   sig=("sT" if i == j else None))
                    av_tok[v] = tk
                    emit_ctx_ev(v)
                v = u - 5
                if 0 <= v < NU:
                    _, _, j = unit(v)
                    span = (j + 1) * 128
                    pv = v % 2
                    w = [("sT", pt_tok[v])]
                    if v >= 2 and av_tok[v - 2]:
                        w.append(("sT", av_tok[v - 2]))
                    ptev_tok[v] = sc.op("vector",
                                        CP(PTs[pv][:, 0:span],
                                           psPT[:, pv * 1024: pv * 1024 + span]),
                                        waits=w, sig="sV")
                v = u - 4
                if 0 <= v < NU:
                    _, _, j = unit(v)
                    pv = v % 2
                    w = [("sV", scale_tok[v])]
                    if v >= 2 and ptev_tok[v - 2]:
                        w.append(("sV", ptev_tok[v - 2]))
                    for i in range(j + 1):
                        tk = sc.op("tensor",
                                   TR(psPT[:, pv * 1024 + i * 128: pv * 1024 + i * 128 + 128],
                                      Pbuf[pv][:, ts(i, 128)], id16[:, :]),
                                   waits=(w if i == 0 else []),
                                   sig=("sT" if i == j else None))
                    pt_tok[v] = tk
                v = u - 1
                if 0 <= v < NU:
                    _, _, j = unit(v)
                    pv = v % 2
                    mk_tok[v] = sc.op("vector", TT_(psS[pv][:, ts(j, 128)],
                                                    psS[pv][:, ts(j, 128)],
                                                    msk[:, :], ALU.add),
                                      waits=[("sT", sc_tok[v]), ("sXD", mask_tok)],
                                      sig="sV")
                if u < NU:
                    ci, hh, j = unit(u)
                    pq = ci % 2
                    if hh == 0 and j == 0:
                        w = [("sXD", kld)]
                        if q_last_mm[pq]:
                            w.append(("sT", q_last_mm[pq]))
                        q_ld[pq] = sc.op("sync",
                                         DMA(Qc[pq][:, :], qk16v[:, ci, ts(b, BLK)]),
                                         waits=w, sig="sXD", inc=16)
                    r = hh * 64
                    span = (j + 1) * 128
                    slot = u % 2
                    w = [("sXD", q_ld[pq])]
                    if u >= 2 and exp_tok[u - 2]:
                        w.append(("sS", exp_tok[u - 2]))
                    first = True
                    nsp = (span + 511) // 512
                    for s_i in range(nsp):
                        lo, hi = s_i * 512, min(span, (s_i + 1) * 512)
                        tk = sc.op("tensor",
                                   MM(psS[slot][:, lo:hi], Qc[pq][r:r + 64, ts(j, 128)],
                                      Kb[r:r + 64, ci, lo:hi], True, True),
                                   waits=(w if first else []),
                                   sig=("sT" if s_i == nsp - 1 else None))
                        first = False
                    sc_tok[u] = tk
                    q_last_mm[pq] = tk
            ctx_st = sc.op("sync", DMA(ctx16v[:, :, ts(b, BLK)], ctxF[:, :, :]),
                           waits=[("sS", ctx_ev_tok[NU - 1])], sig="sXD", inc=16)
    sc.barrier()

    sc.mark("ph3")
    with ExitStack() as ph:
        _sb = lambda n, s, d: ph.enter_context(nc.sbuf_tensor(n, s, d))
        _ps = lambda n, s, d=F32: ph.enter_context(nc.psum_tensor(n, s, d))
        woTb = _sb("woTb", [128, 16, D], BF16)
        gamf = _sb("gamf", [128, D], F32)
        ctxh = [_sb(f"ctxh{i}", [128, 16, BLK], BF16) for i in range(2)]
        xnld = [_sb(f"xnld{i}", [128, D], F32) for i in range(2)]
        ytile = [_sb(f"ytile{i}", [128, D], F32) for i in range(2)]
        h2t = [_sb(f"h2t{i}", [128, D], BF16) for i in range(2)]
        h2stg = [_sb(f"h2stg{i}", [128, 16, 128], BF16) for i in range(2)]
        ysq = [_sb(f"ysq{i}", [128, D], F32) for i in range(2)]
        st3 = _sb("st3", [128, 8], F32)
        ps3 = [_ps(f"ps3_{i}", [128, 512], F32) for i in range(6)]
        psT3 = _ps("psT3", [128, NT], BF16)

        wo_ld = sc.op("gpsimd", DMA(woTb[:, :, :], woTv[:, :, :]), sig="sWD", inc=16)
        gf_ld = sc.op("sync", DMA(gamf[:, :], gam_f[:, :]), sig="sXD", inc=16)

        ev3 = [None] * 6
        ylast = [None, None]
        ydma = [None, None]
        h2dma = [None, None]
        h2ev = [None, None]
        sqts = [None, None]
        h2toks = [None] * NTT
        ctx_ld = None

        def emit_h2tr(t_j):
            pp = t_j % 2
            base = pp * 1024
            w = [("sV", h2toks[t_j])]
            if h2ev[pp]:
                w.append(("sV", h2ev[pp]))
            for ci in range(8):
                tk = sc.op("tensor", TR(psT3[:, base + ci * 128: base + ci * 128 + 128],
                                        h2t[pp][:, ts(ci, 128)], id16[:, :]),
                           waits=(w if ci == 0 else []),
                           sig=("sT" if ci == 7 else None))
            ev_a = sc.op("vector",
                         CP(h2stg[pp][:, 0:8, :],
                            psT3.ap().rearrange("p (c n) -> p c n", n=128)[:, 8 * pp: 8 * pp + 8, :]),
                         waits=[("sT", tk)], sig="sV")
            for ci in range(8):
                tk = sc.op("tensor", TR(psT3[:, base + ci * 128: base + ci * 128 + 128],
                                        h2t[pp][:, ts(8 + ci, 128)], id16[:, :]),
                           waits=([("sV", ev_a)] if ci == 0 else []),
                           sig=("sT" if ci == 7 else None))
            h2ev[pp] = sc.op("vector",
                             CP(h2stg[pp][:, 8:16, :],
                                psT3.ap().rearrange("p (c n) -> p c n", n=128)[:, 8 * pp: 8 * pp + 8, :]),
                             waits=[("sT", tk)], sig="sV")
            h2dma[pp] = sc.op("sync", DMA(h2T16v[:, :, ts(t_j, TT)],
                                          h2stg[pp][:, :, :]),
                              waits=[("sV", h2ev[pp])], sig="sXD", inc=16)

        def emit_stats(t_j):
            pp = t_j % 2
            b0 = 4 * pp
            SQC, VV, SQR, RSD = b0, b0 + 1, b0 + 2, b0 + 3
            col = lambda c: st3[:, c:c + 1]
            sqt = sc.op("scalar", ACTV(ysq[pp][:, :], ytile[pp][:, :], AF.Square,
                                       acc=col(SQC)), waits=[("sV", ytok3[t_j])],
                        sig="sS")
            sqts[pp] = sqt
            vv_tok = sc.op("vector", TSC(col(VV), col(SQC), 1.0 / D, 1e-5, ALU.mult,
                                         ALU.add), waits=[("sS", sqt)], sig="sV")
            sq2 = sc.op("scalar", ACTV(col(SQR), col(VV), AF.Sqrt),
                        waits=[("sV", vv_tok)], sig="sS")
            rst = sc.op("vector", RECIP(col(RSD), col(SQR)), waits=[("sS", sq2)],
                        sig="sV")
            w = [("sV", rst), ("sXD", gf_ld)]
            if h2dma[pp]:
                w.append(("sXD", h2dma[pp]))
            h2tok = sc.op("vector", STT(h2t[pp][:, :], ytile[pp][:, :], col(RSD),
                                        gamf[:, :], ALU.mult, ALU.mult),
                          waits=w, sig="sV", self_edge=True)
            ylast[pp] = h2tok
            h2toks[t_j] = h2tok

        ytok3 = [None] * NTT
        ctx_lds = [sc.op("sync", DMA(ctxh[hf][:, :, :], ctx16v[:, :, ts(hf, BLK)]),
                         sig="sXD", inc=16) for hf in range(2)]
        for half in range(2):
            ctx_ld = ctx_lds[half]
            for tt in range(8):
                t_i = half * 8 + tt
                pp = t_i % 2
                w0 = [("sXD", ctx_ld), ("sWD", wo_ld)]
                slots = [(4 * t_i + k) % 6 for k in range(4)]
                mmtok = [None] * 4
                for os_ in range(4):
                    slot = slots[os_]
                    for ci in range(16):
                        w = []
                        if ci == 0:
                            if os_ == 0:
                                w += w0
                            if ev3[slot]:
                                w.append(("sV", ev3[slot]))
                        tk = sc.op("tensor", MM(ps3[slot][:, :],
                                                ctxh[half][:, ci, ts(tt, 128)],
                                                woTb[:, ci, ts(os_, 512)],
                                                ci == 0, ci == 15),
                                   waits=w,
                                   sig=("sT" if ci == 15 else None))
                    mmtok[os_] = tk
                if t_i >= 2:
                    emit_h2tr(t_i - 2)
                w = [("sV", ylast[pp])] if ylast[pp] else []
                xn_ld = sc.op("sync", DMA(xnld[pp][:, :], xn32[ts(t_i, TT), :]),
                              waits=w, sig="sXD", inc=16)
                for os_ in range(4):
                    off = os_ * 512
                    w = [("sT", mmtok[os_]), ("sXD", xn_ld)]
                    if os_ == 0:
                        if ydma[pp]:
                            w.append(("sXD", ydma[pp]))
                        if sqts[pp]:
                            w.append(("sS", sqts[pp]))
                    etok = sc.op("vector", TT_(ytile[pp][:, off:off + 512],
                                               ps3[slots[os_]][:, :],
                                               xnld[pp][:, off:off + 512], ALU.add),
                                 waits=w, sig="sV")
                    ev3[slots[os_]] = etok
                ytok3[t_i] = etok
                ydma[pp] = sc.op("sync", DMA(y32[ts(t_i, TT), :], ytile[pp][:, :]),
                                 waits=[("sV", etok)], sig="sXD", inc=16)
                if t_i >= 1:
                    emit_stats(t_i - 1)
        emit_stats(NTT - 1)
        emit_h2tr(NTT - 2)
        emit_h2tr(NTT - 1)
    sc.barrier()

    sc.mark("ph4")
    with ExitStack() as ph:
        _sb = lambda n, s, d: ph.enter_context(nc.sbuf_tensor(n, s, d))
        h2Tg = _sb("h2Tg", [128, 16, BLK], BF16)
        actF = _sb("actF", [128, 32, BLK], BF16)
        wup_b = [_sb(f"wup_{i}", [128, 16, 128], BF16) for i in range(4)]
        sgb = [_sb(f"sgb_{i}", [128, BLK], F32) for i in range(2)]
        wdn_b = [_sb(f"wdn_{i}", [128, 32, 128], BF16) for i in range(2)]
        dF = [_sb(f"dF_{i}", [128, BLK], BF16) for i in range(2)]
        down2T = _sb("down2T", [128, 8, D], BF16)
        yb = [_sb(f"yb_{i}", [128, D], F32) for i in range(2)]
        d2f = _sb("d2f", [128, D], F32)
        ab = _sb("ab", [128, D], F32)

        wtu = [None, None]
        mmup = [None, None]
        evup = [None, None]
        wtd = [None, None]
        mmd = [None, None]
        evd = [None, None]
        trd_tok = [None, None]
        d2ev_last = None
        adma = None
        atok = [None, None]
        up_mm_last = None
        down_mm_last = None
        down_psum_free = []
        h2_ld_next = sc.op("sync", DMA(h2Tg[:, :, :], h2T16v[:, :, ts(0, BLK)]),
                           sig="sXD", inc=16)
        for g in range(2):
            h2_ld = h2_ld_next
            with ExitStack() as php:
                _ps2 = lambda n, s, d=F32: php.enter_context(nc.psum_tensor(n, s, d))
                ps4 = [_ps2(f"ps4_{g}_{i}", [128, BLK], F32) for i in range(4)]
                for c in range(32):
                    p = c % 2
                    pg_, pv_ = 2 * p, 2 * p + 1
                    w = [("sT", mmup[p])] if mmup[p] else []
                    sc.op("gpsimd", DMA(wup_b[pg_][:, :, :], wupT.ap()[c]),
                          waits=w, sig="sWD", inc=16)
                    wtu[p] = sc.op("gpsimd", DMA(wup_b[pv_][:, :, :],
                                                 wupT.ap()[c + 32]),
                                   sig="sWD", inc=16)
                    w = [("sWD", wtu[p]), ("sXD", h2_ld)]
                    if evup[p]:
                        w.append(("sV", evup[p]))
                    if c < 2:
                        for t_ in down_psum_free:
                            w.append(t_)
                    first = True
                    for which in range(2):
                        slot = 2 * p + which
                        for sp in range(2):
                            for ci in range(16):
                                tk = sc.op("tensor",
                                           MM(ps4[slot][:, ts(sp, 512)],
                                              wup_b[slot][:, ci, :],
                                              h2Tg[:, ci, ts(sp, 512)], ci == 0, ci == 15),
                                           waits=(w if first else []),
                                           sig=("sT" if (which == 1 and sp == 1 and ci == 15) else None))
                                first = False
                    mmup[p] = tk
                    w = [("sT", tk)]
                    if evup[p]:
                        w.append(("sV", evup[p]))
                    wact = list(w)
                    if c < 2 and down_mm_last:
                        wact.append(("sT", down_mm_last))
                    sg_tok = sc.op("scalar", ACTV(sgb[p][:, :], ps4[pg_][:, :], AF.Silu),
                                   waits=w, sig="sS")
                    wact.append(("sS", sg_tok))
                    evup[p] = sc.op("vector", TT_(actF[:, c, :], sgb[p][:, :],
                                                  ps4[pv_][:, :], ALU.mult),
                                    waits=wact, sig="sV")
                up_done = sc.counts["sV"]
                up_mm_last = mmup[0] if (mmup[0] or 0) > (mmup[1] or 0) else mmup[1]
            with ExitStack() as php:
                _ps2 = lambda n, s, d=F32: php.enter_context(nc.psum_tensor(n, s, d))
                ps4d = [_ps2(f"ps4d_{g}_{i}", [128, BLK], F32) for i in range(2)]
                psT4 = _ps2(f"psT4_{g}", [128, 2 * BLK], BF16)

                def emit_tr4(oc):
                    nonlocal d2ev_last
                    p = oc % 2
                    base = p * 1024
                    w = [("sV", evd[p])]
                    if d2ev_last:
                        w.append(("sV", d2ev_last))
                    for ti in range(8):
                        tk = sc.op("tensor",
                                   TR(psT4[:, base + ti * 128: base + ti * 128 + 128],
                                      dF[p][:, ts(ti, 128)], id16[:, :]),
                                   waits=(w if ti == 0 else []),
                                   sig=("sT" if ti == 7 else None))
                    trd_tok[p] = tk
                    d2ev_last = sc.op("vector",
                                      CP(down2T[:, :, ts(oc, 128)],
                                         psT4.ap().rearrange("p (c n) -> p c n", n=128)[:, 8 * p: 8 * p + 8, :]),
                                      waits=[("sT", tk)], sig="sV")

                for oc in range(16):
                    p = oc % 2
                    w = [("sV", up_done)]
                    if mmd[p]:
                        w.append(("sT", mmd[p]))
                    wtd[p] = sc.op("gpsimd", DMA(wdn_b[p][:, :, :], wdnT.ap()[oc]),
                                   waits=w, sig="sWD", inc=16)
                    w = [("sWD", wtd[p])]
                    if evd[p]:
                        w.append(("sV", evd[p]))
                    if oc < 2 and g == 0:
                        pass
                    first = True
                    for sp in range(2):
                        for ic in range(32):
                            tk = sc.op("tensor", MM(ps4d[p][:, ts(sp, 512)],
                                                    wdn_b[p][:, ic, :],
                                                    actF[:, ic, ts(sp, 512)],
                                                    ic == 0, ic == 31),
                                       waits=(w if first else []),
                                       sig=("sT" if (sp == 1 and ic == 31) else None))
                            first = False
                    mmd[p] = tk
                    if oc >= 1:
                        emit_tr4(oc - 1)
                    w = [("sT", tk)]
                    if trd_tok[p]:
                        w.append(("sT", trd_tok[p]))
                    evd[p] = sc.op("vector", CP(dF[p][:, :], ps4d[p][:, :]),
                                   waits=w, sig="sV")
                emit_tr4(15)
                down_mm_last = mmd[0] if (mmd[0] or 0) > (mmd[1] or 0) else mmd[1]
                down_psum_free = [("sV", d2ev_last), ("sV", evd[0]), ("sV", evd[1])]
                if g == 0:
                    h2_ld_next = sc.op("sync",
                                       DMA(h2Tg[:, :, :], h2T16v[:, :, ts(1, BLK)]),
                                       waits=[("sT", up_mm_last)], sig="sXD", inc=16)
                for ti in range(8):
                    t_i = g * 8 + ti
                    pp = ti % 2
                    w = [("sV", atok[pp])] if atok[pp] else []
                    y_ld = sc.op("scalar", DMA(yb[pp][:, :], y32[ts(t_i, TT), :]),
                                 waits=w, sig="sSD", inc=16)
                    w = [("sV", d2ev_last)]
                    if atok[1 - pp]:
                        w.append(("sV", atok[1 - pp]))
                    cvt = sc.op("vector", CP(d2f[:, :], down2T[:, ti, :]),
                                waits=w, sig="sV")
                    w = [("sSD", y_ld), ("sV", cvt)]
                    if adma:
                        w.append(("sXD", adma))
                    atok[pp] = sc.op("vector", TT_(ab[:, :], yb[pp][:, :],
                                                   d2f[:, :], ALU.add),
                                     waits=w, sig="sV")
                    adma = sc.op("sync", DMA(att32[ts(t_i, TT), :], ab[:, :]),
                                 waits=[("sV", atok[pp])], sig="sXD", inc=16)
    sc.barrier()

    sc.mark("ph5a")
    with ExitStack() as ph:
        _sb = lambda n, s, d: ph.enter_context(nc.sbuf_tensor(n, s, d))
        _ps = lambda n, s, d=F32: ph.enter_context(nc.psum_tensor(n, s, d))
        xnT2 = _sb("xnT2", [128, 16, NT], BF16)
        wgb = [_sb(f"wgb{i}", [128, 16, 512], BF16) for i in range(2)]
        wub = [_sb(f"wub{i}", [128, 16, 512], BF16) for i in range(2)]
        ps5 = [_ps(f"ps5_{i}", [128, 512], F32) for i in range(4)]
        psT5 = _ps("psT5", [128, 2 * BLK], BF16)
        sg5 = [_sb(f"sg5{i}", [128, 512], F32) for i in range(2)]
        hb = [_sb(f"hb{i}", [128, 512], BF16) for i in range(2)]
        qb = [_sb(f"qb{i}", [128, 512], BF16) for i in range(2)]
        cb = [_sb(f"cb{i}", [128, 6, 128], BF16) for i in range(2)]
        tb = [_sb(f"tb{i}", [128, 4, 128], BF16) for i in range(2)]
        kb = [_sb(f"kb{i}", [128, 512], BF16) for i in range(2)]
        hsb = [_sb(f"hsb{i}", [128, 512], BF16) for i in range(2)]
        stg5 = [_sb(f"stg5_{i}", [128, 4, 512], BF16) for i in range(2)]

        xnT2_ld = sc.op("sync", DMA(xnT2[:, :, :], xnT16v[:, :, :]), sig="sXD", inc=16)
        wt = [None, None]
        mm5 = [None, None]
        ev5 = [None] * (16 * NTT)
        sg_prev = [None, None]
        ht_prev = [None, None]
        tr_tok = [None, None]
        tr5ev = [None, None]
        stgdma = [None, None]
        TSKEW = 2

        def emit_tr5(v):
            s_v, t_v = divmod(v, NTT)
            pt = v % 2
            base = pt * 1024
            w = [("sV", ev5[v])]
            if tr5ev[pt]:
                w.append(("sV", tr5ev[pt]))
            for ii in range(4):
                tk = sc.op("tensor", TR(psT5[:, base + ii * 128: base + ii * 128 + 128],
                                        hsb[pt][:, ts(ii, 128)], id16[:, :]),
                           waits=(w if ii == 0 else []),
                           sig=("sT" if ii == 3 else None))
            tr_tok[pt] = tk
            tsub = t_v % 4
            pb = (t_v // 4) % 2
            w = [("sT", tk)]
            if tsub == 0 and stgdma[pb]:
                w.append(("sXD", stgdma[pb]))
            tr5ev[pt] = sc.op("vector",
                              CP(stg5[pb].ap().rearrange("p c (u n) -> p c u n", n=128)[:, :, tsub, :],
                                 psT5.ap().rearrange("p (c n) -> p c n", n=128)[:, 8 * pt: 8 * pt + 4, :]),
                              waits=w, sig="sV")
            if tsub == 3:
                stgdma[pb] = sc.op("sync",
                                   DMA(hsT16v[:, ts(s_v, 4), t_v // 4 * 512: t_v // 4 * 512 + 512],
                                       stg5[pb][:, :, :]),
                                   waits=[("sV", tr5ev[0]), ("sV", tr5ev[1])],
                                   sig="sXD", inc=16)

        for s in range(16):
            pw = s % 2
            w = [("sT", mm5[pw])] if wt[pw] else []
            sc.op("gpsimd", DMA(wgb[pw][:, :, :], wg.ap()[s]), waits=w,
                  sig="sWD", inc=16)
            wt[pw] = sc.op("gpsimd", DMA(wub[pw][:, :, :], wu.ap()[s]),
                           sig="sWD", inc=16)
            for t_i in range(NTT):
                u = s * NTT + t_i
                pt = u % 2
                gsl, usl = 2 * pt, 2 * pt + 1
                w = [("sWD", wt[pw]), ("sXD", xnT2_ld)]
                if sg_prev[pt]:
                    w.append(("sS", sg_prev[pt]))
                if ht_prev[pt]:
                    w.append(("sV", ht_prev[pt]))
                first = True
                for which in range(2):
                    slot = 2 * pt + which
                    wsrc = wgb[pw] if which == 0 else wub[pw]
                    for ci in range(16):
                        tk = sc.op("tensor", MM(ps5[slot][:, :],
                                                xnT2[:, ci, ts(t_i, 128)],
                                                wsrc[:, ci, :], ci == 0, ci == 15),
                                   waits=(w if first else []),
                                   sig=("sT" if (which == 1 and ci == 15) else None))
                        first = False
                mm5[pw] = tk
                if u >= TSKEW:
                    emit_tr5(u - TSKEW)
                w = [("sT", tk)]
                if ht_prev[pt]:
                    w.append(("sV", ht_prev[pt]))
                sg_prev[pt] = sc.op("scalar", ACTV(sg5[pt][:, :], ps5[gsl][:, :],
                                                   AF.Silu), waits=w, sig="sS")
                w = [("sS", sg_prev[pt])]
                if tr_tok[pt]:
                    w.append(("sT", tr_tok[pt]))
                htok = sc.op("vector", TT_(hb[pt][:, :], sg5[pt][:, :], ps5[usl][:, :],
                                           ALU.mult), waits=w, sig="sV")
                ht_prev[pt] = htok
                A = lambda buf: buf[:, 0:128]
                B = lambda buf: buf[:, 128:256]
                C = lambda buf: buf[:, 256:384]
                Dp = lambda buf: buf[:, 384:512]
                sc.op("vector", TT_(qb[pt][:, :], hb[pt][:, :], hb[pt][:, :], ALU.mult),
                      sig="sV")
                q_ = qb[pt]
                c_ = cb[pt]
                sc.op("vector", TT_(c_[:, 0, :], A(q_), B(q_), ALU.is_ge), sig="sV")
                sc.op("vector", TT_(c_[:, 1, :], A(q_), C(q_), ALU.is_ge), sig="sV")
                sc.op("vector", TT_(c_[:, 2, :], A(q_), Dp(q_), ALU.is_ge), sig="sV")
                sc.op("vector", TT_(c_[:, 3, :], B(q_), C(q_), ALU.is_ge), sig="sV")
                sc.op("vector", TT_(c_[:, 4, :], B(q_), Dp(q_), ALU.is_ge), sig="sV")
                sc.op("vector", TT_(c_[:, 5, :], C(q_), Dp(q_), ALU.is_ge), sig="sV")
                t_ = tb[pt]
                k_ = kb[pt]
                tA, tB, tC, tD = t_[:, 0, :], t_[:, 1, :], t_[:, 2, :], t_[:, 3, :]
                sc.op("vector", TT_(tA, c_[:, 0, :], c_[:, 1, :], ALU.add), sig="sV")
                sc.op("vector", TT_(tB, c_[:, 3, :], c_[:, 4, :], ALU.add), sig="sV")
                sc.op("vector", TT_(tC, c_[:, 1, :], c_[:, 3, :], ALU.add), sig="sV")
                sc.op("vector", TT_(tD, c_[:, 2, :], c_[:, 4, :], ALU.add), sig="sV")
                sc.op("vector", TT_(tA, tA, c_[:, 2, :], ALU.add), sig="sV")
                sc.op("vector", TT_(tB, tB, c_[:, 0, :], ALU.subtract), sig="sV")
                sc.op("vector", TT_(tC, tC, c_[:, 5, :], ALU.subtract), sig="sV")
                sc.op("vector", TT_(tD, tD, c_[:, 5, :], ALU.add), sig="sV")
                sc.op("vector", TSC(A(k_), tA, 1.5, None, ALU.is_ge), sig="sV")
                sc.op("vector", TSC(B(k_), tB, 0.5, None, ALU.is_ge), sig="sV")
                sc.op("vector", TSC(C(k_), tC, 0.5, None, ALU.is_le), sig="sV")
                sc.op("vector", TSC(Dp(k_), tD, 1.5, None, ALU.is_le), sig="sV")
                ev5[u] = sc.op("vector", TT_(hsb[pt][:, :], hb[pt][:, :], kb[pt][:, :],
                                             ALU.mult), sig="sV", self_edge=True)
        for v in range(16 * NTT - TSKEW, 16 * NTT):
            emit_tr5(v)
    sc.barrier()

    sc.mark("ph5b")
    with ExitStack() as ph:
        _sb = lambda n, s, d: ph.enter_context(nc.sbuf_tensor(n, s, d))
        _ps = lambda n, s, d=F32: ph.enter_context(nc.psum_tensor(n, s, d))
        hsTq = _sb("hsTq", [128, 64, 512], BF16)
        wdb = [_sb(f"wdb{i}", [128, 64, 128], BF16) for i in range(4)]
        dFb = [_sb(f"dFb{i}", [128, 512], BF16) for i in range(2)]
        ffnT = _sb("ffnT", [128, 4, D], BF16)
        xb = [_sb(f"xb5{i}", [128, D], F32) for i in range(2)]
        atb = [_sb(f"atb5{i}", [128, D], F32) for i in range(2)]
        ff32 = _sb("ff32", [128, D], F32)
        ob = _sb("ob5", [128, D], F32)
        ps5d = [_ps(f"ps5d_{i}", [128, 512], F32) for i in range(2)]
        psT5b = _ps("psT5b", [128, 2 * BLK], BF16)

        hs_ld = None
        odma = [None, None]
        wt = [None, None, None, None]
        mmd = [None, None, None, None]
        slot_oc = [None, None, None, None]
        evd = [None, None]
        tr_tok = [None, None]
        f_ev_last = None
        def preload_hs(qt):
            w = [("sT", t) for t in mmd if t]
            a = sc.op("sync", DMA(hsTq[:, 0:32, :], hsT16v[:, 0:32, ts(qt, 512)]),
                      waits=w, sig="sXD", inc=16)
            b = sc.op("sync", DMA(hsTq[:, 32:64, :], hsT16v[:, 32:64, ts(qt, 512)]),
                      sig="sXD", inc=16)
            return a, b

        hs_next = preload_hs(0)
        for qt in range(4):
            hs_ld1, hs_ld = hs_next

            def emit_tr5b(oc):
                nonlocal f_ev_last
                p = oc % 2
                base = p * 1024
                w = [("sV", evd[p])]
                if f_ev_last:
                    w.append(("sV", f_ev_last))
                for ii in range(4):
                    tk = sc.op("tensor",
                               TR(psT5b[:, base + ii * 128: base + ii * 128 + 128],
                                  dFb[p][:, ts(ii, 128)], id16[:, :]),
                               waits=(w if ii == 0 else []),
                               sig=("sT" if ii == 3 else None))
                tr_tok[p] = tk
                f_ev_last = sc.op("vector",
                                  CP(ffnT[:, :, ts(oc, 128)],
                                     psT5b.ap().rearrange("p (c n) -> p c n", n=128)[:, 8 * p: 8 * p + 4, :]),
                                  waits=[("sT", tk)], sig="sV")

            oc_order = [(12 * qt + k) % 16 for k in range(16)]
            for oc in oc_order:
                p = oc % 2
                p4 = oc % 4
                if slot_oc[p4] != oc:
                    w = []
                    if wt[p4]:
                        w.append(("sT", mmd[p4]))
                    wt[p4] = sc.op("gpsimd", DMA(wdb[p4][:, :, :], wd.ap()[oc]),
                                   waits=w, sig="sWD", inc=16)
                    slot_oc[p4] = oc
                w = [("sWD", wt[p4]), ("sXD", hs_ld1)]
                if evd[p]:
                    w.append(("sV", evd[p]))
                first = True
                for ic in range(64):
                    if ic == 32:
                        w2 = [("sXD", hs_ld)]
                    else:
                        w2 = []
                    tk = sc.op("tensor", MM(ps5d[p][:, :], wdb[p4][:, ic, :],
                                            hsTq[:, ic, :], ic == 0, ic == 63),
                               waits=(w if first else w2),
                               sig=("sT" if ic == 63 else None))
                    first = False
                mmd[p4] = tk
                ko = oc_order.index(oc)
                if ko >= 1:
                    emit_tr5b(oc_order[ko - 1])
                w = [("sT", tk)]
                if tr_tok[p]:
                    w.append(("sT", tr_tok[p]))
                evd[p] = sc.op("vector", CP(dFb[p][:, :], ps5d[p][:, :]),
                               waits=w, sig="sV")
            emit_tr5b(oc_order[15])
            if qt < 3:
                hs_next = preload_hs(qt + 1)
            for ti in range(4):
                t_i = qt * 4 + ti
                pp = ti % 2
                w = [("sXD", odma[pp])] if odma[pp] else []
                x_ld = sc.op("scalar", DMA(xb[pp][:, :], x[ts(t_i, TT), :]), waits=w,
                             sig="sSD", inc=16)
                at_ld = sc.op("sync", DMA(atb[pp][:, :], att32[ts(t_i, TT), :]),
                              waits=w, sig="sXD", inc=16)
                cvt = sc.op("vector", CP(ff32[:, :], ffnT[:, ti, :]),
                            waits=[("sV", f_ev_last)], sig="sV")
                wo_ = [("sSD", x_ld), ("sXD", at_ld), ("sV", cvt)]
                for t_ in odma:
                    if t_:
                        wo_.append(("sXD", t_))
                o1 = sc.op("vector", TT_(ob[:, :], xb[pp][:, :], atb[pp][:, :],
                                         ALU.add), waits=wo_, sig="sV")
                o2 = sc.op("vector", TT_(ob[:, :], ob[:, :], ff32[:, :],
                                         ALU.add), sig="sV")
                odma[pp] = sc.op("sync", DMA(out[ts(t_i, TT), :], ob[:, :]),
                                 waits=[("sV", o2)], sig="sXD", inc=16)

    sc.op("sync", (lambda e: e.nop()),
          waits=[("sXD", sc.counts["sXD"]), ("sSD", sc.counts["sSD"]),
                 ("sWD", sc.counts["sWD"])])

    sc.mark("tail")
    sc.emit()
    nc.finalize()
    nc._sched = sc
    nc._keepalive = stack
    return nc


def _host_inputs(inputs):
    f32 = np.float32
    xall = np.ascontiguousarray(np.asarray(inputs["L0_attn_x"], f32).reshape(NCORE * NT, D))
    wqkvT0 = np.asarray(inputs["L0_attn_w_qkv"], f32).T
    wqkvT = np.ascontiguousarray(
        wqkvT0.reshape(16, 128, 48, 128).transpose(2, 1, 0, 3))
    wvT = np.ascontiguousarray(
        wqkvT0[:, 4096:].reshape(16, 128, 4, 512).transpose(2, 1, 0, 3))
    woT = np.ascontiguousarray(np.asarray(inputs["L0_attn_w_o"], f32).T)
    wupT = np.ascontiguousarray(
        np.asarray(inputs["L0_attn_w_up"], f32).T
        .reshape(16, 128, 64, 128).transpose(2, 1, 0, 3))
    wdnT = np.ascontiguousarray(
        np.asarray(inputs["L0_attn_w_down"], f32).T
        .reshape(32, 128, 16, 128).transpose(2, 1, 0, 3))
    wg0 = np.asarray(inputs["L0_ffn_w_gate"], f32)
    wu0 = np.asarray(inputs["L0_ffn_w_up"], f32)
    wd0 = np.asarray(inputs["L0_ffn_w_down"], f32)
    gidx = np.arange(IF) // 4
    m = np.arange(IF) % 4
    new = (gidx // 128) * 512 + m * 128 + (gidx % 128)
    inv = np.empty(IF, np.int64)
    inv[new] = np.arange(IF)
    wg_ = np.ascontiguousarray(
        wg0[:, inv].reshape(16, 128, 16, 512).transpose(2, 1, 0, 3))
    wu_ = np.ascontiguousarray(
        wu0[:, inv].reshape(16, 128, 16, 512).transpose(2, 1, 0, 3))
    wd_ = np.ascontiguousarray(
        wd0[inv, :].reshape(64, 128, 16, 128).transpose(2, 1, 0, 3))
    gam_a = np.ascontiguousarray(np.tile(np.asarray(inputs["L0_attn_attn_norm_w"], f32)[None, :], (128, 1)))
    gam_f = np.ascontiguousarray(np.tile(np.asarray(inputs["L0_attn_ffn_norm_w"], f32)[None, :], (128, 1)))
    ident32 = np.eye(128, dtype=f32)
    ident16 = np.eye(128, dtype=ml_dtypes.bfloat16)
    qq, kk = np.meshgrid(np.arange(128), np.arange(128), indexing="ij")
    masktri = np.where(kk > qq, NEG, 0.0).astype(f32)
    invf = (1.0 / 10000.0 ** (np.arange(0, HD, 2, dtype=np.float64) / HD)).astype(f32)
    p = np.arange(128)
    sgn = np.where((p % 64) < 32, -1.0, 1.0).astype(f32)
    maps = []
    for c in range(NCORE):
        pos = ((c % 2) * NT + np.arange(NT)).astype(f32)
        ang = pos[None, :] * invf[p % 32][:, None]
        cosT = np.cos(ang).astype(f32)
        sinmT = (sgn[:, None] * np.sin(ang)).astype(f32)
        maps.append({
            "x": np.ascontiguousarray(xall[c * NT:(c + 1) * NT]),
            "wqkvT": wqkvT, "wvT": wvT, "woT": woT, "wupT": wupT, "wdnT": wdnT,
            "wg": wg_, "wu": wu_, "wd": wd_,
            "gam_a": gam_a, "gam_f": gam_f,
            "cosT": np.ascontiguousarray(cosT),
            "sinmT": np.ascontiguousarray(sinmT),
            "masktri": masktri, "ident32": ident32, "ident16": ident16,
        })
    return maps


def kernel(**inputs):
    maps = _host_inputs(inputs)
    nc = build()
    res = run_bass_kernel_spmd(nc, maps, core_ids=list(range(NCORE)))
    outs = [res.results[c]["out"] for c in range(NCORE)]
    return np.concatenate(outs, axis=0).reshape(4, 4096, D).astype(np.float32)


# version 13
# speedup vs baseline: 1.1558x; 1.1071x over previous
import numpy as np
import ml_dtypes
import concourse.bass as bass
import concourse.mybir as mybir
from concourse.bass_utils import run_bass_kernel_spmd
from contextlib import ExitStack

F32 = mybir.dt.float32
BF16 = mybir.dt.bfloat16
AF = mybir.ActivationFunctionType
ALU = mybir.AluOpType
AX = mybir.AxisListType

D = 2048
NT = 2048
TT = 128
NTT = 16
BLK = 1024
NBLK = 2
NH = 32
HD = 64
QKV = 3 * D
IA = 8192
IAH = 4096
IF = 8192
NCORE = 8
NEG = -1.0e30

DEBUG_TAPS = False


def ts(i, n):
    return slice(i * n, (i + 1) * n)


def DMA(out, in_):
    return lambda e: e.dma_start(out=out, in_=in_)


def MM(out, lhsT, rhs, start, stop):
    return lambda e: e.matmul(out, lhsT, rhs, start=start, stop=stop)


def TR(out, in_, idn):
    return lambda e: e.transpose(out, in_, idn)


def CP(out, in_):
    return lambda e: e.tensor_copy(out, in_)


def SCP(out, in_):
    return lambda e: e.copy(out, in_)


def TT_(out, a, b, op):
    return lambda e: e.tensor_tensor(out, a, b, op)


def TSC(out, a, s1, s2, op0, op1=None, acc=None):
    if op1 is None:
        return lambda e: e.tensor_scalar(out, a, s1, None, op0, accum_out=acc)
    return lambda e: e.tensor_scalar(out, a, s1, s2, op0, op1, accum_out=acc)


def STT(out, in0, s, in1, op0, op1):
    return lambda e: e.scalar_tensor_tensor(out, in0, s, in1, op0, op1)


def ACTV(out, in_, f, bias=0.0, scale=1.0, acc=None):
    return lambda e: e.activation(out, in_, f, bias=bias, scale=scale, accum_out=acc)


def RSUM(out, in_):
    return lambda e: e.reduce_sum(out, in_, axis=AX.X)


def RECIP(out, in_):
    return lambda e: e.reciprocal(out, in_)


class Sched:
    ENGS = ("sync", "scalar", "vector", "tensor", "gpsimd")

    def __init__(self, nc, stack):
        self.nc = nc
        self.ops = {e: [] for e in self.ENGS}
        self.sems = {}
        self.counts = {}
        self.guaranteed = {e: {} for e in self.ENGS}
        self.pending = None
        self.pending_applied = set()
        self.marks = []
        self.names = {e: [] for e in self.ENGS}

    def mark(self, label):
        self.marks.append((label, {e: len(self.ops[e]) for e in self.ENGS}))

    def sem(self, name, stack):
        self.sems[name] = stack.enter_context(self.nc.semaphore(name))
        self.counts[name] = 0
        return name

    def barrier(self):
        self.pending = {s: c for s, c in self.counts.items() if c > 0}
        self.pending_applied = set()

    def op(self, eng, emit, waits=(), sig=None, inc=1, self_edge=False):
        tok = None
        if self_edge and sig is not None and self.counts[sig] > 0:
            waits = list(waits) + [(sig, self.counts[sig])]
        if sig is not None:
            self.counts[sig] += inc
            tok = self.counts[sig]
        ws = []
        if self.pending is not None and eng not in self.pending_applied:
            self.pending_applied.add(eng)
            for s, v in self.pending.items():
                ws.append((s, v))
        for s, v in waits:
            if v is None:
                continue
            ws.append((s, v))
        g = self.guaranteed[eng]
        ws2 = []
        for s, v in ws:
            if g.get(s, -1) >= v:
                continue
            g[s] = v
            ws2.append((s, v))
        self.ops[eng].append((emit, ws2, sig, inc))
        return tok

    def emit(self):
        nc = self.nc
        with nc.Block(no_gpsimd_drain=True) as block:
            for eng in self.ENGS:
                deco = getattr(block, eng)
                lst = self.ops[eng]
                sems = self.sems

                def make(lst, eng):
                    def body(handle):
                        nm = self.names[eng]
                        for emit, ws, sig, inc in lst:
                            for s, v in ws:
                                handle.wait_ge(sems[s], v)
                            inst = emit(handle)
                            if sig is not None:
                                inst.then_inc(sems[sig], inc)
                            try:
                                nm.append(inst.ins.name)
                            except Exception:
                                nm.append(None)
                    return body

                deco(make(lst, eng))


def build():
    nc = bass.Bass("TRN2", target_bir_lowering=False)
    dbg = "ExternalOutput" if DEBUG_TAPS else "Internal"

    x = nc.dram_tensor("x", [NT, D], F32, kind="ExternalInput")
    wqkvT = nc.dram_tensor("wqkvT", [48, 128, 16, 128], F32, kind="ExternalInput")
    wvT = nc.dram_tensor("wvT", [4, 128, 16, 512], F32, kind="ExternalInput")
    woT = nc.dram_tensor("woT", [D, D], F32, kind="ExternalInput")
    wupT = nc.dram_tensor("wupT", [64, 128, 16, 128], F32, kind="ExternalInput")
    wdnT = nc.dram_tensor("wdnT", [16, 128, 32, 128], F32, kind="ExternalInput")
    wg = nc.dram_tensor("wg", [16, 128, 16, 512], F32, kind="ExternalInput")
    wu = nc.dram_tensor("wu", [16, 128, 16, 512], F32, kind="ExternalInput")
    wd = nc.dram_tensor("wd", [16, 128, 64, 128], F32, kind="ExternalInput")
    gam_a = nc.dram_tensor("gam_a", [128, D], F32, kind="ExternalInput")
    gam_f = nc.dram_tensor("gam_f", [128, D], F32, kind="ExternalInput")
    cosTd = nc.dram_tensor("cosT", [128, NT], F32, kind="ExternalInput")
    sinmTd = nc.dram_tensor("sinmT", [128, NT], F32, kind="ExternalInput")
    masktri = nc.dram_tensor("masktri", [128, 128], F32, kind="ExternalInput")
    ident32d = nc.dram_tensor("ident32", [128, 128], F32, kind="ExternalInput")
    ident16d = nc.dram_tensor("ident16", [128, 128], BF16, kind="ExternalInput")

    out = nc.dram_tensor("out", [NT, D], F32, kind="ExternalOutput")

    xn32 = nc.dram_tensor("xn32", [NT, D], F32, kind=dbg)
    y32 = nc.dram_tensor("y32", [NT, D], F32, kind=dbg)
    att32 = nc.dram_tensor("att32", [NT, D], F32, kind=dbg)
    xnT16 = nc.dram_tensor("xnT16", [16, 128, NT], BF16, kind=dbg)
    qk16 = nc.dram_tensor("qk16", [32, 128, NT], BF16, kind=dbg)
    v16 = nc.dram_tensor("v16", [16, 128, D], BF16, kind=dbg)
    ctx16 = nc.dram_tensor("ctx16", [16, 128, NT], BF16, kind=dbg)
    h2T16 = nc.dram_tensor("h2T16", [16, 128, NT], BF16, kind=dbg)
    hsT16 = nc.dram_tensor("hsT16", [64, 128, NT], BF16, kind=dbg)

    xnT16v = xnT16.ap().rearrange("c p n -> p c n")
    qk16v = qk16.ap().rearrange("c p n -> p c n")
    v16v = v16.ap().rearrange("c p n -> p c n")
    ctx16v = ctx16.ap().rearrange("c p n -> p c n")
    h2T16v = h2T16.ap().rearrange("c p n -> p c n")
    hsT16v = hsT16.ap().rearrange("c p n -> p c n")
    woTv = woT.ap().rearrange("(c p) n -> p c n", p=128)

    stack = ExitStack()
    sc = Sched(nc, stack)
    for s in ("sWD", "sXD", "sSD", "sT", "sV", "sS"):
        sc.sem(s, stack)

    def sb(name, shape, dtype):
        return stack.enter_context(nc.sbuf_tensor(name, shape, dtype))

    id32 = sb("id32", [128, 128], F32)
    id16 = sb("id16", [128, 128], BF16)
    msk = sb("msk", [128, 128], F32)
    sc.op("sync", DMA(id32[:, :], ident32d[:, :]), sig="sXD", inc=16)
    sc.op("sync", DMA(id16[:, :], ident16d[:, :]), sig="sXD", inc=16)
    mask_tok = sc.op("sync", DMA(msk[:, :], masktri[:, :]), sig="sXD", inc=16)

    ph01 = ExitStack()
    h1T = ph01.enter_context(nc.sbuf_tensor("h1T", [128, 16, NT], BF16))
    cos_sb = ph01.enter_context(nc.sbuf_tensor("cos_sb", [128, NT], BF16))
    sinm_sb = ph01.enter_context(nc.sbuf_tensor("sinm_sb", [128, NT], BF16))
    wbuf = [ph01.enter_context(nc.sbuf_tensor(f"w1_{i}", [128, 16, 128], BF16))
            for i in range(2)]
    vwbuf = [ph01.enter_context(nc.sbuf_tensor(f"vw{i}", [128, 16, 512], BF16))
             for i in range(2)]
    sc.op("gpsimd", DMA(cos_sb[:, :], cosTd[:, :]), sig="sWD", inc=16)
    rope_tok = sc.op("gpsimd", DMA(sinm_sb[:, :], sinmTd[:, :]), sig="sWD", inc=16)
    wtok = [None, None]
    wtok[0] = sc.op("gpsimd", DMA(wbuf[0][:, :, :], wqkvT.ap()[0]), sig="sWD", inc=16)
    wtok[1] = sc.op("gpsimd", DMA(wbuf[1][:, :, :], wqkvT.ap()[1]), sig="sWD", inc=16)

    sc.mark("ph0")
    with ExitStack() as ph:
        _sb = lambda n, s, d: ph.enter_context(nc.sbuf_tensor(n, s, d))
        _ps = lambda n, s, d=F32: ph.enter_context(nc.psum_tensor(n, s, d))
        xt = [_sb(f"xt{i}", [128, D], F32) for i in range(2)]
        xnsb = [_sb(f"xnsb{i}", [128, D], F32) for i in range(2)]
        h1sb = [_sb(f"h1sb{i}", [128, D], F32) for i in range(2)]
        sqscr = [_sb(f"sqscr{i}", [128, D], F32) for i in range(2)]
        stgx = [_sb(f"stgx{i}", [128, 16, 128], BF16) for i in range(2)]
        st0 = _sb("st0", [128, 32], F32)
        gama = _sb("gama", [128, D], F32)
        psTr = _ps("psTr", [128, D], F32)

        ga_tok = sc.op("sync", DMA(gama[:, :], gam_a[:, :]), sig="sXD", inc=16)
        ld = [None, None]
        xn_done = [None, None]
        h1_done = [None, None]
        dx = [None, None]
        dxT = [None, None]
        h1ev = [None, None]
        trev_last = None
        for t_i in range(NTT):
            p = t_i % 2
            b0 = 16 * p
            (S1, M, MSQ, VAR, VA, SQT, RSTD, S2C, BB, S2S, S2R) = (
                b0, b0 + 1, b0 + 2, b0 + 3, b0 + 4, b0 + 5, b0 + 6, b0 + 7,
                b0 + 8, b0 + 9, b0 + 10)
            col = lambda c: st0[:, c:c + 1]
            w = [("sV", xn_done[p])] if xn_done[p] else []
            ld[p] = sc.op("sync", DMA(xt[p][:, :], x[ts(t_i, TT), :]), waits=w,
                          sig="sXD", inc=16)
            w = [("sXD", ld[p])]
            if h1_done[p]:
                w.append(("sV", h1_done[p]))
            s2tok = sc.op("scalar", ACTV(sqscr[p][:, :], xt[p][:, :], AF.Square,
                                         acc=col(S2C)), waits=w, sig="sS")
            sc.op("vector", RSUM(col(S1), xt[p][:, :]), waits=[("sXD", ld[p])], sig="sV")
            sc.op("vector", TSC(col(M), col(S1), 1.0 / D, None, ALU.mult), sig="sV",
                  self_edge=True)
            sc.op("vector", TT_(col(MSQ), col(M), col(M), ALU.mult), sig="sV",
                  self_edge=True)
            sc.op("vector", TSC(col(VAR), col(S2C), 1.0 / D, col(MSQ), ALU.mult,
                                ALU.subtract), waits=[("sS", s2tok)], sig="sV",
                  self_edge=True)
            va_tok = sc.op("vector", TSC(col(VA), col(VAR), 1e-5, None, ALU.add),
                           sig="sV", self_edge=True)
            sqtok = sc.op("scalar", ACTV(col(SQT), col(VA), AF.Sqrt),
                          waits=[("sV", va_tok)], sig="sS")
            sc.op("vector", RECIP(col(RSTD), col(SQT)), waits=[("sS", sqtok)], sig="sV")
            w = []
            if dxT[p]:
                w.append(("sXD", dxT[p]))
            xn_done[p] = sc.op("vector", TSC(xnsb[p][:, :], xt[p][:, :], col(M),
                                             col(RSTD), ALU.subtract, ALU.mult),
                               waits=w, sig="sV", self_edge=True)
            dx[p] = sc.op("sync", DMA(xn32[ts(t_i, TT), :], xnsb[p][:, :]),
                          waits=[("sV", xn_done[p])], sig="sXD", inc=16)
            sc.op("vector", RECIP(col(BB), col(VA)), sig="sV")
            sc.op("vector", TT_(col(BB), col(VAR), col(BB), ALU.mult), sig="sV",
                  self_edge=True)
            bb_tok = sc.op("vector", TSC(col(BB), col(BB), 1e-5, None, ALU.add),
                           sig="sV", self_edge=True)
            sq2 = sc.op("scalar", ACTV(col(S2S), col(BB), AF.Sqrt),
                        waits=[("sV", bb_tok)], sig="sS")
            sc.op("vector", RECIP(col(S2R), col(S2S)), waits=[("sS", sq2)], sig="sV")
            w = [("sXD", ga_tok)]
            if h1ev[p]:
                w.append(("sV", h1ev[p]))
            h1_done[p] = sc.op("vector", STT(h1sb[p][:, :], xnsb[p][:, :], col(S2R),
                                             gama[:, :], ALU.mult, ALU.mult),
                               waits=w, sig="sV", self_edge=True)
            w = [("sV", xn_done[p])]
            if trev_last:
                w.append(("sV", trev_last))
            for ci in range(16):
                tk = sc.op("tensor", TR(psTr[:, ts(ci, 128)], xnsb[p][:, ts(ci, 128)],
                                        id32[:, :]),
                           waits=(w if ci == 0 else []), sig=("sT" if ci == 15 else None))
            wv = [("sT", tk)]
            if dxT[p]:
                wv.append(("sXD", dxT[p]))
            trev = sc.op("scalar", SCP(stgx[p][:, :, :],
                                       psTr.ap().rearrange("p (c n) -> p c n", n=128)),
                         waits=wv, sig="sS")
            dxT[p] = sc.op("sync", DMA(xnT16v[:, :, ts(t_i, TT)], stgx[p][:, :, :]),
                           waits=[("sS", trev)], sig="sXD", inc=16)
            w = [("sV", h1_done[p]), ("sS", trev)]
            for ci in range(16):
                tk = sc.op("tensor", TR(psTr[:, ts(ci, 128)], h1sb[p][:, ts(ci, 128)],
                                        id32[:, :]),
                           waits=(w if ci == 0 else []), sig=("sT" if ci == 15 else None))
            trev_last = sc.op("vector",
                              CP(h1T[:, :, ts(t_i, TT)],
                                 psTr.ap().rearrange("p (c n) -> p c n", n=128)),
                              waits=[("sT", tk)], sig="sV")
            h1ev[p] = trev_last
        ph0_h1_done = trev_last

    sc.mark("ph1")
    with ExitStack() as ph:
        _sb = lambda n, s, d: ph.enter_context(nc.sbuf_tensor(n, s, d))
        _ps = lambda n, s, d=F32: ph.enter_context(nc.psum_tensor(n, s, d))
        qraw = [_sb(f"qraw{i}", [128, NT], BF16) for i in range(2)]
        qshf = [_sb(f"qshf{i}", [128, NT], BF16) for i in range(2)]
        qtmp = [_sb(f"qtmp{i}", [128, NT], BF16) for i in range(2)]
        qout = [_sb(f"qout{i}", [128, NT], BF16) for i in range(2)]
        vstg = [_sb(f"vstg{i}", [128, D], BF16) for i in range(2)]
        ps1 = [_ps(f"ps1_{i}", [128, NT], F32) for i in range(2)]

        mm_done = [None, None]
        ev_first = [None, None]
        ev_done = [None, None]
        st_done = [None, None]
        for c in range(32):
            p = c % 2
            if c >= 2:
                w = [("sT", mm_done[p])] if mm_done[p] else []
                wtok[p] = sc.op("gpsimd", DMA(wbuf[p][:, :, :], wqkvT.ap()[c]),
                                waits=w, sig="sWD", inc=16)
            w = [("sWD", wtok[p]), ("sV", ph0_h1_done)]
            if ev_first[p]:
                w.append(("sV", ev_first[p]))
            first = True
            for sp in range(4):
                for ci in range(16):
                    tk = sc.op("tensor", MM(ps1[p][:, ts(sp, 512)], wbuf[p][:, ci, :],
                                            h1T[:, ci, ts(sp, 512)], ci == 0, ci == 15),
                               waits=(w if first else []),
                               sig=("sT" if (sp == 3 and ci == 15) else None))
                    first = False
            mm_done[p] = tk
            w = [("sT", mm_done[p])]
            if st_done[p]:
                w.append(("sXD", st_done[p]))
            ev_first[p] = sc.op("vector", CP(qraw[p][:, :], ps1[p][:, :]), waits=w,
                                sig="sV")
            sc.op("vector", CP(qshf[p][0:32, :], qraw[p][32:64, :]), sig="sV")
            sc.op("vector", CP(qshf[p][32:64, :], qraw[p][0:32, :]), sig="sV")
            sc.op("vector", CP(qshf[p][64:96, :], qraw[p][96:128, :]), sig="sV")
            sc.op("vector", CP(qshf[p][96:128, :], qraw[p][64:96, :]), sig="sV")
            sc.op("vector", TT_(qtmp[p][:, :], qraw[p][:, :], cos_sb[:, :], ALU.mult),
                  waits=[("sWD", rope_tok)], sig="sV")
            sc.op("vector", TT_(qshf[p][:, :], qshf[p][:, :], sinm_sb[:, :], ALU.mult),
                  sig="sV")
            ev_done[p] = sc.op("vector", TT_(qout[p][:, :], qtmp[p][:, :],
                                             qshf[p][:, :], ALU.add), sig="sV")
            st_done[p] = sc.op("sync", DMA(qk16v[:, c, :], qout[p][:, :]),
                               waits=[("sV", ev_done[p])], sig="sXD", inc=16)
        qk_v_fence = sc.counts["sV"]
        vw_tok = [None, None]
        vmm = [None, None]
        vev = [None, None]
        vst = [None, None]
        for vs in range(4):
            pw = vs % 2
            w = [("sT", vmm[0]), ("sT", vmm[1])] if vs >= 2 else []
            vw_tok[pw] = sc.op("gpsimd", DMA(vwbuf[pw][:, :, :], wvT.ap()[vs]),
                               waits=w, sig="sWD", inc=16)
            for ti in range(NTT):
                pp = ti % 2
                w = [("sWD", vw_tok[pw]), ("sV", qk_v_fence)]
                if vev[pp]:
                    w.append(("sV", vev[pp]))
                first = True
                for ci in range(16):
                    tk = sc.op("tensor", MM(ps1[pp][:, 0:512], h1T[:, ci, ts(ti, 128)],
                                            vwbuf[pw][:, ci, :], ci == 0, ci == 15),
                               waits=(w if first else []),
                               sig=("sT" if ci == 15 else None))
                    first = False
                vmm[pp] = tk
                w = [("sT", vmm[pp])]
                if vst[pp]:
                    w.append(("sXD", vst[pp]))
                vev[pp] = sc.op("vector", CP(vstg[pp][:, ts(vs, 512)], ps1[pp][:, 0:512]),
                                waits=w, sig="sV")
                vst[pp] = sc.op("sync", DMA(v16v[:, ti, ts(vs, 512)],
                                            vstg[pp][:, ts(vs, 512)]),
                                waits=[("sV", vev[pp])], sig="sXD", inc=16)
    sc.barrier()

    ph01.close()

    sc.mark("ph2")
    with ExitStack() as ph:
        _sb = lambda n, s, d: ph.enter_context(nc.sbuf_tensor(n, s, d))
        _ps = lambda n, s, d=F32: ph.enter_context(nc.psum_tensor(n, s, d))
        Kb = _sb("Kb", [128, 16, BLK], BF16)
        Vb = _sb("Vb", [128, 8, D], BF16)
        ctxF = _sb("ctxF", [128, 16, BLK], BF16)
        Qc = [_sb(f"Qc{i}", [128, BLK], BF16) for i in range(2)]
        Ebuf = [_sb(f"Ebuf{i}", [128, BLK], BF16) for i in range(2)]
        Pbuf = [_sb(f"Pbuf{i}", [128, BLK], BF16) for i in range(2)]
        PTs = [_sb(f"PTs{i}", [128, BLK], BF16) for i in range(2)]
        sumc = _sb("sumc", [128, 4], F32)
        psS = [_ps(f"psS{i}", [128, BLK], F32) for i in range(2)]
        psPT = _ps("psPT", [128, 2 * BLK], BF16)
        psCtx = _ps("psCtx", [128, BLK], F32)
        SKEW = 2
        NU = 256

        ctx_st = None
        for b in range(NBLK):
            w = []
            if b > 0:
                w = [("sT", sc.counts["sT"]), ("sXD", ctx_st)]
            kld = sc.op("sync", DMA(Kb[:, :, :], qk16v[:, 16:32, ts(b, BLK)]),
                        waits=w, sig="sXD", inc=16)
            vld = sc.op("sync", DMA(Vb[:, :, :], v16v[:, ts(b, 8), :]),
                        sig="sXD", inc=16)
            q_ld = [None, None]
            q_last_mm = [None, None]
            sc_tok = [None] * NU
            exp_tok = [None] * NU
            av_tok = [None] * NU
            ctx_ev_tok = [None] * NU
            pscale_by_par = [None, None]
            pt_ev_last = None
            first_ev_of_block = True

            def unit(u):
                ci, rem = divmod(u, 16)
                hh, j = divmod(rem, 8)
                return ci, hh, j

            def emit_ctx_ev(v):
                nonlocal first_ev_of_block
                ci, hh, j = unit(v)
                r = hh * 64
                w = [("sT", av_tok[v])]
                if first_ev_of_block and ctx_st:
                    w.append(("sXD", ctx_st))
                first_ev_of_block = False
                ctx_ev_tok[v] = sc.op("scalar",
                                      SCP(ctxF[r:r + 64, ci, ts(j, 128)],
                                          psCtx[0:64, (v % 2) * 512:(v % 2) * 512 + 128]),
                                      waits=w, sig="sS")

            mk_tok = [None] * NU
            scale_tok = [None] * NU
            pt_tok = [None] * NU
            ptev_tok = [None] * NU
            for u in range(NU + 7):
                v = u - 2
                if 0 <= v < NU:
                    _, _, j = unit(v)
                    span = (j + 1) * 128
                    pv = v % 2
                    we = [("sV", mk_tok[v])]
                    if v >= 2 and scale_tok[v - 2]:
                        we.append(("sV", scale_tok[v - 2]))
                    exp_tok[v] = sc.op("scalar",
                                       ACTV(Ebuf[pv][:, 0:span], psS[pv][:, 0:span],
                                            AF.Exp, scale=0.125,
                                            acc=sumc[:, pv:pv + 1]),
                                       waits=we, sig="sS")
                v = u - 3
                if 0 <= v < NU:
                    _, _, j = unit(v)
                    span = (j + 1) * 128
                    pv = v % 2
                    sc.op("vector", RECIP(sumc[:, 2 + pv:3 + pv], sumc[:, pv:pv + 1]),
                          waits=[("sS", exp_tok[v])], sig="sV")
                    w2 = []
                    if v >= 2 and pt_tok[v - 2]:
                        w2.append(("sT", pt_tok[v - 2]))
                    scale_tok[v] = sc.op("vector", TSC(Pbuf[pv][:, 0:span],
                                                       Ebuf[pv][:, 0:span],
                                                       sumc[:, 2 + pv:3 + pv], None,
                                                       ALU.mult), waits=w2,
                                         sig="sV", self_edge=True)
                v = u - 6
                if 0 <= v < NU:
                    ci, hh, j = unit(v)
                    h = ci * 2 + hh
                    pv = v % 2
                    w = [("sV", ptev_tok[v]), ("sXD", vld)]
                    if v >= 2 and ctx_ev_tok[v - 2]:
                        w.append(("sS", ctx_ev_tok[v - 2]))
                    cbase = pv * 512
                    for i in range(j + 1):
                        tk = sc.op("tensor", MM(psCtx[0:64, cbase:cbase + 128],
                                                Vb[:, i, ts(h, 64)],
                                                PTs[pv][:, ts(i, 128)], i == 0, i == j),
                                   waits=(w if i == 0 else []),
                                   sig=("sT" if i == j else None))
                    av_tok[v] = tk
                    emit_ctx_ev(v)
                v = u - 5
                if 0 <= v < NU:
                    _, _, j = unit(v)
                    span = (j + 1) * 128
                    pv = v % 2
                    w = [("sT", pt_tok[v])]
                    if v >= 2 and av_tok[v - 2]:
                        w.append(("sT", av_tok[v - 2]))
                    ptev_tok[v] = sc.op("vector",
                                        CP(PTs[pv][:, 0:span],
                                           psPT[:, pv * 1024: pv * 1024 + span]),
                                        waits=w, sig="sV")
                v = u - 4
                if 0 <= v < NU:
                    _, _, j = unit(v)
                    pv = v % 2
                    w = [("sV", scale_tok[v])]
                    if v >= 2 and ptev_tok[v - 2]:
                        w.append(("sV", ptev_tok[v - 2]))
                    for i in range(j + 1):
                        tk = sc.op("tensor",
                                   TR(psPT[:, pv * 1024 + i * 128: pv * 1024 + i * 128 + 128],
                                      Pbuf[pv][:, ts(i, 128)], id16[:, :]),
                                   waits=(w if i == 0 else []),
                                   sig=("sT" if i == j else None))
                    pt_tok[v] = tk
                v = u - 1
                if 0 <= v < NU:
                    _, _, j = unit(v)
                    pv = v % 2
                    mk_tok[v] = sc.op("vector", TT_(psS[pv][:, ts(j, 128)],
                                                    psS[pv][:, ts(j, 128)],
                                                    msk[:, :], ALU.add),
                                      waits=[("sT", sc_tok[v]), ("sXD", mask_tok)],
                                      sig="sV")
                if u < NU:
                    ci, hh, j = unit(u)
                    pq = ci % 2
                    if hh == 0 and j == 0:
                        w = [("sXD", kld)]
                        if q_last_mm[pq]:
                            w.append(("sT", q_last_mm[pq]))
                        q_ld[pq] = sc.op("sync",
                                         DMA(Qc[pq][:, :], qk16v[:, ci, ts(b, BLK)]),
                                         waits=w, sig="sXD", inc=16)
                    r = hh * 64
                    span = (j + 1) * 128
                    slot = u % 2
                    w = [("sXD", q_ld[pq])]
                    if u >= 2 and exp_tok[u - 2]:
                        w.append(("sS", exp_tok[u - 2]))
                    first = True
                    nsp = (span + 511) // 512
                    for s_i in range(nsp):
                        lo, hi = s_i * 512, min(span, (s_i + 1) * 512)
                        tk = sc.op("tensor",
                                   MM(psS[slot][:, lo:hi], Qc[pq][r:r + 64, ts(j, 128)],
                                      Kb[r:r + 64, ci, lo:hi], True, True),
                                   waits=(w if first else []),
                                   sig=("sT" if s_i == nsp - 1 else None))
                        first = False
                    sc_tok[u] = tk
                    q_last_mm[pq] = tk
            ctx_st = sc.op("sync", DMA(ctx16v[:, :, ts(b, BLK)], ctxF[:, :, :]),
                           waits=[("sS", ctx_ev_tok[NU - 1])], sig="sXD", inc=16)
    sc.barrier()

    sc.mark("ph3")
    with ExitStack() as ph:
        _sb = lambda n, s, d: ph.enter_context(nc.sbuf_tensor(n, s, d))
        _ps = lambda n, s, d=F32: ph.enter_context(nc.psum_tensor(n, s, d))
        woTb = _sb("woTb", [128, 16, D], BF16)
        gamf = _sb("gamf", [128, D], F32)
        ctxh = [_sb(f"ctxh{i}", [128, 16, BLK], BF16) for i in range(2)]
        xnld = [_sb(f"xnld{i}", [128, D], F32) for i in range(2)]
        ytile = [_sb(f"ytile{i}", [128, D], F32) for i in range(2)]
        h2t = [_sb(f"h2t{i}", [128, D], BF16) for i in range(2)]
        h2stg = [_sb(f"h2stg{i}", [128, 16, 128], BF16) for i in range(2)]
        ysq = [_sb(f"ysq{i}", [128, D], F32) for i in range(2)]
        st3 = _sb("st3", [128, 8], F32)
        ps3 = [_ps(f"ps3_{i}", [128, 512], F32) for i in range(6)]
        psT3 = _ps("psT3", [128, NT], BF16)

        wo_ld = sc.op("gpsimd", DMA(woTb[:, :, :], woTv[:, :, :]), sig="sWD", inc=16)
        gf_ld = sc.op("sync", DMA(gamf[:, :], gam_f[:, :]), sig="sXD", inc=16)

        ev3 = [None] * 6
        ylast = [None, None]
        ydma = [None, None]
        h2dma = [None, None]
        h2ev = [None, None]
        sqts = [None, None]
        h2toks = [None] * NTT
        ctx_ld = None

        def emit_h2tr(t_j):
            pp = t_j % 2
            base = pp * 1024
            w = [("sV", h2toks[t_j])]
            if h2ev[pp]:
                w.append(("sV", h2ev[pp]))
            for ci in range(8):
                tk = sc.op("tensor", TR(psT3[:, base + ci * 128: base + ci * 128 + 128],
                                        h2t[pp][:, ts(ci, 128)], id16[:, :]),
                           waits=(w if ci == 0 else []),
                           sig=("sT" if ci == 7 else None))
            ev_a = sc.op("vector",
                         CP(h2stg[pp][:, 0:8, :],
                            psT3.ap().rearrange("p (c n) -> p c n", n=128)[:, 8 * pp: 8 * pp + 8, :]),
                         waits=[("sT", tk)], sig="sV")
            for ci in range(8):
                tk = sc.op("tensor", TR(psT3[:, base + ci * 128: base + ci * 128 + 128],
                                        h2t[pp][:, ts(8 + ci, 128)], id16[:, :]),
                           waits=([("sV", ev_a)] if ci == 0 else []),
                           sig=("sT" if ci == 7 else None))
            h2ev[pp] = sc.op("vector",
                             CP(h2stg[pp][:, 8:16, :],
                                psT3.ap().rearrange("p (c n) -> p c n", n=128)[:, 8 * pp: 8 * pp + 8, :]),
                             waits=[("sT", tk)], sig="sV")
            h2dma[pp] = sc.op("sync", DMA(h2T16v[:, :, ts(t_j, TT)],
                                          h2stg[pp][:, :, :]),
                              waits=[("sV", h2ev[pp])], sig="sXD", inc=16)

        def emit_stats(t_j):
            pp = t_j % 2
            b0 = 4 * pp
            SQC, VV, SQR, RSD = b0, b0 + 1, b0 + 2, b0 + 3
            col = lambda c: st3[:, c:c + 1]
            sqt = sc.op("scalar", ACTV(ysq[pp][:, :], ytile[pp][:, :], AF.Square,
                                       acc=col(SQC)), waits=[("sV", ytok3[t_j])],
                        sig="sS")
            sqts[pp] = sqt
            vv_tok = sc.op("vector", TSC(col(VV), col(SQC), 1.0 / D, 1e-5, ALU.mult,
                                         ALU.add), waits=[("sS", sqt)], sig="sV")
            sq2 = sc.op("scalar", ACTV(col(SQR), col(VV), AF.Sqrt),
                        waits=[("sV", vv_tok)], sig="sS")
            rst = sc.op("vector", RECIP(col(RSD), col(SQR)), waits=[("sS", sq2)],
                        sig="sV")
            w = [("sV", rst), ("sXD", gf_ld)]
            if h2dma[pp]:
                w.append(("sXD", h2dma[pp]))
            h2tok = sc.op("vector", STT(h2t[pp][:, :], ytile[pp][:, :], col(RSD),
                                        gamf[:, :], ALU.mult, ALU.mult),
                          waits=w, sig="sV", self_edge=True)
            ylast[pp] = h2tok
            h2toks[t_j] = h2tok

        ytok3 = [None] * NTT
        ctx_lds = [sc.op("sync", DMA(ctxh[hf][:, :, :], ctx16v[:, :, ts(hf, BLK)]),
                         sig="sXD", inc=16) for hf in range(2)]
        for half in range(2):
            ctx_ld = ctx_lds[half]
            for tt in range(8):
                t_i = half * 8 + tt
                pp = t_i % 2
                w0 = [("sXD", ctx_ld), ("sWD", wo_ld)]
                slots = [(4 * t_i + k) % 6 for k in range(4)]
                mmtok = [None] * 4
                for os_ in range(4):
                    slot = slots[os_]
                    for ci in range(16):
                        w = []
                        if ci == 0:
                            if os_ == 0:
                                w += w0
                            if ev3[slot]:
                                w.append(("sV", ev3[slot]))
                        tk = sc.op("tensor", MM(ps3[slot][:, :],
                                                ctxh[half][:, ci, ts(tt, 128)],
                                                woTb[:, ci, ts(os_, 512)],
                                                ci == 0, ci == 15),
                                   waits=w,
                                   sig=("sT" if ci == 15 else None))
                    mmtok[os_] = tk
                if t_i >= 2:
                    emit_h2tr(t_i - 2)
                w = [("sV", ylast[pp])] if ylast[pp] else []
                xn_ld = sc.op("scalar", DMA(xnld[pp][:, :], xn32[ts(t_i, TT), :]),
                              waits=w, sig="sSD", inc=16)
                for os_ in range(4):
                    off = os_ * 512
                    w = [("sT", mmtok[os_]), ("sSD", xn_ld)]
                    if os_ == 0:
                        if ydma[pp]:
                            w.append(("sXD", ydma[pp]))
                        if sqts[pp]:
                            w.append(("sS", sqts[pp]))
                    etok = sc.op("vector", TT_(ytile[pp][:, off:off + 512],
                                               ps3[slots[os_]][:, :],
                                               xnld[pp][:, off:off + 512], ALU.add),
                                 waits=w, sig="sV")
                    ev3[slots[os_]] = etok
                ytok3[t_i] = etok
                ydma[pp] = sc.op("sync", DMA(y32[ts(t_i, TT), :], ytile[pp][:, :]),
                                 waits=[("sV", etok)], sig="sXD", inc=16)
                if t_i >= 1:
                    emit_stats(t_i - 1)
        emit_stats(NTT - 1)
        emit_h2tr(NTT - 2)
        emit_h2tr(NTT - 1)
    sc.barrier()

    sc.mark("ph4")
    with ExitStack() as ph:
        _sb = lambda n, s, d: ph.enter_context(nc.sbuf_tensor(n, s, d))
        h2Tg = _sb("h2Tg", [128, 16, BLK], BF16)
        actF = _sb("actF", [128, 32, BLK], BF16)
        wup_b = [_sb(f"wup_{i}", [128, 16, 128], BF16) for i in range(4)]
        sgb = [_sb(f"sgb_{i}", [128, BLK], F32) for i in range(2)]
        wdn_b = [_sb(f"wdn_{i}", [128, 32, 128], BF16) for i in range(2)]
        dF = [_sb(f"dF_{i}", [128, BLK], BF16) for i in range(2)]
        down2T = _sb("down2T", [128, 8, D], BF16)
        yb = [_sb(f"yb_{i}", [128, D], F32) for i in range(2)]
        d2f = _sb("d2f", [128, D], F32)
        ab = _sb("ab", [128, D], F32)

        wtu = [None, None]
        mmup = [None, None]
        evup = [None, None]
        wtd = [None, None]
        mmd = [None, None]
        evd = [None, None]
        trd_tok = [None, None]
        d2ev_last = None
        adma = None
        atok = [None, None]
        up_mm_last = None
        down_mm_last = None
        down_psum_free = []
        h2_ld_next = sc.op("sync", DMA(h2Tg[:, :, :], h2T16v[:, :, ts(0, BLK)]),
                           sig="sXD", inc=16)
        for g in range(2):
            h2_ld = h2_ld_next
            with ExitStack() as php:
                _ps2 = lambda n, s, d=F32: php.enter_context(nc.psum_tensor(n, s, d))
                ps4 = [_ps2(f"ps4_{g}_{i}", [128, BLK], F32) for i in range(4)]
                for c in range(32):
                    p = c % 2
                    pg_, pv_ = 2 * p, 2 * p + 1
                    w = [("sT", mmup[p])] if mmup[p] else []
                    sc.op("gpsimd", DMA(wup_b[pg_][:, :, :], wupT.ap()[c]),
                          waits=w, sig="sWD", inc=16)
                    wtu[p] = sc.op("gpsimd", DMA(wup_b[pv_][:, :, :],
                                                 wupT.ap()[c + 32]),
                                   sig="sWD", inc=16)
                    w = [("sWD", wtu[p]), ("sXD", h2_ld)]
                    if evup[p]:
                        w.append(("sV", evup[p]))
                    if c < 2:
                        for t_ in down_psum_free:
                            w.append(t_)
                    first = True
                    for which in range(2):
                        slot = 2 * p + which
                        for sp in range(2):
                            for ci in range(16):
                                tk = sc.op("tensor",
                                           MM(ps4[slot][:, ts(sp, 512)],
                                              wup_b[slot][:, ci, :],
                                              h2Tg[:, ci, ts(sp, 512)], ci == 0, ci == 15),
                                           waits=(w if first else []),
                                           sig=("sT" if (which == 1 and sp == 1 and ci == 15) else None))
                                first = False
                    mmup[p] = tk
                    w = [("sT", tk)]
                    if evup[p]:
                        w.append(("sV", evup[p]))
                    wact = list(w)
                    if c < 2 and down_mm_last:
                        wact.append(("sT", down_mm_last))
                    sg_tok = sc.op("scalar", ACTV(sgb[p][:, :], ps4[pg_][:, :], AF.Silu),
                                   waits=w, sig="sS")
                    wact.append(("sS", sg_tok))
                    evup[p] = sc.op("vector", TT_(actF[:, c, :], sgb[p][:, :],
                                                  ps4[pv_][:, :], ALU.mult),
                                    waits=wact, sig="sV")
                up_done = sc.counts["sV"]
                up_mm_last = mmup[0] if (mmup[0] or 0) > (mmup[1] or 0) else mmup[1]
            with ExitStack() as php:
                _ps2 = lambda n, s, d=F32: php.enter_context(nc.psum_tensor(n, s, d))
                ps4d = [_ps2(f"ps4d_{g}_{i}", [128, BLK], F32) for i in range(2)]
                psT4 = _ps2(f"psT4_{g}", [128, 2 * BLK], BF16)

                def emit_tr4(oc):
                    nonlocal d2ev_last
                    p = oc % 2
                    base = p * 1024
                    w = [("sV", evd[p])]
                    if d2ev_last:
                        w.append(("sV", d2ev_last))
                    for ti in range(8):
                        tk = sc.op("tensor",
                                   TR(psT4[:, base + ti * 128: base + ti * 128 + 128],
                                      dF[p][:, ts(ti, 128)], id16[:, :]),
                                   waits=(w if ti == 0 else []),
                                   sig=("sT" if ti == 7 else None))
                    trd_tok[p] = tk
                    d2ev_last = sc.op("vector",
                                      CP(down2T[:, :, ts(oc, 128)],
                                         psT4.ap().rearrange("p (c n) -> p c n", n=128)[:, 8 * p: 8 * p + 8, :]),
                                      waits=[("sT", tk)], sig="sV")

                for oc in range(16):
                    p = oc % 2
                    w = [("sV", up_done)]
                    if mmd[p]:
                        w.append(("sT", mmd[p]))
                    wtd[p] = sc.op("gpsimd", DMA(wdn_b[p][:, :, :], wdnT.ap()[oc]),
                                   waits=w, sig="sWD", inc=16)
                    w = [("sWD", wtd[p])]
                    if evd[p]:
                        w.append(("sV", evd[p]))
                    if oc < 2 and g == 0:
                        pass
                    first = True
                    for sp in range(2):
                        for ic in range(32):
                            tk = sc.op("tensor", MM(ps4d[p][:, ts(sp, 512)],
                                                    wdn_b[p][:, ic, :],
                                                    actF[:, ic, ts(sp, 512)],
                                                    ic == 0, ic == 31),
                                       waits=(w if first else []),
                                       sig=("sT" if (sp == 1 and ic == 31) else None))
                            first = False
                    mmd[p] = tk
                    if oc >= 1:
                        emit_tr4(oc - 1)
                    w = [("sT", tk)]
                    if trd_tok[p]:
                        w.append(("sT", trd_tok[p]))
                    evd[p] = sc.op("vector", CP(dF[p][:, :], ps4d[p][:, :]),
                                   waits=w, sig="sV")
                emit_tr4(15)
                down_mm_last = mmd[0] if (mmd[0] or 0) > (mmd[1] or 0) else mmd[1]
                down_psum_free = [("sV", d2ev_last), ("sV", evd[0]), ("sV", evd[1])]
                if g == 0:
                    h2_ld_next = sc.op("sync",
                                       DMA(h2Tg[:, :, :], h2T16v[:, :, ts(1, BLK)]),
                                       waits=[("sT", up_mm_last)], sig="sXD", inc=16)
                for ti in range(8):
                    t_i = g * 8 + ti
                    pp = ti % 2
                    w = [("sV", atok[pp])] if atok[pp] else []
                    y_ld = sc.op("scalar", DMA(yb[pp][:, :], y32[ts(t_i, TT), :]),
                                 waits=w, sig="sSD", inc=16)
                    w = [("sV", d2ev_last)]
                    if atok[1 - pp]:
                        w.append(("sV", atok[1 - pp]))
                    cvt = sc.op("vector", CP(d2f[:, :], down2T[:, ti, :]),
                                waits=w, sig="sV")
                    w = [("sSD", y_ld), ("sV", cvt)]
                    if adma:
                        w.append(("sXD", adma))
                    atok[pp] = sc.op("vector", TT_(ab[:, :], yb[pp][:, :],
                                                   d2f[:, :], ALU.add),
                                     waits=w, sig="sV")
                    adma = sc.op("sync", DMA(att32[ts(t_i, TT), :], ab[:, :]),
                                 waits=[("sV", atok[pp])], sig="sXD", inc=16)
    sc.barrier()

    sc.mark("ph5a")
    with ExitStack() as ph:
        _sb = lambda n, s, d: ph.enter_context(nc.sbuf_tensor(n, s, d))
        _ps = lambda n, s, d=F32: ph.enter_context(nc.psum_tensor(n, s, d))
        xnT2 = _sb("xnT2", [128, 16, NT], BF16)
        wgb = [_sb(f"wgb{i}", [128, 16, 512], BF16) for i in range(2)]
        wub = [_sb(f"wub{i}", [128, 16, 512], BF16) for i in range(2)]
        ps5 = [_ps(f"ps5_{i}", [128, 512], F32) for i in range(4)]
        psT5 = _ps("psT5", [128, 2 * BLK], BF16)
        sg5 = [_sb(f"sg5{i}", [128, 512], F32) for i in range(2)]
        hb = [_sb(f"hb{i}", [128, 512], BF16) for i in range(2)]
        qb = [_sb(f"qb{i}", [128, 512], BF16) for i in range(2)]
        cb = [_sb(f"cb{i}", [128, 6, 128], BF16) for i in range(2)]
        tb = [_sb(f"tb{i}", [128, 4, 128], BF16) for i in range(2)]
        kb = [_sb(f"kb{i}", [128, 512], BF16) for i in range(2)]
        hsb = [_sb(f"hsb{i}", [128, 512], BF16) for i in range(2)]
        stg5 = [_sb(f"stg5_{i}", [128, 4, 512], BF16) for i in range(2)]

        xnT2_ld = sc.op("sync", DMA(xnT2[:, :, :], xnT16v[:, :, :]), sig="sXD", inc=16)
        wt = [None, None]
        mm5 = [None, None]
        ev5 = [None] * (16 * NTT)
        sg_prev = [None, None]
        ht_prev = [None, None]
        tr_tok = [None, None]
        tr5ev = [None, None]
        stgdma = [None, None]
        TSKEW = 2

        def emit_tr5(v):
            s_v, t_v = divmod(v, NTT)
            pt = v % 2
            base = pt * 1024
            w = [("sV", ev5[v])]
            if tr5ev[pt]:
                w.append(("sV", tr5ev[pt]))
            for ii in range(4):
                tk = sc.op("tensor", TR(psT5[:, base + ii * 128: base + ii * 128 + 128],
                                        hsb[pt][:, ts(ii, 128)], id16[:, :]),
                           waits=(w if ii == 0 else []),
                           sig=("sT" if ii == 3 else None))
            tr_tok[pt] = tk
            tsub = t_v % 4
            pb = (t_v // 4) % 2
            w = [("sT", tk)]
            if tsub == 0 and stgdma[pb]:
                w.append(("sXD", stgdma[pb]))
            tr5ev[pt] = sc.op("vector",
                              CP(stg5[pb].ap().rearrange("p c (u n) -> p c u n", n=128)[:, :, tsub, :],
                                 psT5.ap().rearrange("p (c n) -> p c n", n=128)[:, 8 * pt: 8 * pt + 4, :]),
                              waits=w, sig="sV")
            if tsub == 3:
                stgdma[pb] = sc.op("sync",
                                   DMA(hsT16v[:, ts(s_v, 4), t_v // 4 * 512: t_v // 4 * 512 + 512],
                                       stg5[pb][:, :, :]),
                                   waits=[("sV", tr5ev[0]), ("sV", tr5ev[1])],
                                   sig="sXD", inc=16)

        for s in range(16):
            pw = s % 2
            w = [("sT", mm5[pw])] if wt[pw] else []
            sc.op("gpsimd", DMA(wgb[pw][:, :, :], wg.ap()[s]), waits=w,
                  sig="sWD", inc=16)
            wt[pw] = sc.op("gpsimd", DMA(wub[pw][:, :, :], wu.ap()[s]),
                           sig="sWD", inc=16)
            for t_i in range(NTT):
                u = s * NTT + t_i
                pt = u % 2
                gsl, usl = 2 * pt, 2 * pt + 1
                w = [("sWD", wt[pw]), ("sXD", xnT2_ld)]
                if sg_prev[pt]:
                    w.append(("sS", sg_prev[pt]))
                if ht_prev[pt]:
                    w.append(("sV", ht_prev[pt]))
                first = True
                for which in range(2):
                    slot = 2 * pt + which
                    wsrc = wgb[pw] if which == 0 else wub[pw]
                    for ci in range(16):
                        tk = sc.op("tensor", MM(ps5[slot][:, :],
                                                xnT2[:, ci, ts(t_i, 128)],
                                                wsrc[:, ci, :], ci == 0, ci == 15),
                                   waits=(w if first else []),
                                   sig=("sT" if (which == 1 and ci == 15) else None))
                        first = False
                mm5[pw] = tk
                if u >= TSKEW:
                    emit_tr5(u - TSKEW)
                w = [("sT", tk)]
                if ht_prev[pt]:
                    w.append(("sV", ht_prev[pt]))
                sg_prev[pt] = sc.op("scalar", ACTV(sg5[pt][:, :], ps5[gsl][:, :],
                                                   AF.Silu), waits=w, sig="sS")
                w = [("sS", sg_prev[pt])]
                if tr_tok[pt]:
                    w.append(("sT", tr_tok[pt]))
                htok = sc.op("vector", TT_(hb[pt][:, :], sg5[pt][:, :], ps5[usl][:, :],
                                           ALU.mult), waits=w, sig="sV")
                ht_prev[pt] = htok
                A = lambda buf: buf[:, 0:128]
                B = lambda buf: buf[:, 128:256]
                C = lambda buf: buf[:, 256:384]
                Dp = lambda buf: buf[:, 384:512]
                sc.op("vector", TT_(qb[pt][:, :], hb[pt][:, :], hb[pt][:, :], ALU.mult),
                      sig="sV")
                q_ = qb[pt]
                c_ = cb[pt]
                sc.op("vector", TT_(c_[:, 0, :], A(q_), B(q_), ALU.is_ge), sig="sV")
                sc.op("vector", TT_(c_[:, 1, :], A(q_), C(q_), ALU.is_ge), sig="sV")
                sc.op("vector", TT_(c_[:, 2, :], A(q_), Dp(q_), ALU.is_ge), sig="sV")
                sc.op("vector", TT_(c_[:, 3, :], B(q_), C(q_), ALU.is_ge), sig="sV")
                sc.op("vector", TT_(c_[:, 4, :], B(q_), Dp(q_), ALU.is_ge), sig="sV")
                sc.op("vector", TT_(c_[:, 5, :], C(q_), Dp(q_), ALU.is_ge), sig="sV")
                t_ = tb[pt]
                k_ = kb[pt]
                tA, tB, tC, tD = t_[:, 0, :], t_[:, 1, :], t_[:, 2, :], t_[:, 3, :]
                sc.op("vector", TT_(tA, c_[:, 0, :], c_[:, 1, :], ALU.add), sig="sV")
                sc.op("vector", TT_(tB, c_[:, 3, :], c_[:, 4, :], ALU.add), sig="sV")
                sc.op("vector", TT_(tC, c_[:, 1, :], c_[:, 3, :], ALU.add), sig="sV")
                sc.op("vector", TT_(tD, c_[:, 2, :], c_[:, 4, :], ALU.add), sig="sV")
                sc.op("vector", TT_(tA, tA, c_[:, 2, :], ALU.add), sig="sV")
                sc.op("vector", TT_(tB, tB, c_[:, 0, :], ALU.subtract), sig="sV")
                sc.op("vector", TT_(tC, tC, c_[:, 5, :], ALU.subtract), sig="sV")
                sc.op("vector", TT_(tD, tD, c_[:, 5, :], ALU.add), sig="sV")
                sc.op("vector", TSC(A(k_), tA, 1.5, None, ALU.is_ge), sig="sV")
                sc.op("vector", TSC(B(k_), tB, 0.5, None, ALU.is_ge), sig="sV")
                sc.op("vector", TSC(C(k_), tC, 0.5, None, ALU.is_le), sig="sV")
                sc.op("vector", TSC(Dp(k_), tD, 1.5, None, ALU.is_le), sig="sV")
                ev5[u] = sc.op("vector", TT_(hsb[pt][:, :], hb[pt][:, :], kb[pt][:, :],
                                             ALU.mult), sig="sV", self_edge=True)
        for v in range(16 * NTT - TSKEW, 16 * NTT):
            emit_tr5(v)
    sc.barrier()

    sc.mark("ph5b")
    with ExitStack() as ph:
        _sb = lambda n, s, d: ph.enter_context(nc.sbuf_tensor(n, s, d))
        _ps = lambda n, s, d=F32: ph.enter_context(nc.psum_tensor(n, s, d))
        hsTq = _sb("hsTq", [128, 64, 512], BF16)
        wdb = [_sb(f"wdb{i}", [128, 64, 128], BF16) for i in range(4)]
        dFb = [_sb(f"dFb{i}", [128, 512], BF16) for i in range(2)]
        ffnT = [_sb(f"ffnT{i}", [128, 4, D], BF16) for i in range(2)]
        xb = [_sb(f"xb5{i}", [128, D], F32) for i in range(2)]
        ff32 = _sb("ff32", [128, D], F32)
        ob = [_sb(f"ob5{i}", [128, D], F32) for i in range(2)]
        ps5d = [_ps(f"ps5d_{i}", [128, 512], F32) for i in range(2)]
        psT5b = _ps("psT5b", [128, 2 * BLK], BF16)

        hs_ld = None
        odma = [None, None]
        wt = [None, None, None, None]
        mmd = [None, None, None, None]
        slot_oc = [None, None, None, None]
        evd = [None, None]
        tr_tok = [None, None]
        f_ev_last = None
        pending_final = None
        cvt_gate = [None, None]
        def preload_hs(qt):
            w = [("sT", t) for t in mmd if t]
            a = sc.op("sync", DMA(hsTq[:, 0:32, :], hsT16v[:, 0:32, ts(qt, 512)]),
                      waits=w, sig="sXD", inc=16)
            b = sc.op("sync", DMA(hsTq[:, 32:64, :], hsT16v[:, 32:64, ts(qt, 512)]),
                      sig="sXD", inc=16)
            return a, b

        hs_next = preload_hs(0)
        for qt in range(4):
            hs_ld1, hs_ld = hs_next

            def emit_tr5b(oc):
                nonlocal f_ev_last
                p = oc % 2
                base = p * 1024
                w = [("sV", evd[p])]
                if f_ev_last:
                    w.append(("sV", f_ev_last))
                if cvt_gate[qt % 2]:
                    w.append(("sV", cvt_gate[qt % 2]))
                for ii in range(4):
                    tk = sc.op("tensor",
                               TR(psT5b[:, base + ii * 128: base + ii * 128 + 128],
                                  dFb[p][:, ts(ii, 128)], id16[:, :]),
                               waits=(w if ii == 0 else []),
                               sig=("sT" if ii == 3 else None))
                tr_tok[p] = tk
                f_ev_last = sc.op("vector",
                                  CP(ffnT[qt % 2][:, :, ts(oc, 128)],
                                     psT5b.ap().rearrange("p (c n) -> p c n", n=128)[:, 8 * p: 8 * p + 4, :]),
                                  waits=[("sT", tk)], sig="sV")

            oc_order = [(12 * qt + k) % 16 for k in range(16)]
            for oc in oc_order:
                p = oc % 2
                p4 = oc % 4
                if slot_oc[p4] != oc:
                    w = []
                    if wt[p4]:
                        w.append(("sT", mmd[p4]))
                    wt[p4] = sc.op("gpsimd", DMA(wdb[p4][:, :, :], wd.ap()[oc]),
                                   waits=w, sig="sWD", inc=16)
                    slot_oc[p4] = oc
                w = [("sWD", wt[p4]), ("sXD", hs_ld1)]
                if evd[p]:
                    w.append(("sV", evd[p]))
                first = True
                for ic in range(64):
                    if ic == 32:
                        w2 = [("sXD", hs_ld)]
                    else:
                        w2 = []
                    tk = sc.op("tensor", MM(ps5d[p][:, :], wdb[p4][:, ic, :],
                                            hsTq[:, ic, :], ic == 0, ic == 63),
                               waits=(w if first else w2),
                               sig=("sT" if ic == 63 else None))
                    first = False
                mmd[p4] = tk
                ko = oc_order.index(oc)
                if ko >= 1:
                    emit_tr5b(oc_order[ko - 1])
                w = [("sT", tk)]
                if tr_tok[p]:
                    w.append(("sT", tr_tok[p]))
                evd[p] = sc.op("vector", CP(dFb[p][:, :], ps5d[p][:, :]),
                               waits=w, sig="sV")
                if pending_final is not None and ko % 4 == 3:
                    tok = pending_final(ko // 4)
                    if ko == 15:
                        cvt_gate[(qt - 1) % 2] = tok
                        pending_final = None
            emit_tr5b(oc_order[15])
            if qt < 3:
                hs_next = preload_hs(qt + 1)

            def make_final(qt_c, f_ev_tok):
                def run(ti):
                    t_i = qt_c * 4 + ti
                    pp = ti % 2
                    w = [("sXD", odma[pp])] if odma[pp] else []
                    x_ld = sc.op("scalar", DMA(xb[pp][:, :], x[ts(t_i, TT), :]),
                                 waits=w, sig="sSD", inc=16)
                    at_ld = sc.op("sync", DMA(ob[pp][:, :], att32[ts(t_i, TT), :]),
                                  waits=w, sig="sXD", inc=16)
                    cvt = sc.op("vector", CP(ff32[:, :], ffnT[qt_c % 2][:, ti, :]),
                                waits=[("sV", f_ev_tok)], sig="sV")
                    wo_ = [("sSD", x_ld), ("sXD", at_ld), ("sV", cvt)]
                    sc.op("vector", TT_(ob[pp][:, :], ob[pp][:, :], xb[pp][:, :],
                                        ALU.add), waits=wo_, sig="sV")
                    o2 = sc.op("vector", TT_(ob[pp][:, :], ob[pp][:, :], ff32[:, :],
                                             ALU.add), sig="sV")
                    odma[pp] = sc.op("sync", DMA(out[ts(t_i, TT), :], ob[pp][:, :]),
                                     waits=[("sV", o2)], sig="sXD", inc=16)
                    return o2
                return run

            pending_final = make_final(qt, f_ev_last)
        for ti in range(4):
            pending_final(ti)

    sc.op("sync", (lambda e: e.nop()),
          waits=[("sXD", sc.counts["sXD"]), ("sSD", sc.counts["sSD"]),
                 ("sWD", sc.counts["sWD"])])

    sc.mark("tail")
    sc.emit()
    nc.finalize()
    nc._sched = sc
    nc._keepalive = stack
    return nc


def _host_inputs(inputs):
    f32 = np.float32
    xall = np.ascontiguousarray(np.asarray(inputs["L0_attn_x"], f32).reshape(NCORE * NT, D))
    wqkvT0 = np.asarray(inputs["L0_attn_w_qkv"], f32).T
    wqkvT = np.ascontiguousarray(
        wqkvT0.reshape(16, 128, 48, 128).transpose(2, 1, 0, 3))
    wvT = np.ascontiguousarray(
        wqkvT0[:, 4096:].reshape(16, 128, 4, 512).transpose(2, 1, 0, 3))
    woT = np.ascontiguousarray(np.asarray(inputs["L0_attn_w_o"], f32).T)
    wupT = np.ascontiguousarray(
        np.asarray(inputs["L0_attn_w_up"], f32).T
        .reshape(16, 128, 64, 128).transpose(2, 1, 0, 3))
    wdnT = np.ascontiguousarray(
        np.asarray(inputs["L0_attn_w_down"], f32).T
        .reshape(32, 128, 16, 128).transpose(2, 1, 0, 3))
    wg0 = np.asarray(inputs["L0_ffn_w_gate"], f32)
    wu0 = np.asarray(inputs["L0_ffn_w_up"], f32)
    wd0 = np.asarray(inputs["L0_ffn_w_down"], f32)
    gidx = np.arange(IF) // 4
    m = np.arange(IF) % 4
    new = (gidx // 128) * 512 + m * 128 + (gidx % 128)
    inv = np.empty(IF, np.int64)
    inv[new] = np.arange(IF)
    wg_ = np.ascontiguousarray(
        wg0[:, inv].reshape(16, 128, 16, 512).transpose(2, 1, 0, 3))
    wu_ = np.ascontiguousarray(
        wu0[:, inv].reshape(16, 128, 16, 512).transpose(2, 1, 0, 3))
    wd_ = np.ascontiguousarray(
        wd0[inv, :].reshape(64, 128, 16, 128).transpose(2, 1, 0, 3))
    gam_a = np.ascontiguousarray(np.tile(np.asarray(inputs["L0_attn_attn_norm_w"], f32)[None, :], (128, 1)))
    gam_f = np.ascontiguousarray(np.tile(np.asarray(inputs["L0_attn_ffn_norm_w"], f32)[None, :], (128, 1)))
    ident32 = np.eye(128, dtype=f32)
    ident16 = np.eye(128, dtype=ml_dtypes.bfloat16)
    qq, kk = np.meshgrid(np.arange(128), np.arange(128), indexing="ij")
    masktri = np.where(kk > qq, NEG, 0.0).astype(f32)
    invf = (1.0 / 10000.0 ** (np.arange(0, HD, 2, dtype=np.float64) / HD)).astype(f32)
    p = np.arange(128)
    sgn = np.where((p % 64) < 32, -1.0, 1.0).astype(f32)
    maps = []
    for c in range(NCORE):
        pos = ((c % 2) * NT + np.arange(NT)).astype(f32)
        ang = pos[None, :] * invf[p % 32][:, None]
        cosT = np.cos(ang).astype(f32)
        sinmT = (sgn[:, None] * np.sin(ang)).astype(f32)
        maps.append({
            "x": np.ascontiguousarray(xall[c * NT:(c + 1) * NT]),
            "wqkvT": wqkvT, "wvT": wvT, "woT": woT, "wupT": wupT, "wdnT": wdnT,
            "wg": wg_, "wu": wu_, "wd": wd_,
            "gam_a": gam_a, "gam_f": gam_f,
            "cosT": np.ascontiguousarray(cosT),
            "sinmT": np.ascontiguousarray(sinmT),
            "masktri": masktri, "ident32": ident32, "ident16": ident16,
        })
    return maps


def kernel(**inputs):
    maps = _host_inputs(inputs)
    nc = build()
    res = run_bass_kernel_spmd(nc, maps, core_ids=list(range(NCORE)))
    outs = [res.results[c]["out"] for c in range(NCORE)]
    return np.concatenate(outs, axis=0).reshape(4, 4096, D).astype(np.float32)
